# version 01
# baseline (speedup 1.0000x reference)
import numpy as np
import ml_dtypes
from contextlib import ExitStack

import concourse.bass as bass
import concourse.mybir as mybir
from concourse.bass import IndirectOffsetOnAxis
from concourse.bass_utils import run_bass_kernel_spmd

F32 = mybir.dt.float32
BF16 = mybir.dt.bfloat16
U32 = mybir.dt.uint32
I32 = mybir.dt.int32
AF = mybir.ActivationFunctionType
ALU = mybir.AluOpType
AX = mybir.AxisListType.X

BF16NP = ml_dtypes.bfloat16

B, S, D = 2, 8192, 1024
NCORE = 8
TPC = 2048
NCH = 8


def core_blocks(c):
    r = c % 4
    return c // 4, r, 7 - r


ENGINES = ("sy", "g", "t", "v", "s")


class Prog:
    def __init__(self, name="prog"):
        self.nc = bass.Bass("TRN2", target_bir_lowering=False, debug=False)
        self.ctx = ExitStack()
        self.cmds = {e: [] for e in ENGINES}
        self.sems = {}
        self.cnt = {}

    def sem(self, name):
        if name not in self.sems:
            self.sems[name] = self.ctx.enter_context(self.nc.semaphore(name))
            self.cnt[name] = 0
        return self.sems[name]

    def sb(self, name, shape, dt):
        return self.ctx.enter_context(self.nc.sbuf_tensor(name, shape, dt))

    def psum(self, name, shape, dt):
        return self.ctx.enter_context(self.nc.psum_tensor(name, shape, dt))

    def emit(self, eng, fn):
        self.cmds[eng].append(fn)

    def wait(self, eng, sname, val=None):
        if val is None:
            val = self.cnt.get(sname, 0)
        if val <= 0:
            return
        h = self.sem(sname)
        self.emit(eng, lambda e: e.wait_ge(h, val))

    def op(self, eng, fn, sname=None, k=1):
        if sname is None:
            self.emit(eng, lambda e: fn(e))
            return None
        h = self.sem(sname)
        self.cnt[sname] += k
        self.emit(eng, lambda e: fn(e).then_inc(h, k))
        return self.cnt[sname]

    def dma(self, eng, out, in_, sname, **kw):
        return self.op(eng, lambda e: e.dma_start(out=out, in_=in_, **kw),
                       sname, 16)

    def sop(self, eng, fn):
        ordn = eng + "_ord"
        self.wait(eng, ordn)
        return self.op(eng, fn, ordn)

    def build(self):
        block = self.ctx.enter_context(self.nc.Block())
        cmds = self.cmds

        @block.sync
        def _(e):
            for f in cmds["sy"]:
                f(e)

        @block.gpsimd
        def _(e):
            for f in cmds["g"]:
                f(e)

        @block.tensor
        def _(e):
            for f in cmds["t"]:
                f(e)

        @block.vector
        def _(e):
            for f in cmds["v"]:
                f(e)

        @block.scalar
        def _(e):
            for f in cmds["s"]:
                f(e)

        self.ctx.close()
        return self.nc


def make_consts():
    c = {}
    c["ident_bf"] = np.eye(128, dtype=BF16NP)
    c["ident_f32"] = np.eye(128, dtype=np.float32)
    c["ones_row_bf"] = np.ones((1, 512), dtype=BF16NP)
    c["ones_col_bf"] = np.ones((128, 1), dtype=BF16NP)
    c["ones_col_f32"] = np.ones((128, 1), dtype=np.float32)
    c["ones1_f32"] = np.ones((1, 128), dtype=np.float32)
    c["tri128"] = np.triu(np.ones((128, 128), np.float32), 1)
    c["tri32"] = np.triu(np.ones((32, 32), np.float32), 1)
    c["iota32_f"] = np.broadcast_to(np.arange(32, dtype=np.float32), (128, 32)).copy()
    c["iota128_f"] = np.arange(128, dtype=np.float32).reshape(128, 1)
    c["tbases"] = np.broadcast_to((np.arange(47, dtype=np.float32) * 128)[None, :],
                                  (32, 47)).copy()
    c["tbrow"] = (np.arange(47, dtype=np.float32) * 128)[None, :].copy()
    return c


CONST_SPECS = [
    ("ident_bf", [128, 128], BF16), ("ident_f32", [128, 128], F32), ("ones_row_bf", [1, 512], BF16),
    ("ones_col_bf", [128, 1], BF16), ("ones_col_f32", [128, 1], F32),
    ("ones1_f32", [1, 128], F32), ("tri128", [128, 128], F32),
    ("tri32", [32, 32], F32), ("iota32_f", [128, 32], F32),
    ("iota128_f", [128, 1], F32), ("tbases", [32, 47], F32),
    ("tbrow", [1, 47], F32),
]


class Consts:
    def __init__(self, p):
        nc = p.nc
        for name, shape, dt in CONST_SPECS:
            d = nc.dram_tensor(name, shape, dt, kind="ExternalInput").ap()
            t = p.sb("c_" + name, shape, dt)
            setattr(self, name, t)
            p.dma("sy", t[...], d[...], "c_ld")


def ln_scratch(p, tag):
    return {
        "xsq0": p.sb(f"{tag}_xsq0", [128, 512], BF16),
        "xsq1": p.sb(f"{tag}_xsq1", [128, 512], BF16),
        "xsq2": p.sb(f"{tag}_xsq2", [128, 512], BF16),
        "xsq3": p.sb(f"{tag}_xsq3", [128, 512], BF16),
        "row": p.sb(f"{tag}_row", [1, 1024], F32),
        "rtmp": p.sb(f"{tag}_rtmp", [1, 512], F32),
        "nt": p.sb(f"{tag}_nt", [128, 512], F32),
        "nt2": p.sb(f"{tag}_nt2", [128, 512], F32),
    }


def emit_ln(p, C, tag, x_sb, xln_sb, t0, TT, pb, sc):
    sq, st, rw, bc, nm = (f"{tag}_sq", f"{tag}_st", f"{tag}_rw",
                          f"{tag}_bc", f"{tag}_nm")
    ts = slice(t0, t0 + TT)
    ps_sx = pb[0][0:1, 0:TT]
    ps_sx2 = pb[1][0:1, 0:TT]
    ps_mb = pb[2][:, 0:TT]
    ps_rb = pb[3][:, 0:TT]
    row = sc["row"]

    p.wait("t", rw)
    p.wait("t", nm)
    for c in range(NCH):
        bs = sc["xsq0"] if c % 2 == 0 else sc["xsq1"]
        bq = sc["xsq2"] if c % 2 == 0 else sc["xsq3"]
        if p.cnt.get(st, 0) >= 4:
            p.wait("s", st, p.cnt[st] - 2)
        p.op("s", lambda e, b=bs, c_=c: e.copy(b[:, 0:TT], x_sb[:, c_, ts]), sq)
        p.op("s", lambda e, b=bq, c_=c: e.activation(
            b[:, 0:TT], x_sb[:, c_, ts], AF.Square), sq)
        p.wait("t", sq)
        p.op("t", lambda e, b=bs, c_=c: e.matmul(
            ps_sx, C.ones_col_bf[:, :], b[:, 0:TT],
            start=(c_ == 0), stop=(c_ == 7)), st)
        p.op("t", lambda e, b=bq, c_=c: e.matmul(
            ps_sx2, C.ones_col_bf[:, :], b[:, 0:TT],
            start=(c_ == 0), stop=(c_ == 7)), st)
    p.wait("v", st)
    p.op("v", lambda e: e.tensor_scalar(row[0:1, 0:TT], ps_sx, 1.0 / D, None, op0=ALU.mult), rw)
    p.op("v", lambda e: e.tensor_scalar(row[0:1, 512:512 + TT], ps_sx2, 1.0 / D, None, op0=ALU.mult), rw)
    p.wait("v", rw)
    p.op("v", lambda e: e.tensor_tensor(sc["rtmp"][0:1, 0:TT], row[0:1, 0:TT], row[0:1, 0:TT], op=ALU.mult), rw)
    p.wait("v", rw)
    p.op("v", lambda e: e.tensor_tensor(row[0:1, 512:512 + TT], row[0:1, 512:512 + TT], sc["rtmp"][0:1, 0:TT], op=ALU.subtract), rw)
    p.wait("v", rw)
    p.op("v", lambda e: e.tensor_scalar(row[0:1, 512:512 + TT], row[0:1, 512:512 + TT], 1e-5, None, op0=ALU.add), rw)
    p.wait("s", rw)
    p.op("s", lambda e: e.activation(row[0:1, 512:512 + TT], row[0:1, 512:512 + TT], AF.Sqrt), sq)
    p.wait("v", sq)
    p.op("v", lambda e: e.reciprocal(row[0:1, 512:512 + TT], row[0:1, 512:512 + TT]), rw)
    p.wait("t", rw)
    p.op("t", lambda e: e.matmul(ps_mb, C.ones1_f32[:, :], row[0:1, 0:TT], start=True, stop=True), bc)
    p.op("t", lambda e: e.matmul(ps_rb, C.ones1_f32[:, :], row[0:1, 512:512 + TT], start=True, stop=True), bc)
    p.wait("v", bc)
    for c in range(NCH):
        ntb = sc["nt"] if c % 2 == 0 else sc["nt2"]
        if c >= 2:
            p.wait("v", nm, p.cnt[nm] - 1)
        p.op("v", lambda e, c_=c, b=ntb: e.tensor_tensor(b[:, 0:TT], x_sb[:, c_, ts], ps_mb, op=ALU.subtract), nm)
        p.wait("v", nm)
        p.op("v", lambda e, c_=c, b=ntb: e.tensor_tensor(xln_sb[:, c_, ts], b[:, 0:TT], ps_rb, op=ALU.mult), nm)
    return nm


def shard_x(x):
    out = []
    for c in range(NCORE):
        b, b0, b1 = core_blocks(c)
        xa = x[b, b0 * 1024:(b0 + 1) * 1024]
        xb = x[b, b1 * 1024:(b1 + 1) * 1024]
        out.append(np.ascontiguousarray(np.concatenate([xa, xb], 0).T))
    return out


def unshard_x(shards):
    x = np.zeros((B, S, D), dtype=np.asarray(shards[0]).dtype)
    for c in range(NCORE):
        b, b0, b1 = core_blocks(c)
        sh = np.asarray(shards[c]).T
        x[b, b0 * 1024:(b0 + 1) * 1024] = sh[:1024]
        x[b, b1 * 1024:(b1 + 1) * 1024] = sh[1024:]
    return x


def emit_ffn1(p, C, tag, x_sb, xln, pb, w1t_d, w2t_d, bufs):
    h_sb = bufs["h_sb"]
    lt_sb, ltln = bufs["lt_sb"], bufs["ltln"]
    lora, lraw = f"{tag}_lo", f"{tag}_lodma"

    p.op("v", lambda e: e.bn_stats(bufs["ltst"][:, 0:6], lt_sb[:, 0:512]), lora)
    p.op("v", lambda e: e.bn_stats(bufs["ltst"][:, 6:12], lt_sb[:, 512:1024]), lora)
    p.wait("v", lora)
    p.op("v", lambda e: e.bn_aggr(bufs["ltmv"][:, 0:2], bufs["ltst"][:, 0:12]), lora)
    p.wait("v", lora)
    p.op("v", lambda e: e.tensor_scalar(bufs["ltrs"][:, 0:1], bufs["ltmv"][:, 1:2], 1e-5, None, op0=ALU.add), lora)
    p.wait("s", lora)
    p.op("s", lambda e: e.activation(bufs["ltrs"][:, 0:1], bufs["ltrs"][:, 0:1], AF.Sqrt), lora)
    p.wait("v", lora)
    p.op("v", lambda e: e.reciprocal(bufs["ltrs"][:, 0:1], bufs["ltrs"][:, 0:1]), lora)
    p.wait("v", lora)
    p.op("v", lambda e: e.tensor_tensor(bufs["ltnm"][:, 0:1], bufs["ltmv"][:, 0:1], bufs["ltrs"][:, 0:1], op=ALU.mult), lora)
    p.wait("v", lora)
    p.op("v", lambda e: e.tensor_scalar(bufs["ltnm"][:, 0:1], bufs["ltnm"][:, 0:1], -1.0, None, op0=ALU.mult), lora)
    p.wait("v", lora)
    p.op("v", lambda e: e.tensor_scalar(ltln[:, :], lt_sb[:, :], bufs["ltrs"][:, 0:1], bufs["ltnm"][:, 0:1], op0=ALU.mult, op1=ALU.add), lora)
    pT = pb[4].bitcast(BF16)
    p.wait("t", lora)
    for c in range(NCH):
        p.op("t", lambda e, c_=c: e.transpose(
            pT[:, c_ * 16:(c_ + 1) * 16], ltln[0:16, c_ * 128:(c_ + 1) * 128],
            C.ident_bf[0:16, 0:16]), lora)
    p.wait("v", lora)
    p.op("v", lambda e: e.tensor_copy(bufs["loraT"][:, :, :], pT[:, 0:128].rearrange("p (a b) -> p a b", a=8)), lora)
    lora_done = p.cnt[lora]
    p.wait("g", lora, lora_done)
    for i in range(4):
        p.dma("g", bufs["e1s"][0:1, i * 1024:(i + 1) * 1024],
              ltln[2 * i:2 * i + 1, :], lraw)
    p.wait("v", lraw)
    p.op("v", lambda e: e.tensor_scalar(bufs["e1s"][0:1, :], bufs["e1s"][0:1, :], bufs["sc_sb"][0:1, 0:1], None, op0=ALU.mult), lora)
    for ch in range(NCH):
        p.op("g", lambda e, ch_=ch: e.memset(bufs["e2b"][:, ch_, :], 0.0), lora)
    p.wait("g", lora)
    for ch in range(NCH):
        istar, half = ch // 2, ch % 2
        for jj in range(4):
            src = ltln[8 + 2 * istar:9 + 2 * istar,
                       512 * half + jj:512 * half + jj + 509:4]
            p.dma("g", bufs["e2b"][4 * jj + istar:4 * jj + istar + 1, ch, 0:128],
                  src, lraw, allow_slow_non_contiguous=True)
    lraw_done = p.cnt[lraw]

    w1l, w1m, hgel = f"{tag}_w1l", f"{tag}_w1m", f"{tag}_hg"
    td, tdr = f"{tag}_td", f"{tag}_tdr"
    uacc, uev, udr = f"{tag}_ua", f"{tag}_ue", f"{tag}_ud"
    w2l, w2m, res = f"{tag}_w2l", f"{tag}_w2m", f"{tag}_res"
    wbufs1 = [bufs["w1a"], bufs["w1b"]]
    wbufs2 = [bufs["w2a"], bufs["w2b"]]
    w1use = [0, 0]
    w2use = [0, 0]
    for ti in range(4):
        ts = slice(ti * 512, (ti + 1) * 512)
        nm = emit_ln(p, C, f"{tag}_ln", x_sb, xln, ti * 512, 512, pb[0:4],
                     bufs["lns"])
        p.wait("t", lora, lora_done)
        p.wait("t", nm)
        p.wait("t", uev)
        for c in range(NCH):
            p.op("t", lambda e, c_=c, ts_=ts: e.matmul(
                pb[5][0:4, 0:512], bufs["loraT"][:, c_, 1:9:2],
                xln[:, c_, ts_], start=(c_ == 0), stop=(c_ == 7)), td)
        p.wait("v", td)
        p.op("v", lambda e: e.tensor_copy(bufs["t_f"][0:4, :], pb[5][0:4, 0:512]), td)
        p.wait("g", td)
        for i in range(4):
            p.dma("g", bufs["t_row"][0:1, i * 512:(i + 1) * 512],
                  bufs["t_f"][i:i + 1, 0:512], tdr)
        p.wait("g", lraw, lraw_done)
        for ch in range(32):
            wb = wbufs1[ch % 2]
            p.wait("g", w1m, w1use[ch % 2])
            w1l_c = p.dma("g", wb[...],
                          w1t_d[:, ch * 128:(ch + 1) * 128].rearrange(
                              "(c p) m -> p c m", p=128), w1l + str(ch % 2))
            psH = pb[6 if ch % 2 == 0 else 7][:, 0:512]
            p.wait("t", w1l + str(ch % 2), w1l_c)
            p.wait("t", tdr)
            if ch < 2:
                p.wait("t", res)
            else:
                p.wait("t", hgel, p.cnt.get(hgel, 0) - 1)
            for c in range(NCH):
                p.op("t", lambda e, c_=c, w=wb, ps=psH, ts_=ts: e.matmul(
                    ps, w[:, c_, :], xln[:, c_, ts_], start=(c_ == 0), stop=False))
            p.op("t", lambda e, ch_=ch, ps=psH: e.matmul(
                ps, bufs["b1row"][0:1, ch_ * 128:(ch_ + 1) * 128],
                C.ones_row_bf[:, :], start=False, stop=False))
            p.op("t", lambda e, ch_=ch, ps=psH: e.matmul(
                ps, bufs["e1s"][0:1, ch_ * 128:(ch_ + 1) * 128],
                bufs["t_row"][0:1, (ch_ // 8) * 512:(ch_ // 8 + 1) * 512],
                start=False, stop=True), w1m)
            w1use[ch % 2] = p.cnt[w1m]
            p.wait("s", w1m)
            p.op("s", lambda e, ch_=ch, ps=psH: e.activation(
                h_sb[:, ch_, :], ps, AF.Gelu), hgel)
        p.wait("t", hgel)
        p.wait("t", nm)
        for c in range(32):
            p.op("t", lambda e, c_=c: e.matmul(
                pb[4 + c_ // 8][0:4, 0:512], bufs["loraT"][:, c_ % 8, 9:16:2],
                h_sb[:, c_, :], start=(c_ % 8 == 0), stop=(c_ % 8 == 7)), uacc)
        p.wait("v", uacc)
        for jj in range(4):
            p.op("v", lambda e, j=jj: e.tensor_scalar(
                bufs["u_f"][0:4, j, :], pb[4 + j][0:4, 0:512],
                bufs["sc4_sb"][0:4, 0:1], None, op0=ALU.mult), uev)
        p.wait("g", uev)
        for jj in range(4):
            p.dma("g", bufs["u_all"][4 * jj:4 * jj + 4, 0:512],
                  bufs["u_f"][0:4, jj, :], udr)
        for ch in range(NCH):
            wb = wbufs2[ch % 2]
            p.wait("g", w2m, w2use[ch % 2])
            w2l_c = p.dma("g", wb[...],
                          w2t_d[:, ch * 128:(ch + 1) * 128].rearrange(
                              "(c p) m -> p c m", p=128), w2l + str(ch % 2))
            psO = pb[6 if ch % 2 == 0 else 7][:, 0:512]
            p.wait("t", w2l + str(ch % 2), w2l_c)
            p.wait("t", udr)
            if ch < 2:
                p.wait("t", hgel)
                p.wait("t", uev)
            else:
                p.wait("t", res, p.cnt.get(res, 0) - 1)
            for c in range(32):
                p.op("t", lambda e, c_=c, w=wb, ps=psO: e.matmul(
                    ps, w[:, c_, :], h_sb[:, c_, :], start=(c_ == 0), stop=False))
            p.op("t", lambda e, ch_=ch, ps=psO: e.matmul(
                ps, bufs["b2row"][0:1, ch_ * 128:(ch_ + 1) * 128],
                C.ones_row_bf[:, :], start=False, stop=False))
            p.op("t", lambda e, ch_=ch, ps=psO: e.matmul(
                ps, bufs["e2b"][0:16, ch_, :], bufs["u_all"][0:16, 0:512],
                start=False, stop=True), w2m)
            w2use[ch % 2] = p.cnt[w2m]
            p.wait("v", w2m)
            p.op("v", lambda e, ch_=ch, ps=psO, ts_=ts: e.scalar_tensor_tensor(
                x_sb[:, ch_, ts_], ps, 0.5, x_sb[:, ch_, ts_],
                op0=ALU.mult, op1=ALU.add), res)
    return res


def ffn1_bufs(p, tag):
    return {
        "h_sb": p.sb(f"{tag}_h", [128, 32, 512], BF16),
        "lt_sb": p.sb(f"{tag}_lt", [16, D], F32),
        "ltln": p.sb(f"{tag}_ltln", [16, D], BF16),
        "ltst": p.sb(f"{tag}_ltst", [16, 12], F32),
        "ltmv": p.sb(f"{tag}_ltmv", [16, 2], F32),
        "ltrs": p.sb(f"{tag}_ltrs", [16, 1], F32),
        "ltnm": p.sb(f"{tag}_ltnm", [16, 1], F32),
        "loraT": p.sb(f"{tag}_loraT", [128, NCH, 16], BF16),
        "e1s": p.sb(f"{tag}_e1s", [1, 4096], BF16),
        "e2b": p.sb(f"{tag}_e2b", [16, NCH, 128], BF16),
        "t_row": p.sb(f"{tag}_trow", [1, 4 * 512], BF16),
        "t_f": p.sb(f"{tag}_tf", [4, 512], BF16),
        "u_all": p.sb(f"{tag}_uall", [16, 512], BF16),
        "u_f": p.sb(f"{tag}_uf", [4, 4, 512], BF16),
        "b1row": p.sb(f"{tag}_b1r", [1, 4096], BF16),
        "b2row": p.sb(f"{tag}_b2r", [1, D], BF16),
        "sc_sb": p.sb(f"{tag}_sc", [1, 1], F32),
        "sc4_sb": p.sb(f"{tag}_sc4", [4, 1], F32),
        "w1a": p.sb(f"{tag}_w1a", [128, NCH, 128], BF16),
        "w1b": p.sb(f"{tag}_w1b", [128, NCH, 128], BF16),
        "w2a": p.sb(f"{tag}_w2a", [128, 32, 128], BF16),
        "w2b": p.sb(f"{tag}_w2b", [128, 32, 128], BF16),
        "lns": ln_scratch(p, f"{tag}_lns"),
    }


def build_ffn1():
    p = Prog("ffn1")
    nc = p.nc
    x_d = nc.dram_tensor("x_fm", [D, TPC], F32, kind="ExternalInput").ap()
    w1t_d = nc.dram_tensor("w1t", [D, 4096], F32, kind="ExternalInput").ap()
    w2t_d = nc.dram_tensor("w2t", [4096, D], F32, kind="ExternalInput").ap()
    b1r_d = nc.dram_tensor("b1row", [1, 4096], F32, kind="ExternalInput").ap()
    b2r_d = nc.dram_tensor("b2row", [1, D], F32, kind="ExternalInput").ap()
    sc_d = nc.dram_tensor("scal", [1, 1], F32, kind="ExternalInput").ap()
    sc4_d = nc.dram_tensor("scal4", [4, 1], F32, kind="ExternalInput").ap()
    lt_d = nc.dram_tensor("ltoks", [16, D], F32, kind="ExternalInput").ap()
    y_d = nc.dram_tensor("y_fm", [D, TPC], F32, kind="ExternalOutput").ap()

    C = Consts(p)
    x_sb = p.sb("x_sb", [128, NCH, TPC], F32)
    xln = p.sb("xln", [128, NCH, 512], BF16)
    pb = [p.psum(f"pb{i}", [128, 512], F32) for i in range(8)]
    bufs = ffn1_bufs(p, "f1")

    for c in range(NCH):
        p.dma("sy", x_sb[:, c, :], x_d[c * 128:(c + 1) * 128, :], "xld")
    p.dma("sy", bufs["lt_sb"][:, :], lt_d[:, :], "xld")
    p.dma("g", bufs["b1row"][:, :], b1r_d[:, :], "wld")
    p.dma("g", bufs["b2row"][:, :], b2r_d[:, :], "wld")
    p.dma("sy", bufs["sc_sb"][:, :], sc_d[:, :], "xld")
    p.dma("sy", bufs["sc4_sb"][:, :], sc4_d[:, :], "xld")
    for e in ("v", "s", "t", "g"):
        p.wait(e, "xld")
        p.wait(e, "wld")
        p.wait(e, "c_ld")

    class XlnView:
        def __getitem__(self, idx):
            _, c, ts = idx
            off = ts.start % 512
            return xln[:, c, off:off + (ts.stop - ts.start)]
    res = emit_ffn1(p, C, "f1", x_sb, XlnView(), pb, w1t_d, w2t_d, bufs)

    p.wait("sy", res)
    for c in range(NCH):
        p.dma("sy", y_d[c * 128:(c + 1) * 128, :], x_sb[:, c, :], "yst")
    p.wait("sy", "yst")
    return p


def ffn1_inputs(core, x_fm, W1, b1, W2, b2, scale, x_full):
    b = core // 4
    cd = make_consts()
    return dict(
        x_fm=np.ascontiguousarray(x_fm),
        w1t=np.ascontiguousarray(W1.T),
        w2t=np.ascontiguousarray(W2.T),
        b1row=np.ascontiguousarray(b1.reshape(1, 4096)),
        b2row=np.ascontiguousarray(b2.reshape(1, D)),
        scal=np.full((1, 1), scale, np.float32),
        scal4=np.full((4, 1), scale, np.float32),
        ltoks=np.ascontiguousarray(x_full[b, 1:17]),
        **cd,
    )



def attn_params(layer):
    if layer == 0:
        return dict(nheads=8, dh=128, inv_scale=1.0 / 128.0, rope=True)
    return dict(nheads=16, dh=64, inv_scale=1.0 / 8.0, rope=False)


def own_tiles(rr):
    return [2 * rr, 2 * rr + 1, 2 * (7 - rr), 2 * (7 - rr) + 1]


def group_bases(rr):
    ba, bb = 1024 * rr, 1024 * (7 - rr)
    return [ba, ba + 512, bb, bb + 512]


def r3(ap2, a):
    return ap2.rearrange("p (a b) -> p a b", a=a)


def attn_bufs(p, tag):
    return {
        "arena_a": p.sb(f"{tag}_aa", [128, 24576], BF16),
        "arena_b": p.sb(f"{tag}_ab", [128, 16384], BF16),
        "woT": p.sb(f"{tag}_woT", [128, NCH, 1024], BF16),
        "o_all": p.sb(f"{tag}_oall", [128, NCH, 2048], BF16),
        "lns": ln_scratch(p, f"{tag}_lns"),
        "xfl0": p.sb(f"{tag}_xfl0", [128, NCH, 512], BF16),
        "xfl1": p.sb(f"{tag}_xfl1", [128, NCH, 512], BF16),
        "csb": p.sb(f"{tag}_csb", [128, 1024], BF16),
        "csb2": p.sb(f"{tag}_csb2", [128, 1024], BF16),
        "kout": p.sb(f"{tag}_kout", [128, NCH, 128], BF16),
        "vout": p.sb(f"{tag}_vout", [128, NCH, 128], BF16),
        "qout": p.sb(f"{tag}_qout", [128, NCH, 128], BF16),
        "rt1": p.sb(f"{tag}_rt1", [128, 4, 64], F32),
        "rt2": p.sb(f"{tag}_rt2", [128, 4, 64], F32),
        "rt3": p.sb(f"{tag}_rt3", [128, 4, 64], F32),
        "rt4": p.sb(f"{tag}_rt4", [128, 4, 64], F32),
        "ksq": p.sb(f"{tag}_ksq", [128, NCH, 128], BF16),
        "kn": p.sb(f"{tag}_kn", [128, 8], F32),
        "knacc": p.sb(f"{tag}_knacc", [128, 8], F32),
        "qnacc": p.sb(f"{tag}_qnacc", [128, 8], F32),
        "brow": p.sb(f"{tag}_brow", [1, 16], F32),
        "negb": p.sb(f"{tag}_negb", [128, 1], F32),
        "masks": p.sb(f"{tag}_masks", [128, 4, 512], BF16),
        "sccol": p.sb(f"{tag}_sccol", [128, 1], F32),
        "p_sb": p.sb(f"{tag}_psb", [128, 4, 512], BF16),
        "rrow": p.sb(f"{tag}_rrow", [1, 512], F32),
        "rb_sb": p.sb(f"{tag}_rb", [128, 512], F32),
        "xc0": p.sb(f"{tag}_xc0", [128, 512], F32),
        "xc1": p.sb(f"{tag}_xc1", [128, 512], F32),
        "yc0": p.sb(f"{tag}_yc0", [128, 512], F32),
        "yc1": p.sb(f"{tag}_yc1", [128, 512], F32),
    }


def _rope6(p, bufs, cs3, pvsrc, dst, sem):
    p.wait("v", sem)
    p.op("v", lambda e: e.tensor_tensor(
        bufs["rt1"][:, :, :], pvsrc[:, :, 0:64], cs3[:, :, 0:64], op=ALU.mult), sem)
    p.op("v", lambda e: e.tensor_tensor(
        bufs["rt2"][:, :, :], pvsrc[:, :, 64:128], cs3[:, :, 64:128], op=ALU.mult), sem)
    p.op("v", lambda e: e.tensor_tensor(
        bufs["rt3"][:, :, :], pvsrc[:, :, 0:64], cs3[:, :, 64:128], op=ALU.mult), sem)
    p.op("v", lambda e: e.tensor_tensor(
        bufs["rt4"][:, :, :], pvsrc[:, :, 64:128], cs3[:, :, 0:64], op=ALU.mult), sem)
    p.wait("v", sem)
    p.op("v", lambda e: e.tensor_tensor(
        dst[:, :, 0:64], bufs["rt1"][:, :, :], bufs["rt2"][:, :, :], op=ALU.subtract), sem)
    p.op("v", lambda e: e.tensor_tensor(
        dst[:, :, 64:128], bufs["rt3"][:, :, :], bufs["rt4"][:, :, :], op=ALU.add), sem)


def emit_attn(p, C, tag, layer, rr, xfull_d, x_d, y_d, wq_d, wk_d, wv_d,
              woT_d, cs_d, masks_d, sccol_d, kbuf, vbuf, qbuf, pb, bufs,
              a_tiles=16):
    ap = attn_params(layer)
    aA, aB = bufs["arena_a"], bufs["arena_b"]
    wq = r3(aA[:, 0:8192], NCH)
    wk = r3(aA[:, 8192:16384], NCH)
    wv = r3(aA[:, 16384:24576], NCH)
    aBf = aB.bitcast(F32)
    xf0 = r3(aBf[:, 0:4096], NCH)
    xf1 = r3(aBf[:, 4096:8192], NCH)
    khb = aA[:, 0:8192]
    vhb = r3(aA[:, 8192:16384], 64)
    q_sb = r3(aB[:, 0:16384], NCH)
    woT, o_all = bufs["woT"], bufs["o_all"]
    lns = bufs["lns"]
    kout, vout, qout = bufs["kout"], bufs["vout"], bufs["qout"]
    ot, gbs = own_tiles(rr), group_bases(rr)

    xld, wld, csld = f"{tag}_xld", f"{tag}_wld", f"{tag}_cs"
    kvm, rpd, kvev = f"{tag}_kvm", f"{tag}_rpd", f"{tag}_kvev"
    knsm, kst = f"{tag}_knsm", f"{tag}_kst"
    for c in range(NCH):
        p.dma("g", wq[:, c, :], wq_d[c * 128:(c + 1) * 128, :], wld)
        p.dma("g", wk[:, c, :], wk_d[c * 128:(c + 1) * 128, :], wld)
        p.dma("g", wv[:, c, :], wv_d[c * 128:(c + 1) * 128, :], wld)
        p.dma("g", woT[:, c, :], woT_d[c * 128:(c + 1) * 128, :], wld)
    for m in range(4):
        p.dma("g", bufs["masks"][:, m, :], masks_d[m, :, :], wld)
    p.dma("sy", bufs["sccol"][:, :], sccol_d[:, :], f"{tag}_scl")
    wld_done = p.cnt[wld]
    for e in ("t", "v", "s", "g"):
        p.wait(e, wld, wld_done)
        p.wait(e, f"{tag}_scl")
        p.wait(e, "c_ld")
    p.op("v", lambda e: e.memset(bufs["knacc"][:, :], 0.0), knsm)
    p.op("v", lambda e: e.memset(bufs["qnacc"][:, :], 0.0), knsm)

    kvm_hist, sq_hist = {}, {}
    for tt in range(a_tiles):
        xf = xf0 if tt % 2 == 0 else xf1
        xfl = bufs["xfl0"] if tt % 2 == 0 else bufs["xfl1"]
        if tt >= 2:
            p.wait("sy", kvm, kvm_hist[tt - 2])
            p.wait("sy", f"{tag}_ln_sq", sq_hist[tt - 2])
        xldp = xld + str(tt % 2)
        for c in range(NCH):
            p.dma("sy", xf[:, c, :], xfull_d[c * 128:(c + 1) * 128,
                                             tt * 512:(tt + 1) * 512], xldp)
        xld_c = p.cnt[xldp]
        p.wait("s", xldp, xld_c)
        p.wait("v", xldp, xld_c)
        nm = emit_ln(p, C, f"{tag}_ln", xf, xfl, 0, 512, pb[0:4], lns)
        own = tt in ot
        for sub in range(4):
            cst = bufs["csb"] if (tt * 4 + sub) % 2 == 0 else bufs["csb2"]
            cs3 = r3(cst, NCH)
            csp = csld + str((tt * 4 + sub) % 2)
            if ap["rope"]:
                if p.cnt.get(rpd, 0) > 0 and (tt * 4 + sub) >= 2:
                    p.wait("sy", rpd)
                p.dma("sy", cst[:, :], cs_d[tt * 512 + sub * 128:
                                            tt * 512 + sub * 128 + 128, :], csp)
            cs_c = p.cnt.get(csp, 0)
            p.wait("t", nm)
            p.wait("t", rpd)
            for g2 in range(2):
                for c in range(NCH):
                    p.op("t", lambda e, c_=c, g_=g2, s_=sub, xl=xfl: e.matmul(
                        pb[4 + g_][:, :], xl[:, c_, s_ * 128:(s_ + 1) * 128],
                        wk[:, c_, g_ * 512:(g_ + 1) * 512],
                        start=(c_ == 0), stop=(c_ == 7)), kvm)
            kvm_k = p.cnt[kvm]
            p.wait("t", kvev)
            for g2 in range(2):
                for c in range(NCH):
                    p.op("t", lambda e, c_=c, g_=g2, s_=sub, xl=xfl: e.matmul(
                        pb[6 + g_][:, :], xl[:, c_, s_ * 128:(s_ + 1) * 128],
                        wv[:, c_, g_ * 512:(g_ + 1) * 512],
                        start=(c_ == 0), stop=(c_ == 7)), kvm)
            kvm_v = p.cnt[kvm]
            p.wait("v", kvm, kvm_k)
            p.wait("v", kst)
            if ap["rope"]:
                p.wait("v", csp, cs_c)
                _rope6(p, bufs, cs3[:, 0:4, :], r3(pb[4][:, :], 4), kout[:, 0:4, :], rpd)
                _rope6(p, bufs, cs3[:, 4:8, :], r3(pb[5][:, :], 4), kout[:, 4:8, :], rpd)
            else:
                p.op("v", lambda e: e.tensor_copy(kout[:, 0:4, :], r3(pb[4][:, :], 4)), rpd)
                p.op("v", lambda e: e.tensor_copy(kout[:, 4:8, :], r3(pb[5][:, :], 4)), rpd)
            rpd_k = p.cnt[rpd]
            p.wait("s", kvm, kvm_v)
            p.wait("s", kst)
            p.op("s", lambda e: e.copy(vout[:, 0:4, :], r3(pb[6][:, :], 4)), kvev)
            p.op("s", lambda e: e.copy(vout[:, 4:8, :], r3(pb[7][:, :], 4)), kvev)
            kvev_c = p.cnt[kvev]
            p.wait("s", rpd, rpd_k)
            p.op("s", lambda e: e.activation(
                bufs["ksq"].rearrange("p a b -> p (a b)"),
                kout.rearrange("p a b -> p (a b)"), AF.Square), knsm)
            knsq_c = p.cnt[knsm]
            p.wait("v", knsm, knsq_c)
            for h in range(NCH):
                p.op("v", lambda e, h_=h: e.reduce_sum(
                    bufs["kn"][:, h_:h_ + 1], bufs["ksq"][:, h_, :], axis=AX), knsm)
            p.wait("v", knsm)
            p.op("v", lambda e: e.tensor_tensor(
                bufs["knacc"][:, :], bufs["knacc"][:, :], bufs["kn"][:, :], op=ALU.max), knsm)
            trow = tt * 512 + sub * 128
            p.wait("g", rpd, rpd_k)
            p.dma("g", kbuf[trow:trow + 128, :], kout.rearrange("p a b -> p (a b)"), kst)
            p.wait("g", kvev, kvev_c)
            p.dma("g", vbuf[trow:trow + 128, :], vout.rearrange("p a b -> p (a b)"), kst)
            if own:
                qsub = ot.index(tt) * 4 + sub
                p.wait("t", rpd, p.cnt[rpd])
                for g2 in range(2):
                    for c in range(NCH):
                        p.op("t", lambda e, c_=c, g_=g2, s_=sub, xl=xfl: e.matmul(
                            pb[4 + g_][:, :], xl[:, c_, s_ * 128:(s_ + 1) * 128],
                            wq[:, c_, g_ * 512:(g_ + 1) * 512],
                            start=(c_ == 0), stop=(c_ == 7)), kvm)
                qm_c = p.cnt[kvm]
                p.wait("v", kvm, qm_c)
                p.wait("v", kst)
                if ap["rope"]:
                    _rope6(p, bufs, cs3[:, 0:4, :], r3(pb[4][:, :], 4), qout[:, 0:4, :], rpd)
                    _rope6(p, bufs, cs3[:, 4:8, :], r3(pb[5][:, :], 4), qout[:, 4:8, :], rpd)
                else:
                    p.op("v", lambda e: e.tensor_copy(qout[:, 0:4, :], r3(pb[4][:, :], 4)), rpd)
                    p.op("v", lambda e: e.tensor_copy(qout[:, 4:8, :], r3(pb[5][:, :], 4)), rpd)
                rpd_q = p.cnt[rpd]
                p.wait("s", rpd, rpd_q)
                p.op("s", lambda e: e.activation(
                    bufs["ksq"].rearrange("p a b -> p (a b)"),
                    qout.rearrange("p a b -> p (a b)"), AF.Square), knsm)
                qsq_c = p.cnt[knsm]
                p.wait("v", knsm, qsq_c)
                for h in range(NCH):
                    p.op("v", lambda e, h_=h: e.reduce_sum(
                        bufs["kn"][:, h_:h_ + 1], bufs["ksq"][:, h_, :], axis=AX), knsm)
                p.wait("v", knsm)
                p.op("v", lambda e: e.tensor_tensor(
                    bufs["qnacc"][:, :], bufs["qnacc"][:, :], bufs["kn"][:, :], op=ALU.max), knsm)
                p.wait("g", rpd, rpd_q)
                p.dma("g", qbuf[qsub * 128:qsub * 128 + 128, :],
                      qout.rearrange("p a b -> p (a b)"), kst)
        kvm_hist[tt] = p.cnt[kvm]
        sq_hist[tt] = p.cnt[f"{tag}_ln_sq"]
    phaseA_kst = p.cnt[kst]

    p.wait("t", knsm)
    p.wait("t", kvev)
    p.op("t", lambda e: e.transpose(pb[4][0:8, 0:128], bufs["knacc"][:, 0:8], C.ident_f32[:, :]), knsm)
    p.op("t", lambda e: e.transpose(pb[5][0:8, 0:128], bufs["qnacc"][:, 0:8], C.ident_f32[:, :]), knsm)
    p.wait("v", knsm)
    p.op("v", lambda e: e.reduce_max(bufs["kn"][0:8, 0:1], pb[4][0:8, 0:128], axis=AX), knsm)
    p.op("v", lambda e: e.reduce_max(bufs["kn"][0:8, 1:2], pb[5][0:8, 0:128], axis=AX), knsm)
    p.wait("t", knsm)
    p.op("t", lambda e: e.transpose(pb[6][0:1, 0:8], bufs["kn"][0:8, 0:1], C.ident_f32[0:8, 0:8]), knsm)
    p.op("t", lambda e: e.transpose(pb[6][0:1, 8:16], bufs["kn"][0:8, 1:2], C.ident_f32[0:8, 0:8]), knsm)
    p.wait("v", knsm)
    p.op("v", lambda e: e.tensor_copy(bufs["brow"][0:1, 0:16], pb[6][0:1, 0:16]), knsm)
    p.wait("v", knsm)
    p.op("v", lambda e: e.tensor_tensor(bufs["brow"][0:1, 0:8], bufs["brow"][0:1, 0:8], bufs["brow"][0:1, 8:16], op=ALU.mult), knsm)
    p.wait("v", knsm)
    p.op("v", lambda e: e.tensor_scalar(bufs["brow"][0:1, 0:8], bufs["brow"][0:1, 0:8], ap["inv_scale"] ** 2, None, op0=ALU.mult), knsm)
    p.wait("s", knsm)
    p.op("s", lambda e: e.activation(bufs["brow"][0:1, 0:8], bufs["brow"][0:1, 0:8], AF.Sqrt), knsm)
    p.wait("v", knsm)
    p.op("v", lambda e: e.tensor_scalar(bufs["brow"][0:1, 0:8], bufs["brow"][0:1, 0:8], -1.0, None, op0=ALU.mult), knsm)
    brow_done = p.cnt[knsm]
    p.wait("sy", kst, phaseA_kst)
    qld = f"{tag}_qld"
    for c in range(NCH):
        p.dma("sy", q_sb[:, c, :], qbuf[:, c * 128:(c + 1) * 128], qld,
              transpose=True)
    p.wait("t", qld)

    NH = ap["nheads"]
    sS, sE, sMK, sAV, sOE = (f"{tag}_S", f"{tag}_E", f"{tag}_MK",
                             f"{tag}_AV", f"{tag}_OE")
    khl, vhl, nbs = f"{tag}_khl", f"{tag}_vhl", f"{tag}_nb"
    exp_hist = [0, 0]
    for h in range(NH):
        hc = h if layer == 0 else h // 2
        dsl = slice(0, 128) if layer == 0 else slice(64 * (h % 2), 64 * (h % 2) + 64)
        p.wait("t", knsm, brow_done)
        p.wait("t", nbs)
        p.wait("t", sE)
        p.wait("t", sOE)
        p.op("t", lambda e, hc_=hc: e.matmul(pb[7][:, 0:1], C.ones1_f32[:, :],
             bufs["brow"][0:1, hc_:hc_ + 1], start=True, stop=True), nbs)
        p.wait("v", nbs)
        p.wait("v", sE)
        p.op("v", lambda e: e.tensor_copy(bufs["negb"][:, 0:1], pb[7][:, 0:1]), nbs)
        negb_c = p.cnt[nbs]
        p.wait("s", nbs, negb_c)
        if layer == 0 or h % 2 == 0:
            p.wait("sy", sS)
            p.wait("sy", sAV)
            p.wait("sy", kvm)
            khl_c = p.dma("sy", khb[:, :], kbuf[:, hc * 128:(hc + 1) * 128],
                          khl, transpose=True)
            for part in range(8):
                p.dma("sy", vhb[:, part * 8:(part + 1) * 8, :],
                      vbuf[part * 1024:(part + 1) * 1024,
                           hc * 128:(hc + 1) * 128]
                      .rearrange("(kt p) d -> p kt d", p=128), vhl)
            vhl_c = p.cnt[vhl]
            p.wait("t", khl, khl_c)
            p.wait("t", vhl, vhl_c)
        for g in range(4):
            nreg = gbs[g] // 512 + 1
            for r in range(nreg):
                kg = 512 * r
                diag = (r == nreg - 1)
                p.wait("t", sE, exp_hist[-1])
                for j in range(4):
                    p.op("t", lambda e, j_=j, kg_=kg, d_=dsl, g_=g, hc_=hc: e.matmul(
                        pb[j_][:, :], khb[d_, kg_ + j_ * 128:kg_ + (j_ + 1) * 128],
                        q_sb[d_, hc_, g_ * 512:(g_ + 1) * 512],
                        start=True, stop=True), sS)
                sS_c = p.cnt[sS]
                p.wait("s", sS, sS_c)
                p.wait("s", sAV)
                p.wait("s", sMK)
                for j in range(4):
                    p.op("s", lambda e, j_=j: e.activation(
                        bufs["p_sb"][:, j_, :], pb[j_][:, :], AF.Exp,
                        bias=bufs["negb"][:, 0:1], scale=bufs["sccol"][:, 0:1]), sE)
                exp_hist.append(p.cnt[sE])
                if diag:
                    p.wait("v", sE)
                    for j in range(4):
                        p.op("v", lambda e, j_=j: e.tensor_tensor(
                            bufs["p_sb"][:, j_, :], bufs["p_sb"][:, j_, :],
                            bufs["masks"][:, j_, :], op=ALU.mult), sMK)
                p.wait("t", sE)
                if diag:
                    p.wait("t", sMK)
                if r == 0:
                    p.wait("t", sOE)
                for j in range(4):
                    p.op("t", lambda e, j_=j, r_=r, nr=nreg: e.matmul(
                        pb[5][0:1, :], C.ones_col_bf[:, :], bufs["p_sb"][:, j_, :],
                        start=(r_ == 0 and j_ == 0),
                        stop=(r_ == nr - 1 and j_ == 3)), sAV)
                vsl = slice(0, 128) if layer == 0 else slice(64 * (h % 2), 64 * (h % 2) + 64)
                osl = slice(0, 128) if layer == 0 else slice(0, 64)
                for j in range(4):
                    p.op("t", lambda e, j_=j, kg_=kg, r_=r, nr=nreg, v_=vsl, o_=osl: e.matmul(
                        pb[6][o_, :], vhb[:, kg_ // 128 + j_, v_],
                        bufs["p_sb"][:, j_, :],
                        start=(r_ == 0 and j_ == 0), stop=(r_ == nr - 1 and j_ == 3)), sAV)
            sAV_c = p.cnt[sAV]
            p.wait("v", sAV, sAV_c)
            p.op("v", lambda e: e.reciprocal(bufs["rrow"][0:1, :], pb[5][0:1, :]), sOE)
            rr_c = p.cnt[sOE]
            p.wait("t", sOE, rr_c)
            p.op("t", lambda e: e.matmul(pb[7][:, :], C.ones1_f32[:, :],
                 bufs["rrow"][0:1, :], start=True, stop=True), sOE)
            p.wait("v", sOE)
            p.op("v", lambda e: e.tensor_copy(bufs["rb_sb"][:, :], pb[7][:, :]), sOE)
            p.wait("v", sOE)
            osl = slice(0, 128) if layer == 0 else slice(0, 64)
            if layer == 0:
                p.op("v", lambda e, g_=g, hc_=hc: e.tensor_tensor(
                    o_all[:, hc_, g_ * 512:(g_ + 1) * 512],
                    pb[6][:, :], bufs["rb_sb"][:, :], op=ALU.mult), sOE)
            else:
                p.wait("v", f"{tag}_ost")
                p.op("v", lambda e, o_=osl: e.tensor_tensor(
                    bufs["yc0"].bitcast(BF16)[o_, 0:512],
                    pb[6][o_, :], bufs["rb_sb"][o_, :], op=ALU.mult), sOE)
                p.wait("g", sOE)
                p.dma("g", o_all[64 * (h % 2):64 * (h % 2) + 64, hc,
                                 g * 512:(g + 1) * 512],
                      bufs["yc0"].bitcast(BF16)[0:64, 0:512], f"{tag}_ost")
    oall_done = p.cnt[sOE]
    ost_done = p.cnt.get(f"{tag}_ost", 0)

    oproj, resid, xcl = f"{tag}_op", f"{tag}_res", f"{tag}_xc"
    yst_hist = []
    for g in range(4):
        for mc in range(NCH):
            idx = g * NCH + mc
            xc = bufs["xc0"] if idx % 2 == 0 else bufs["xc1"]
            yc = bufs["yc0"] if idx % 2 == 0 else bufs["yc1"]
            if idx >= 2:
                p.wait("sy", resid, p.cnt.get(resid, 0) - 1)
            p.dma("sy", xc[:, :], x_d[mc * 128:(mc + 1) * 128,
                                      g * 512:(g + 1) * 512], xcl)
            xcl_c = p.cnt[xcl]
            p.wait("t", sOE, oall_done)
            if layer == 1:
                p.wait("t", f"{tag}_ost", ost_done)
            if g * NCH + mc >= 2:
                p.wait("t", resid, p.cnt.get(resid, 0) - 1)
            bnk = pb[0] if (g * NCH + mc) % 2 == 0 else pb[1]
            for hd in range(NCH):
                p.op("t", lambda e, hd_=hd, mc_=mc, g_=g, bk=bnk: e.matmul(
                    bk[:, :], woT[:, hd_, mc_ * 128:(mc_ + 1) * 128],
                    o_all[:, hd_, g_ * 512:(g_ + 1) * 512],
                    start=(hd_ == 0), stop=(hd_ == 7)), oproj)
            op_c = p.cnt[oproj]
            p.wait("v", oproj, op_c)
            p.wait("v", xcl, xcl_c)
            ystp = f"{tag}_yst" + str(idx % 2)
            if idx >= 2:
                p.wait("v", ystp, yst_hist[idx - 2])
            if layer == 1 and idx == 0:
                p.wait("v", f"{tag}_ost", ost_done)
            p.op("v", lambda e, bk=bnk, xc_=xc, yc_=yc: e.tensor_tensor(
                yc_[:, :], xc_[:, :], bk[:, :], op=ALU.add), resid)
            p.wait("sy", resid)
            p.dma("sy", y_d[mc * 128:(mc + 1) * 128, g * 512:(g + 1) * 512],
                  yc[:, :], ystp)
            yst_hist.append(p.cnt[ystp])
    p.wait("sy", f"{tag}_yst0")
    p.wait("sy", f"{tag}_yst1")
    return None


NTILE = 47


def moe_bufs(p, tag):
    ar = p.sb(f"{tag}_ar", [128, 20480], BF16)
    b = {
        "ar": ar,
        "xln": p.sb(f"{tag}_xln", [128, NCH, TPC], BF16),
        "xs_all": p.sb(f"{tag}_xs", [128, 16, 1024], BF16),
        "gw": p.sb(f"{tag}_gw", [128, NCH, 32], BF16),
        "oh_sb": p.sb(f"{tag}_oh", [128, 16, 32], F32),
        "rank_sb": p.sb(f"{tag}_rank", [128, 16, 32], F32),
        "cs_sb": p.sb(f"{tag}_cs", [1, 16, 32], F32),
        "score": p.sb(f"{tag}_scr", [128, 16], F32),
        "maxv": p.sb(f"{tag}_maxv", [128, 1], F32),
        "eid": p.sb(f"{tag}_eid", [128, 1], F32),
        "eq": p.sb(f"{tag}_eq", [128, 32], F32),
        "sel": p.sb(f"{tag}_sel", [128, 32], F32),
        "imb": p.sb(f"{tag}_imb", [128, 32], F32),
        "cnt_i": p.sb(f"{tag}_cnti", [32, 1], I32),
        "pad_f": p.sb(f"{tag}_padf", [32, 1], F32),
        "cmp": p.sb(f"{tag}_cmp", [32, NTILE], F32),
        "erow": p.sb(f"{tag}_erow", [1, NTILE], F32),
        "esml": p.sb(f"{tag}_esml", [1, NTILE], F32),
        "eoff": p.sb(f"{tag}_eoff", [1, 2], F32),
        "base_row": p.sb(f"{tag}_base", [1, 32], F32),
        "t2": p.sb(f"{tag}_t2", [128, 32], F32),
        "t2r": p.sb(f"{tag}_t2r", [1, 32], F32),
        "posu": p.sb(f"{tag}_posu", [128, 16], U32),
        "posf": p.sb(f"{tag}_posf", [128, 16], F32),
        "idxg": p.sb(f"{tag}_idxg", [128, 8], U32),
        "idxd": p.sb(f"{tag}_idxd", [128, 16], U32),
        "idgf": p.sb(f"{tag}_idgf", [128, 8], F32),
        "iddf": p.sb(f"{tag}_iddf", [128, 16], F32),
        "kcoff": p.sb(f"{tag}_kcoff", [128, 8], F32),
        "ichoff": p.sb(f"{tag}_ichoff", [128, 16], F32),
        "xsf0": p.sb(f"{tag}_xsf0", [128, NCH, 128], BF16),
        "xsf1": p.sb(f"{tag}_xsf1", [128, NCH, 128], BF16),
        "dsb": p.sb(f"{tag}_dsb", [128, 1024], BF16),
        "rtc": p.sb(f"{tag}_rtc", [128, 128], F32),
        "lns": ln_scratch(p, f"{tag}_lns"),
    }
    b["wg0"] = ar[:, 0:2048];      b["wg1"] = ar[:, 2048:4096]
    b["wu0"] = ar[:, 4096:6144];   b["wu1"] = ar[:, 6144:8192]
    b["wd0"] = ar[:, 8192:9216];   b["wd1"] = ar[:, 9216:10240]
    b["gs"] = r3(ar[:, 10240:12288], 4)
    b["us"] = r3(ar[:, 12288:14336], 4)
    b["hh"] = r3(ar[:, 14336:16384], 4)
    b["hT"] = r3(ar[:, 16384:18432], 16)
    b["xs_t0"] = ar[:, 18432:19456]
    b["xs_t1"] = ar[:, 19456:20480]
    b["sgwb0"] = r3(ar[:, 0:1024], NCH);   b["suwb0"] = r3(ar[:, 1024:2048], NCH)
    b["sgwb1"] = r3(ar[:, 2048:3072], NCH); b["suwb1"] = r3(ar[:, 3072:4096], NCH)
    b["sdwb0"] = r3(ar[:, 4096:6144], 16)
    b["sdwb1"] = r3(ar[:, 6144:8192], 16)
    b["h_sh"] = r3(ar[:, 10240:18432], 16)
    b["gsh"] = ar[:, 18432:18944]
    b["ush"] = ar[:, 18944:19456]
    return b


def emit_moe(p, C, tag, x_sb, gwd, sgw_d, suw_d, sdw_d,
             rgw_d, ruw_d, rdw_d, xs_sorted, dout, pb, bufs):
    xln = bufs["xln"]
    lns = bufs["lns"]
    ysh = r3(bufs["xs_all"][:, :, :].rearrange("p a b -> p (a b)"), NCH)
    rt, sc_ld = f"{tag}_rt", f"{tag}_scld"

    for c in range(NCH):
        p.dma("g", bufs["gw"][:, c, :], gwd[c * 128:(c + 1) * 128, :], sc_ld)
    sc_done = p.cnt[sc_ld]
    for e in ("t", "v", "s", "g"):
        p.wait(e, sc_ld, sc_done)
        p.wait(e, "c_ld")
        p.wait(e, "xld")
    p.op("v", lambda e: e.tensor_scalar(bufs["imb"][:, :], C.iota32_f[:, :], -65536.0, None, op0=ALU.add), rt)
    p.op("v", lambda e: e.memset(bufs["base_row"][:, :], 0.0), rt)
    for k in range(8):
        p.op("v", lambda e, k_=k: e.tensor_scalar(bufs["kcoff"][:, k_:k_ + 1], C.iota128_f[:, :], 128.0 * k_, None, op0=ALU.add), rt)
    for k in range(16):
        p.op("v", lambda e, k_=k: e.tensor_scalar(bufs["ichoff"][:, k_:k_ + 1], C.iota128_f[:, :], 128.0 * k_, None, op0=ALU.add), rt)

    lgm, ohm, sct = f"{tag}_lg", f"{tag}_ohm", f"{tag}_sct"
    cntm, rkm, rke = f"{tag}_cntm", f"{tag}_rkm", f"{tag}_rke"
    for tt in range(4):
        nm = emit_ln(p, C, f"{tag}_ln", x_sb, xln, tt * 512, 512, pb[0:4], lns)
        for sub4 in range(4):
            s = tt * 4 + sub4
            tsl = slice(tt * 512 + sub4 * 128, tt * 512 + sub4 * 128 + 128)
            p.wait("t", nm)
            p.wait("t", rt)
            p.wait("t", sct)
            for c in range(NCH):
                p.op("t", lambda e, c_=c, ts_=tsl: e.matmul(
                    pb[4][:, 0:32], xln[:, c_, ts_], bufs["gw"][:, c_, :],
                    start=(c_ == 0), stop=(c_ == 7)), lgm)
            lg_c = p.cnt[lgm]
            p.wait("v", lgm, lg_c)
            p.op("v", lambda e: e.reduce_max(bufs["maxv"][:, :], pb[4][:, 0:32], axis=AX), rt)
            p.wait("v", rt)
            p.op("v", lambda e: e.tensor_scalar(bufs["eq"][:, :], pb[4][:, 0:32], bufs["maxv"][:, 0:1], None, op0=ALU.is_equal), rt)
            p.wait("v", rt)
            p.op("v", lambda e: e.tensor_tensor(bufs["sel"][:, :], bufs["eq"][:, :], bufs["imb"][:, :], op=ALU.mult), rt)
            p.wait("v", rt)
            p.op("v", lambda e: e.tensor_scalar(bufs["sel"][:, :], bufs["sel"][:, :], 65536.0, None, op0=ALU.add), rt)
            p.wait("v", rt)
            p.op("v", lambda e: e.tensor_reduce(bufs["eid"][:, :], bufs["sel"][:, :], axis=AX, op=ALU.min), rt)
            p.wait("v", rt)
            p.op("v", lambda e, s_=s: e.tensor_scalar(bufs["oh_sb"][:, s_, :], C.iota32_f[:, :], bufs["eid"][:, 0:1], None, op0=ALU.is_equal), rt)
            rt_oh = p.cnt[rt]
            p.wait("s", rt, rt_oh)
            p.op("s", lambda e, s_=s: e.activation(bufs["score"][:, s_:s_ + 1], bufs["maxv"][:, :], AF.Sigmoid), ohm)
            score_c = p.cnt[ohm]
            p.wait("t", rt, rt_oh)
            p.op("t", lambda e, s_=s: e.matmul(
                pb[5][0:32, 0:1], bufs["oh_sb"][:, s_, :], C.ones_col_f32[:, :],
                start=(s_ == 0), stop=(s_ == 15)), cntm)
            p.wait("t", rke)
            p.op("t", lambda e, s_=s: e.matmul(
                pb[6][:, 0:32], C.tri128[:, :], bufs["oh_sb"][:, s_, :],
                start=True, stop=True), rkm)
            p.op("t", lambda e, s_=s: e.matmul(
                pb[7][0:1, 0:32], C.ones_col_f32[:, :], bufs["oh_sb"][:, s_, :],
                start=True, stop=True), rkm)
            rkm_c = p.cnt[rkm]
            p.wait("v", rkm, rkm_c)
            p.op("v", lambda e, s_=s: e.tensor_copy(bufs["rank_sb"][:, s_, :], pb[6][:, 0:32]), rke)
            p.op("v", lambda e, s_=s: e.tensor_copy(bufs["cs_sb"][0:1, s_, :], pb[7][0:1, 0:32]), rke)
            pTa = pb[4].bitcast(BF16)
            p.wait("t", rt, rt_oh)
            for c in range(NCH):
                p.op("t", lambda e, c_=c, ts_=tsl, pT_=pTa: e.transpose(
                    pT_[:, c_ * 128:(c_ + 1) * 128], xln[:, c_, ts_],
                    C.ident_bf[:, :]), sct)
            sct_c = p.cnt[sct]
            p.wait("v", sct, sct_c)
            p.wait("v", ohm, score_c)
            p.op("v", lambda e, s_=s, pT_=pTa: e.tensor_scalar(
                bufs["xs_all"][:, s_, :], pT_[:, 0:1024], bufs["score"][:, s_:s_ + 1],
                None, op0=ALU.mult), sct)
    cnt_done = p.cnt[cntm]
    xln_done = p.cnt[f"{tag}_ln_nm"]

    a3, posb = f"{tag}_a3", f"{tag}_posb"
    p.wait("v", cntm, cnt_done)
    p.op("v", lambda e: e.tensor_copy(bufs["cnt_i"][:, :], pb[5][0:32, 0:1]), a3)
    p.wait("v", a3)
    p.op("v", lambda e: e.tensor_scalar(bufs["cnt_i"][:, :], bufs["cnt_i"][:, :], 127.0, None, op0=ALU.add), a3)
    p.wait("v", a3)
    p.op("v", lambda e: e.tensor_scalar(bufs["cnt_i"][:, :], bufs["cnt_i"][:, :], 7, None, op0=ALU.arith_shift_right), a3)
    p.wait("v", a3)
    p.op("v", lambda e: e.tensor_scalar(bufs["cnt_i"][:, :], bufs["cnt_i"][:, :], 7, None, op0=ALU.arith_shift_left), a3)
    p.wait("v", a3)
    p.op("v", lambda e: e.tensor_copy(bufs["pad_f"][:, :], bufs["cnt_i"][:, :]), a3)
    a3_pad = p.cnt[a3]
    p.wait("t", a3, a3_pad)
    p.wait("t", sct)
    p.op("t", lambda e: e.matmul(pb[4][0:32, 0:1], C.tri32[:, :], bufs["pad_f"][:, :], start=True, stop=True), a3)
    po_c = p.cnt[a3]
    p.wait("v", a3, po_c)
    p.op("v", lambda e: e.tensor_scalar(bufs["cmp"][:, :], C.tbases[:, 0:NTILE], pb[4][0:32, 0:1], None, op0=ALU.is_ge), a3)
    cmp_c = p.cnt[a3]
    p.op("v", lambda e: e.tensor_copy(bufs["pad_f"][:, :], pb[4][0:32, 0:1]), a3)
    pocp = p.cnt[a3]
    p.wait("t", a3, cmp_c)
    p.wait("t", a3, pocp)
    p.op("t", lambda e: e.matmul(pb[6][0:1, 0:NTILE], C.ones_col_f32[0:32, :], bufs["cmp"][:, :], start=True, stop=True), a3)
    p.op("t", lambda e: e.transpose(pb[7][0:1, 0:32], bufs["pad_f"][:, :], C.ident_f32[0:32, 0:32]), a3)
    por_c = p.cnt[a3]
    p.wait("v", a3, por_c)
    p.op("v", lambda e: e.tensor_scalar(bufs["erow"][0:1, :], pb[6][0:1, 0:NTILE], -1.0, None, op0=ALU.add), a3)
    p.op("v", lambda e: e.tensor_copy(bufs["t2r"][0:1, :], pb[7][0:1, 0:32]), a3)
    a3_er = p.cnt[a3]
    p.wait("t", a3, a3_er)
    p.op("t", lambda e: e.matmul(pb[5][0:1, 0:1], bufs["pad_f"][:, :], C.ones_col_f32[0:32, 0:1], start=True, stop=True), a3)
    tot_c = p.cnt[a3]
    p.wait("v", a3, tot_c)
    p.op("v", lambda e: e.tensor_scalar(bufs["esml"][0:1, :], C.tbrow[0:1, 0:NTILE], pb[5][0:1, 0:1], None, op0=ALU.is_ge), a3)
    p.wait("v", a3)
    p.op("v", lambda e: e.tensor_scalar(bufs["esml"][0:1, :], bufs["esml"][0:1, :], 3e9, None, op0=ALU.mult), a3)
    a3_done = p.cnt[a3]

    pos = f"{tag}_pos"
    for s in range(16):
        p.wait("v", pos)
        p.wait("v", a3, a3_done)
        p.op("v", lambda e: e.tensor_tensor(bufs["t2"][0:1, 0:32], bufs["t2r"][0:1, :], bufs["base_row"][0:1, :], op=ALU.add), pos)
        pc = p.cnt[pos]
        p.wait("t", pos, pc)
        p.wait("t", posb)
        p.op("t", lambda e: e.matmul(pb[6][:, 0:32], C.ones1_f32[:, :], bufs["t2"][0:1, 0:32], start=True, stop=True), posb)
        pb_c = p.cnt[posb]
        p.wait("v", posb, pb_c)
        p.op("v", lambda e, s_=s: e.tensor_tensor(bufs["t2"][:, 0:32], bufs["rank_sb"][:, s_, :], pb[6][:, 0:32], op=ALU.add), pos)
        p.wait("v", pos)
        p.op("v", lambda e, s_=s: e.tensor_tensor(bufs["t2"][:, 0:32], bufs["t2"][:, 0:32], bufs["oh_sb"][:, s_, :], op=ALU.mult), pos)
        p.wait("v", pos)
        p.op("v", lambda e, s_=s: e.reduce_sum(bufs["posf"][:, s_:s_ + 1], bufs["t2"][:, 0:32], axis=AX), pos)
        p.wait("v", pos)
        p.op("v", lambda e, s_=s: e.tensor_copy(bufs["posu"][:, s_:s_ + 1], bufs["posf"][:, s_:s_ + 1]), pos)
        p.op("v", lambda e, s_=s: e.tensor_tensor(bufs["base_row"][0:1, :], bufs["base_row"][0:1, :], bufs["cs_sb"][0:1, s_, :], op=ALU.add), pos)
    pos_done = p.cnt[pos]

    scat = f"{tag}_scat"
    zs = f"{tag}_zs"
    p.op("g", lambda e: e.memset(bufs["dsb"][:, :], 0.0), zs + "m")
    p.wait("g", zs + "m")
    for jz in range(NTILE):
        p.dma("g", xs_sorted[jz * 128:(jz + 1) * 128, :], bufs["dsb"][:, :], zs)
    p.wait("g", zs)
    p.wait("g", pos, pos_done)
    for s in range(16):
        p.op("g", lambda e, s_=s: e.indirect_dma_start(
            out=xs_sorted[:, :],
            out_offset=IndirectOffsetOnAxis(ap=bufs["posu"][:, s_:s_ + 1], axis=0),
            in_=bufs["xs_all"][:, s_, :], in_offset=None,
            oob_is_err=False), scat, 16)
    scat_done = p.cnt[scat]

    gth, xsg, rmm, rev, rhm, rtp, rdm, rde, rst, idxs = (
        f"{tag}_gth", f"{tag}_xsg", f"{tag}_rmm", f"{tag}_rev", f"{tag}_rhm",
        f"{tag}_rtp", f"{tag}_rdm", f"{tag}_rde", f"{tag}_rst", f"{tag}_idx")
    REGS = {}

    def _init_regs(e):
        REGS["bc_g"] = e.alloc_register("moe_bc_g")
        e.reg_mov(REGS["bc_g"], 32767)
        REGS["bc_d"] = e.alloc_register("moe_bc_d")
        e.reg_mov(REGS["bc_d"], 65535)
    p.emit("g", _init_regs)
    rmm_hist = {}
    rtp_hist = {}
    wuse = {}
    duse = {}
    for j in range(NTILE):
        par = j % 2
        wg = bufs["wg0"] if par == 0 else bufs["wg1"]
        wu = bufs["wu0"] if par == 0 else bufs["wu1"]
        wd = bufs["wd0"] if par == 0 else bufs["wd1"]
        xst = bufs["xs_t0"] if par == 0 else bufs["xs_t1"]
        xsf = bufs["xsf0"] if par == 0 else bufs["xsf1"]
        gthp, xsgp = gth + str(par), xsg + str(par)
        p.wait("v", a3, a3_done)
        p.wait("v", posb)
        p.op("v", lambda e, j_=j: e.tensor_scalar(
            bufs["eoff"][0:1, 0:1], bufs["erow"][0:1, j_:j_ + 1], 1024.0,
            bufs["esml"][0:1, j_:j_ + 1], op0=ALU.mult, op1=ALU.add), idxs)
        p.op("v", lambda e, j_=j: e.tensor_scalar(
            bufs["eoff"][0:1, 1:2], bufs["erow"][0:1, j_:j_ + 1], 2048.0,
            bufs["esml"][0:1, j_:j_ + 1], op0=ALU.mult, op1=ALU.add), idxs)
        ei_c = p.cnt[idxs]
        p.wait("t", idxs, ei_c)
        p.wait("t", posb)
        p.op("t", lambda e: e.matmul(pb[6][:, 0:2], C.ones1_f32[:, :], bufs["eoff"][0:1, 0:2], start=True, stop=True), posb)
        eb_c = p.cnt[posb]
        p.wait("v", posb, eb_c)
        for gs_ in range(4):
            p.wait("v", gth + str(gs_))
        p.wait("v", gth + "d0")
        p.wait("v", gth + "d1")
        p.op("v", lambda e: e.tensor_scalar(bufs["idgf"][:, :], bufs["kcoff"][:, :], pb[6][:, 0:1], None, op0=ALU.add), idxs)
        p.op("v", lambda e: e.tensor_scalar(bufs["iddf"][:, :], bufs["ichoff"][:, :], pb[6][:, 1:2], None, op0=ALU.add), idxs)
        p.wait("v", idxs)
        p.op("v", lambda e: e.tensor_copy(bufs["idxg"][:, :], bufs["idgf"][:, :]), idxs)
        p.op("v", lambda e: e.tensor_copy(bufs["idxd"][:, :], bufs["iddf"][:, :]), idxs)
        idx_c = p.cnt[idxs]
        p.wait("g", idxs, idx_c)
        p.wait("g", scat, scat_done)
        p.wait("g", rtp, rtp_hist.get(j - 2, 0))
        p.dma("g", xst[:, :], xs_sorted[j * 128:(j + 1) * 128, :], xsgp)
        xsg_c = p.cnt[xsgp]
        p.wait("t", xsgp, xsg_c)
        p.wait("t", rde)
        pTr = pb[0].bitcast(BF16)
        for c in range(NCH):
            p.op("t", lambda e, c_=c, x_=xst, pT_=pTr: e.transpose(
                pT_[:, c_ * 128:(c_ + 1) * 128],
                r3(x_[:, :], NCH)[:, c_, :], C.ident_bf[:, :]), rtp)
        rtp_c = p.cnt[rtp]
        p.wait("v", rtp, rtp_c)
        p.op("v", lambda e, x_=xsf, pT_=pTr: e.tensor_copy(
            x_[:, :, :], r3(pT_[:, 0:1024], NCH)), rtp)
        xsf_c = p.cnt[rtp]
        rtp_hist[j] = xsf_c
        p.wait("t", rtp, xsf_c)
        p.wait("t", rev)
        for half, (wsrc, w0, w1, pbase) in enumerate(
                [(rgw_d, bufs["wg0"], bufs["wg1"], 0),
                 (ruw_d, bufs["wu0"], bufs["wu1"], 4)]):
            for c in range(NCH):
                wpar = w0 if c % 2 == 0 else w1
                gsem = gth + str(half * 2 + c % 2)
                p.wait("g", rmm, wuse.get((j, half, c - 2), wuse.get((j - 1, half, 6 + c), 0)))
                p.op("g", lambda e, c_=c, w=wpar, src=wsrc: e.indirect_dma_start(
                    out=w[:, :], out_offset=None, in_=src[:, :],
                    in_offset=IndirectOffsetOnAxis(ap=bufs["idxg"][:, c_:c_ + 1], axis=0),
                    bounds_check=REGS["bc_g"], oob_is_err=False), gsem, 16)
                g_c = p.cnt[gsem]
                p.wait("t", gsem, g_c)
                for ig in range(4):
                    p.op("t", lambda e, ig_=ig, c_=c, x_=xsf, w=wpar, pb0=pbase: e.matmul(
                        pb[pb0 + ig_][:, :], x_[:, c_, :],
                        w[:, ig_ * 512:(ig_ + 1) * 512],
                        start=(c_ == 0), stop=(c_ == 7)), rmm)
                wuse[(j, half, c)] = p.cnt[rmm]
        rmm_c = p.cnt[rmm]
        rmm_hist[j] = rmm_c
        p.wait("s", rmm, rmm_c)
        p.wait("s", rhm)
        for ig in range(4):
            p.op("s", lambda e, ig_=ig: e.activation(bufs["gs"][:, ig_, :], pb[ig_][:, :], AF.Silu), rev)
            p.op("s", lambda e, ig_=ig: e.copy(bufs["us"][:, ig_, :], pb[4 + ig_][:, :]), rev)
        rev_c = p.cnt[rev]
        p.wait("v", rev, rev_c)
        p.wait("v", rtp)
        for ig in range(4):
            p.op("v", lambda e, ig_=ig: e.tensor_tensor(bufs["hh"][:, ig_, :], bufs["gs"][:, ig_, :], bufs["us"][:, ig_, :], op=ALU.mult), rhm)
        rhm_c = p.cnt[rhm]
        p.wait("t", rhm, rhm_c)
        p.wait("t", rdm)
        pT1r = pb[1].bitcast(BF16)
        for q in range(4):
            for w4 in range(4):
                icc = q * 4 + w4
                p.op("t", lambda e, icc_=icc, w4_=w4, pT_=pT1r: e.transpose(
                    pT_[:, w4_ * 128:(w4_ + 1) * 128],
                    bufs["hh"][:, icc_ // 4, (icc_ % 4) * 128:(icc_ % 4 + 1) * 128],
                    C.ident_bf[:, :]), rtp)
            qc = p.cnt[rtp]
            p.wait("v", rtp, qc)
            p.op("v", lambda e, q_=q, pT_=pT1r: e.tensor_copy(
                bufs["hT"][:, 4 * q_:4 * q_ + 4, :], r3(pT_[:, 0:512], 4)), rtp)
            hTq = p.cnt[rtp]
            p.wait("t", rtp, hTq)
        hT_c = p.cnt[rtp]
        p.wait("t", rtp, hT_c)
        p.wait("t", rde)
        for ic in range(16):
            wdp = bufs["wd0"] if ic % 2 == 0 else bufs["wd1"]
            dsem = gth + "d" + str(ic % 2)
            p.wait("g", rdm, duse.get((j, ic - 2), duse.get((j - 1, 14 + ic), 0)))
            p.op("g", lambda e, ic_=ic, w=wdp: e.indirect_dma_start(
                out=w[:, :], out_offset=None, in_=rdw_d[:, :],
                in_offset=IndirectOffsetOnAxis(ap=bufs["idxd"][:, ic_:ic_ + 1], axis=0),
                bounds_check=REGS["bc_d"], oob_is_err=False), dsem, 16)
            d_c = p.cnt[dsem]
            p.wait("t", dsem, d_c)
            for dg in range(2):
                p.op("t", lambda e, ic_=ic, dg_=dg, w=wdp: e.matmul(
                    pb[2 + dg_][:, :], bufs["hT"][:, ic_, :],
                    w[:, dg_ * 512:(dg_ + 1) * 512],
                    start=(ic_ == 0), stop=(ic_ == 15)), rdm)
            duse[(j, ic)] = p.cnt[rdm]
        rdm_c = p.cnt[rdm]
        p.wait("s", rdm, rdm_c)
        p.wait("s", rst)
        p.wait("s", f"{tag}_zs")
        p.op("s", lambda e: e.copy(bufs["dsb"][:, 0:512], pb[2][:, :]), rde)
        p.op("s", lambda e: e.copy(bufs["dsb"][:, 512:1024], pb[3][:, :]), rde)
        rde_c = p.cnt[rde]
        p.wait("g", rde, rde_c)
        p.dma("g", dout[j * 128:(j + 1) * 128, :], bufs["dsb"][:, :], rst)
    routed_done = p.cnt[rst]
    routed_mm_done = p.cnt[rdm]

    shw, shm, she, shh, shy = (f"{tag}_shw", f"{tag}_shm", f"{tag}_she",
                               f"{tag}_shh", f"{tag}_shy")
    shw_hist = {}
    for tt in range(4):
        ts_ = slice(tt * 512, (tt + 1) * 512)
        for ic in range(16):
            par = ic % 2
            wgs = bufs["sgwb0"] if par == 0 else bufs["sgwb1"]
            wus = bufs["suwb0"] if par == 0 else bufs["suwb1"]
            shwp = shw + str(par)
            p.wait("g", shm, shw_hist.get((tt, ic - 2), 0))
            if tt == 0 and ic < 2:
                p.wait("g", rmm, p.cnt[rmm])
                p.wait("g", rdm, routed_mm_done)
                p.wait("g", rtp, p.cnt[rtp])
                p.wait("g", scat, scat_done)
            p.dma("g", wgs[:, :, :], sgw_d[:, ic * 128:(ic + 1) * 128].rearrange(
                "(c p) m -> p c m", p=128), shwp)
            p.dma("g", wus[:, :, :], suw_d[:, ic * 128:(ic + 1) * 128].rearrange(
                "(c p) m -> p c m", p=128), shwp)
            shw_c = p.cnt[shwp]
            gb_ = pb[4 + par]
            ub_ = pb[6 + par]
            p.wait("t", shwp, shw_c)
            p.wait("t", f"{tag}_ln_nm", xln_done)
            p.wait("t", she)
            for c in range(NCH):
                p.op("t", lambda e, c_=c, w=wgs, ts2=ts_, b=gb_: e.matmul(
                    b[:, :], w[:, c_, :], xln[:, c_, ts2],
                    start=(c_ == 0), stop=(c_ == 7)), shm)
            for c in range(NCH):
                p.op("t", lambda e, c_=c, w=wus, ts2=ts_, b=ub_: e.matmul(
                    b[:, :], w[:, c_, :], xln[:, c_, ts2],
                    start=(c_ == 0), stop=(c_ == 7)), shm)
            shm_c = p.cnt[shm]
            shw_hist[(tt, ic)] = shm_c
            p.wait("s", shm, shm_c)
            p.wait("s", shh)
            p.op("s", lambda e, b=gb_: e.activation(bufs["gsh"][:, :], b[:, :], AF.Silu), she)
            p.op("s", lambda e, b=ub_: e.copy(bufs["ush"][:, :], b[:, :]), she)
            she_c = p.cnt[she]
            p.wait("v", she, she_c)
            p.op("v", lambda e, ic_=ic: e.tensor_tensor(bufs["h_sh"][:, ic_, :], bufs["gsh"][:, :], bufs["ush"][:, :], op=ALU.mult), shh)
        h_done = p.cnt[shh]
        for dm in range(NCH):
            par = dm % 2
            wds = bufs["sdwb0"] if par == 0 else bufs["sdwb1"]
            shwp = shw + "d" + str(par)
            p.wait("g", shm, shw_hist.get((tt, "d", dm - 2), 0))
            p.dma("g", wds[:, :, :], sdw_d[:, dm * 128:(dm + 1) * 128].rearrange(
                "(c p) m -> p c m", p=128), shwp)
            shwd_c = p.cnt[shwp]
            yb = pb[2 + par]
            p.wait("t", shwp, shwd_c)
            p.wait("t", shh, h_done)
            p.wait("t", shy)
            for ic in range(16):
                p.op("t", lambda e, ic_=ic, w=wds, b=yb: e.matmul(
                    b[:, :], w[:, ic_, :], bufs["h_sh"][:, ic_, :],
                    start=(ic_ == 0), stop=(ic_ == 15)), shm)
            y_c = p.cnt[shm]
            shw_hist[(tt, "d", dm)] = y_c
            p.wait("s", shm, y_c)
            p.wait("s", shy)
            p.op("s", lambda e, dm_=dm, ts2=ts_, b=yb: e.copy(ysh[:, dm_, ts2], b[:, :]), shy)
    ysh_done = p.cnt[shy]

    cg, ctp, cres = f"{tag}_cg", f"{tag}_ctp", f"{tag}_cres"
    rt0, rt1 = bufs["xs_t0"], bufs["xs_t1"]
    ctp_hist = {}
    for s in range(16):
        par = s % 2
        rtb = rt0 if par == 0 else rt1
        cgp = cg + str(par)
        tsl = slice(s * 128, (s + 1) * 128)
        p.wait("g", rst, routed_done)
        if s < 2:
            p.wait("g", shm, p.cnt[shm])
        p.wait("g", ctp, ctp_hist.get(s - 2, 0))
        p.op("g", lambda e, s_=s, b=rtb: e.indirect_dma_start(
            out=b[:, :], out_offset=None, in_=dout[:, :],
            in_offset=IndirectOffsetOnAxis(ap=bufs["posu"][:, s_:s_ + 1], axis=0),
            oob_is_err=False), cgp, 16)
        cg_c = p.cnt[cgp]
        p.wait("t", cgp, cg_c)
        p.wait("t", cres)
        p.wait("t", shy, ysh_done)
        pTc = pb[0].bitcast(BF16)
        for c in range(NCH):
            p.op("t", lambda e, c_=c, b=rtb, pT_=pTc: e.transpose(
                pT_[:, c_ * 128:(c_ + 1) * 128], r3(b[:, :], NCH)[:, c_, :],
                C.ident_bf[:, :]), ctp)
        ctp_c = p.cnt[ctp]
        ctp_hist[s] = ctp_c
        p.wait("v", ctp, ctp_c)
        for c in range(NCH):
            p.wait("v", cres)
            p.op("v", lambda e, c_=c, ts2=tsl, pT_=pTc: e.tensor_tensor(
                bufs["rtc"][:, :], pT_[:, c_ * 128:(c_ + 1) * 128], ysh[:, c_, ts2], op=ALU.add), cres)
            p.wait("v", cres)
            p.op("v", lambda e, c_=c, ts2=tsl: e.scalar_tensor_tensor(
                x_sb[:, c_, ts2], bufs["rtc"][:, :], 0.5, x_sb[:, c_, ts2],
                op0=ALU.mult, op1=ALU.add), cres)
    return cres


def build_moe():
    p = Prog("moe")
    nc = p.nc
    x_d = nc.dram_tensor("x_fm", [D, TPC], F32, kind="ExternalInput").ap()
    gwd = nc.dram_tensor("gate_w", [D, 32], F32, kind="ExternalInput").ap()
    sgw_d = nc.dram_tensor("sgw", [D, 2048], F32, kind="ExternalInput").ap()
    suw_d = nc.dram_tensor("suw", [D, 2048], F32, kind="ExternalInput").ap()
    sdw_d = nc.dram_tensor("sdw", [2048, D], F32, kind="ExternalInput").ap()
    rgw_d = nc.dram_tensor("rgw", [32 * 1024, 2048], F32, kind="ExternalInput").ap()
    ruw_d = nc.dram_tensor("ruw", [32 * 1024, 2048], F32, kind="ExternalInput").ap()
    rdw_d = nc.dram_tensor("rdw", [32 * 2048, 1024], F32, kind="ExternalInput").ap()
    y_d = nc.dram_tensor("y_fm", [D, TPC], F32, kind="ExternalOutput").ap()
    xs_sorted = nc.dram_tensor("xs_sorted", [NTILE * 128, D], BF16).ap()
    dout = nc.dram_tensor("dout", [NTILE * 128, D], BF16).ap()

    C = Consts(p)
    x_sb = p.sb("x_sb", [128, NCH, TPC], F32)
    pb = [p.psum(f"pb{i}", [128, 512], F32) for i in range(8)]
    bufs = moe_bufs(p, "m")
    for c in range(NCH):
        p.dma("sy", x_sb[:, c, :], x_d[c * 128:(c + 1) * 128, :], "xld")
    cres = emit_moe(p, C, "m", x_sb, gwd, sgw_d, suw_d, sdw_d,
                    rgw_d, ruw_d, rdw_d, xs_sorted, dout, pb, bufs)
    p.wait("sy", cres)
    for c in range(NCH):
        p.dma("sy", y_d[c * 128:(c + 1) * 128, :], x_sb[:, c, :], "yst")
    p.wait("sy", "yst")
    return p


def moe_inputs(core, x_fm, gate_w, sgw, suw, sdw, rgw, ruw, rdw):
    cd = make_consts()
    return dict(
        x_fm=np.ascontiguousarray(x_fm),
        gate_w=np.ascontiguousarray(gate_w),
        sgw=np.ascontiguousarray(sgw), suw=np.ascontiguousarray(suw),
        sdw=np.ascontiguousarray(sdw),
        rgw=np.ascontiguousarray(rgw.reshape(32 * 1024, 2048)),
        ruw=np.ascontiguousarray(ruw.reshape(32 * 1024, 2048)),
        rdw=np.ascontiguousarray(rdw.reshape(32 * 2048, 1024)),
        **cd,
    )


def build_attn(layer, rr):
    p = Prog(f"attn{layer}_{rr}")
    nc = p.nc
    x_d = nc.dram_tensor("x_fm", [D, TPC], F32, kind="ExternalInput").ap()
    xfull_d = nc.dram_tensor("xfull_fm", [D, S], F32, kind="ExternalInput").ap()
    wq_d = nc.dram_tensor("wq", [D, D], F32, kind="ExternalInput").ap()
    wk_d = nc.dram_tensor("wk", [D, D], F32, kind="ExternalInput").ap()
    wv_d = nc.dram_tensor("wv", [D, D], F32, kind="ExternalInput").ap()
    woT_d = nc.dram_tensor("woT", [D, D], F32, kind="ExternalInput").ap()
    cs_d = nc.dram_tensor("csP", [S, D], BF16, kind="ExternalInput").ap()
    masks_d = nc.dram_tensor("masksT", [4, 128, 512], BF16, kind="ExternalInput").ap()
    sccol_d = nc.dram_tensor("sccol", [128, 1], F32, kind="ExternalInput").ap()
    y_d = nc.dram_tensor("y_fm", [D, TPC], F32, kind="ExternalOutput").ap()
    kbuf = nc.dram_tensor("kbuf", [S, D], BF16).ap()
    vbuf = nc.dram_tensor("vbuf", [S, D], BF16).ap()
    qbuf = nc.dram_tensor("qbuf", [TPC, D], BF16).ap()

    C = Consts(p)
    pb = [p.psum(f"pb{i}", [128, 512], F32) for i in range(8)]
    bufs = attn_bufs(p, "at")
    emit_attn(p, C, "at", layer, rr, xfull_d, x_d, y_d, wq_d, wk_d, wv_d,
              woT_d, cs_d, masks_d, sccol_d, kbuf, vbuf, qbuf, pb, bufs)
    return p


def rope_table():
    half = 64
    dims = np.arange(half, dtype=np.float32)
    ang = np.power(np.float32(10000.0), -dims / np.float32(half))
    radians = np.arange(S, dtype=np.float32)[:, None] * ang[None, :]
    cos = np.cos(radians).astype(np.float32)
    sin = np.sin(radians).astype(np.float32)
    cs = np.concatenate([cos, sin], axis=1)
    csP = np.broadcast_to(cs[:, None, :], (S, 8, 128)).reshape(S, 1024)
    return np.ascontiguousarray(csP.astype(BF16NP))


def attn_masks():
    m = np.zeros((4, 128, 512), np.float32)
    kk = np.arange(128)[:, None]
    qq = np.arange(512)[None, :]
    for j in range(4):
        m[j] = (qq >= kk + 128 * j)
    return m.astype(BF16NP)


def attn_inputs(core, layer, x_fm, x_full, wq, wk, wv, wo):
    b = core // 4
    cd = make_consts()
    ap = attn_params(layer)
    if layer == 0:
        wqr = wq.transpose(1, 0, 2).reshape(D, D)
        wkr = wk.transpose(1, 0, 2).reshape(D, D)
        wvr = wv.transpose(1, 0, 2).reshape(D, D)
        woTr = wo.reshape(D, D)
    else:
        wqr = wq.transpose(2, 0, 1).reshape(D, D)
        wkr = wk.transpose(2, 0, 1).reshape(D, D)
        wvr = wv.transpose(2, 0, 1).reshape(D, D)
        woTr = wo.T
    xfull_fm = np.ascontiguousarray(x_full[b].T)
    return dict(
        x_fm=np.ascontiguousarray(x_fm),
        xfull_fm=xfull_fm,
        wq=np.ascontiguousarray(wqr), wk=np.ascontiguousarray(wkr),
        wv=np.ascontiguousarray(wvr), woT=np.ascontiguousarray(woTr),
        csP=rope_table(), masksT=attn_masks(),
        sccol=np.full((128, 1), ap["inv_scale"], np.float32),
        **cd,
    )


_PROGS = {}


def _get_prog(name, builder):
    if name not in _PROGS:
        _PROGS[name] = builder().build()
    return _PROGS[name]


def kernel(**inputs):
    x = np.asarray(inputs["L0_attn_x"], np.float32)
    gw = np.asarray(inputs["L0_ffn_gate_w"], np.float32)
    sgw = np.asarray(inputs["L0_ffn_shared_gate_w"], np.float32)
    suw = np.asarray(inputs["L0_ffn_shared_up_w"], np.float32)
    sdw = np.asarray(inputs["L0_ffn_shared_down_w"], np.float32)
    rgw = np.asarray(inputs["L0_ffn_routed_gate_w"], np.float32)
    ruw = np.asarray(inputs["L0_ffn_routed_up_w"], np.float32)
    rdw = np.asarray(inputs["L0_ffn_routed_down_w"], np.float32)
    wq0 = np.asarray(inputs["L0_attn_wq"], np.float32)
    wk0 = np.asarray(inputs["L0_attn_wk"], np.float32)
    wv0 = np.asarray(inputs["L0_attn_wv"], np.float32)
    wo0 = np.asarray(inputs["L0_attn_wo"], np.float32)
    wq1 = np.asarray(inputs["L1_attn_wq"], np.float32)
    wk1 = np.asarray(inputs["L1_attn_wk"], np.float32)
    wv1 = np.asarray(inputs["L1_attn_wv"], np.float32)
    wo1 = np.asarray(inputs["L1_attn_wo"], np.float32)
    W1 = np.asarray(inputs["L1_ffn_W1"], np.float32)
    b1 = np.asarray(inputs["L1_ffn_b1"], np.float32)
    W2 = np.asarray(inputs["L1_ffn_W2"], np.float32)
    b2 = np.asarray(inputs["L1_ffn_b2"], np.float32)
    scale = np.float32(np.asarray(inputs["L1_ffn_scale"]))

    shards = shard_x(x)

    def apply_moe(sh):
        nc = _get_prog("moe", build_moe)
        im = [moe_inputs(c, sh[c], gw, sgw, suw, sdw, rgw, ruw, rdw)
              for c in range(NCORE)]
        res = run_bass_kernel_spmd(nc, im, core_ids=list(range(NCORE)))
        return [res.results[i]["y_fm"] for i in range(NCORE)]

    def apply_attn(layer, sh):
        xfull = unshard_x(sh)
        wq, wk, wv, wo = ((wq0, wk0, wv0, wo0) if layer == 0
                          else (wq1, wk1, wv1, wo1))
        outs = [None] * NCORE
        for rr in range(4):
            nc = _get_prog(f"attn{layer}_{rr}",
                           lambda layer=layer, rr=rr: build_attn(layer, rr))
            cores = [rr, rr + 4]
            im = [attn_inputs(c, layer, sh[c], xfull, wq, wk, wv, wo)
                  for c in cores]
            res = run_bass_kernel_spmd(nc, im, core_ids=[0, 1])
            outs[cores[0]] = res.results[0]["y_fm"]
            outs[cores[1]] = res.results[1]["y_fm"]
        return outs

    def apply_ffn1(sh):
        xfull = unshard_x(sh)
        nc = _get_prog("ffn1", build_ffn1)
        im = [ffn1_inputs(c, sh[c], W1, b1, W2, b2, scale, xfull)
              for c in range(NCORE)]
        res = run_bass_kernel_spmd(nc, im, core_ids=list(range(NCORE)))
        return [res.results[i]["y_fm"] for i in range(NCORE)]

    shards = apply_moe(shards)
    shards = apply_attn(0, shards)
    shards = apply_moe(shards)
    shards = apply_ffn1(shards)
    shards = apply_attn(1, shards)
    shards = apply_ffn1(shards)
    return unshard_x(shards).astype(np.float32)


# version 02
# speedup vs baseline: 2.1769x; 2.1769x over previous
import numpy as np
import ml_dtypes
from contextlib import ExitStack

import concourse.bass as bass
import concourse.mybir as mybir
from concourse.bass import IndirectOffsetOnAxis
from concourse.bass_utils import run_bass_kernel_spmd

F32 = mybir.dt.float32
BF16 = mybir.dt.bfloat16
U32 = mybir.dt.uint32
I32 = mybir.dt.int32
AF = mybir.ActivationFunctionType
ALU = mybir.AluOpType
AX = mybir.AxisListType.X

BF16NP = ml_dtypes.bfloat16

B, S, D = 2, 8192, 1024
NCORE = 8
TPC = 2048
NCH = 8


def core_blocks(c):
    r = c % 4
    return c // 4, r, 7 - r


ENGINES = ("sy", "g", "t", "v", "s")


class Prog:
    def __init__(self, name="prog"):
        self.nc = bass.Bass("TRN2", target_bir_lowering=False, debug=False)
        self.ctx = ExitStack()
        self.cmds = {e: [] for e in ENGINES}
        self.sems = {}
        self.cnt = {}

    def sem(self, name):
        if name not in self.sems:
            self.sems[name] = self.ctx.enter_context(self.nc.semaphore(name))
            self.cnt[name] = 0
        return self.sems[name]

    def sb(self, name, shape, dt):
        return self.ctx.enter_context(self.nc.sbuf_tensor(name, shape, dt))

    def psum(self, name, shape, dt):
        return self.ctx.enter_context(self.nc.psum_tensor(name, shape, dt))

    def emit(self, eng, fn):
        self.cmds[eng].append(fn)

    def wait(self, eng, sname, val=None):
        if val is None:
            val = self.cnt.get(sname, 0)
        if val <= 0:
            return
        h = self.sem(sname)
        self.emit(eng, lambda e: e.wait_ge(h, val))

    def op(self, eng, fn, sname=None, k=1):
        if sname is None:
            self.emit(eng, lambda e: fn(e))
            return None
        h = self.sem(sname)
        self.cnt[sname] = self.cnt.get(sname, 0) + k
        self.emit(eng, lambda e: fn(e).then_inc(h, k))
        return self.cnt[sname]

    def dma(self, eng, out, in_, sname, **kw):
        return self.op(eng, lambda e: e.dma_start(out=out, in_=in_, **kw),
                       sname, 16)

    def sop(self, eng, fn):
        ordn = eng + "_ord"
        self.wait(eng, ordn)
        return self.op(eng, fn, ordn)

    def begin_switch(self, sel_ap, n=4):
        if not hasattr(self, "segments"):
            self.segments = []
        self.segments.append(("linear", self.cmds))
        self._switch_bodies = []
        self._switch_sel = sel_ap
        self._switch_n = n

    def begin_body(self, idx=None, prefix=None):
        self.cmds = {e: [] for e in ENGINES}
        self._switch_bodies.append(self.cmds)
        self._body_cnt0 = dict(self.cnt)
        self._body_idx, self._body_prefix = idx, prefix
        if idx is not None and prefix is not None:
            saved = getattr(self, "_body_saved", {}).get(idx, {})
            self.cnt.update({k: v for k, v in saved.items()
                             if k.startswith(prefix)})

    def end_body(self):
        if self._body_idx is not None and self._body_prefix is not None:
            if not hasattr(self, "_body_saved"):
                self._body_saved = {}
            self._body_saved[self._body_idx] = {
                k: v for k, v in self.cnt.items()
                if k.startswith(self._body_prefix)}
        self.cnt = dict(self._body_cnt0)

    def end_switch(self, post_cnt=None):
        self.segments.append(("switch", self._switch_bodies,
                              self._switch_sel, self._switch_n))
        self.cmds = {e: [] for e in ENGINES}
        if post_cnt:
            for k, v in post_cnt.items():
                self.cnt[k] = self.cnt.get(k, 0) + v

    def build(self):
        segs = list(getattr(self, "segments", []))
        segs.append(("linear", self.cmds))
        nc = self.nc
        block = self.ctx.enter_context(nc.Block())
        handles = {"sy": nc.sync, "g": nc.gpsimd, "t": nc.tensor,
                   "v": nc.vector, "s": nc.scalar}

        @block.sync
        def _(_e):
            for seg in segs:
                if seg[0] == "linear":
                    for e in ENGINES:
                        for f in seg[1][e]:
                            f(handles[e])
                else:
                    _, bodies, sel_ap, n = seg
                    idx = [handles[e].value_load(sel_ap) for e in ENGINES]
                    engs = [handles[e] for e in ENGINES]
                    for i in nc.Switch(engines=engs, index=idx, n=n):
                        for e in ENGINES:
                            for f in bodies[i][e]:
                                f(handles[e])

        self.ctx.close()
        return nc


def make_consts():
    c = {}
    c["ident_bf"] = np.eye(128, dtype=BF16NP)
    c["ident_f32"] = np.eye(128, dtype=np.float32)
    c["ones_row_bf"] = np.ones((1, 512), dtype=BF16NP)
    c["ones_col_bf"] = np.ones((128, 1), dtype=BF16NP)
    c["ones_col_f32"] = np.ones((128, 1), dtype=np.float32)
    c["ones1_f32"] = np.ones((1, 128), dtype=np.float32)
    c["tri128"] = np.triu(np.ones((128, 128), np.float32), 1)
    c["tri32"] = np.triu(np.ones((32, 32), np.float32), 1)
    c["iota32_f"] = np.broadcast_to(np.arange(32, dtype=np.float32), (128, 32)).copy()
    c["iota128_f"] = np.arange(128, dtype=np.float32).reshape(128, 1)
    c["tbases"] = np.broadcast_to((np.arange(47, dtype=np.float32) * 128)[None, :],
                                  (32, 47)).copy()
    c["tbrow"] = (np.arange(47, dtype=np.float32) * 128)[None, :].copy()
    return c


CONST_SPECS = [
    ("ident_bf", [128, 128], BF16), ("ident_f32", [128, 128], F32), ("ones_row_bf", [1, 512], BF16),
    ("ones_col_bf", [128, 1], BF16), ("ones_col_f32", [128, 1], F32),
    ("ones1_f32", [1, 128], F32), ("tri128", [128, 128], F32),
    ("tri32", [32, 32], F32), ("iota32_f", [128, 32], F32),
    ("iota128_f", [128, 1], F32), ("tbases", [32, 47], F32),
    ("tbrow", [1, 47], F32),
]


class Consts:
    def __init__(self, p):
        nc = p.nc
        for name, shape, dt in CONST_SPECS:
            d = nc.dram_tensor(name, shape, dt, kind="ExternalInput").ap()
            t = p.sb("c_" + name, shape, dt)
            setattr(self, name, t)
            p.dma("sy", t[...], d[...], "c_ld")


def ln_scratch(p, tag):
    return {
        "xsq0": p.sb(f"{tag}_xsq0", [128, 512], BF16),
        "xsq1": p.sb(f"{tag}_xsq1", [128, 512], BF16),
        "xsq2": p.sb(f"{tag}_xsq2", [128, 512], BF16),
        "xsq3": p.sb(f"{tag}_xsq3", [128, 512], BF16),
        "row": p.sb(f"{tag}_row", [1, 1024], F32),
        "rtmp": p.sb(f"{tag}_rtmp", [1, 512], F32),
        "nt": p.sb(f"{tag}_nt", [128, 512], F32),
        "nt2": p.sb(f"{tag}_nt2", [128, 512], F32),
    }


def emit_ln(p, C, tag, x_sb, xln_sb, t0, TT, pb, sc, out_t0=None):
    sq, st, rw, bc, nm = (f"{tag}_sq", f"{tag}_st", f"{tag}_rw",
                          f"{tag}_bc", f"{tag}_nm")
    ts = slice(t0, t0 + TT)
    ots = slice(out_t0, out_t0 + TT) if out_t0 is not None else ts
    ps_sx = pb[0][0:1, 0:TT]
    ps_sx2 = pb[1][0:1, 0:TT]
    ps_mb = pb[2][:, 0:TT]
    ps_rb = pb[3][:, 0:TT]
    row = sc["row"]

    p.wait("t", rw)
    p.wait("t", nm)
    for c in range(NCH):
        bs = sc["xsq0"] if c % 2 == 0 else sc["xsq1"]
        bq = sc["xsq2"] if c % 2 == 0 else sc["xsq3"]
        if p.cnt.get(st, 0) >= 4:
            p.wait("s", st, p.cnt[st] - 2)
        p.op("s", lambda e, b=bs, c_=c: e.copy(b[:, 0:TT], x_sb[:, c_, ts]), sq)
        p.op("s", lambda e, b=bq, c_=c: e.activation(
            b[:, 0:TT], x_sb[:, c_, ts], AF.Square), sq)
        p.wait("t", sq)
        p.op("t", lambda e, b=bs, c_=c: e.matmul(
            ps_sx, C.ones_col_bf[:, :], b[:, 0:TT],
            start=(c_ == 0), stop=(c_ == 7)), st)
        p.op("t", lambda e, b=bq, c_=c: e.matmul(
            ps_sx2, C.ones_col_bf[:, :], b[:, 0:TT],
            start=(c_ == 0), stop=(c_ == 7)), st)
    p.wait("v", st)
    p.op("v", lambda e: e.tensor_scalar(row[0:1, 0:TT], ps_sx, 1.0 / D, None, op0=ALU.mult), rw)
    p.op("v", lambda e: e.tensor_scalar(row[0:1, 512:512 + TT], ps_sx2, 1.0 / D, None, op0=ALU.mult), rw)
    p.wait("v", rw)
    p.op("v", lambda e: e.tensor_tensor(sc["rtmp"][0:1, 0:TT], row[0:1, 0:TT], row[0:1, 0:TT], op=ALU.mult), rw)
    p.wait("v", rw)
    p.op("v", lambda e: e.tensor_tensor(row[0:1, 512:512 + TT], row[0:1, 512:512 + TT], sc["rtmp"][0:1, 0:TT], op=ALU.subtract), rw)
    p.wait("v", rw)
    p.op("v", lambda e: e.tensor_scalar(row[0:1, 512:512 + TT], row[0:1, 512:512 + TT], 1e-5, None, op0=ALU.add), rw)
    p.wait("s", rw)
    p.op("s", lambda e: e.activation(row[0:1, 512:512 + TT], row[0:1, 512:512 + TT], AF.Sqrt), sq)
    p.wait("v", sq)
    p.op("v", lambda e: e.reciprocal(row[0:1, 512:512 + TT], row[0:1, 512:512 + TT]), rw)
    p.wait("t", rw)
    p.op("t", lambda e: e.matmul(ps_mb, C.ones1_f32[:, :], row[0:1, 0:TT], start=True, stop=True), bc)
    p.op("t", lambda e: e.matmul(ps_rb, C.ones1_f32[:, :], row[0:1, 512:512 + TT], start=True, stop=True), bc)
    p.wait("v", bc)
    for c in range(NCH):
        ntb = sc["nt"] if c % 2 == 0 else sc["nt2"]
        if c >= 2:
            p.wait("v", nm, p.cnt[nm] - 1)
        p.op("v", lambda e, c_=c, b=ntb: e.tensor_tensor(b[:, 0:TT], x_sb[:, c_, ts], ps_mb, op=ALU.subtract), nm)
        p.wait("v", nm)
        p.op("v", lambda e, c_=c, b=ntb: e.tensor_tensor(xln_sb[:, c_, ots], b[:, 0:TT], ps_rb, op=ALU.mult), nm)
    return nm


def shard_x(x):
    out = []
    for c in range(NCORE):
        b, b0, b1 = core_blocks(c)
        xa = x[b, b0 * 1024:(b0 + 1) * 1024]
        xb = x[b, b1 * 1024:(b1 + 1) * 1024]
        out.append(np.ascontiguousarray(np.concatenate([xa, xb], 0).T))
    return out


def unshard_x(shards):
    x = np.zeros((B, S, D), dtype=np.asarray(shards[0]).dtype)
    for c in range(NCORE):
        b, b0, b1 = core_blocks(c)
        sh = np.asarray(shards[c]).T
        x[b, b0 * 1024:(b0 + 1) * 1024] = sh[:1024]
        x[b, b1 * 1024:(b1 + 1) * 1024] = sh[1024:]
    return x


def emit_ffn1(p, C, tag, x_sb, xln, pb, w1t_d, w2t_d, bufs):
    h_sb = bufs["h_sb"]
    lt_sb, ltln = bufs["lt_sb"], bufs["ltln"]
    lora, lraw = f"{tag}_lo", f"{tag}_lodma"

    p.op("v", lambda e: e.bn_stats(bufs["ltst"][:, 0:6], lt_sb[:, 0:512]), lora)
    p.op("v", lambda e: e.bn_stats(bufs["ltst"][:, 6:12], lt_sb[:, 512:1024]), lora)
    p.wait("v", lora)
    p.op("v", lambda e: e.bn_aggr(bufs["ltmv"][:, 0:2], bufs["ltst"][:, 0:12]), lora)
    p.wait("v", lora)
    p.op("v", lambda e: e.tensor_scalar(bufs["ltrs"][:, 0:1], bufs["ltmv"][:, 1:2], 1e-5, None, op0=ALU.add), lora)
    p.wait("s", lora)
    p.op("s", lambda e: e.activation(bufs["ltrs"][:, 0:1], bufs["ltrs"][:, 0:1], AF.Sqrt), lora)
    p.wait("v", lora)
    p.op("v", lambda e: e.reciprocal(bufs["ltrs"][:, 0:1], bufs["ltrs"][:, 0:1]), lora)
    p.wait("v", lora)
    p.op("v", lambda e: e.tensor_tensor(bufs["ltnm"][:, 0:1], bufs["ltmv"][:, 0:1], bufs["ltrs"][:, 0:1], op=ALU.mult), lora)
    p.wait("v", lora)
    p.op("v", lambda e: e.tensor_scalar(bufs["ltnm"][:, 0:1], bufs["ltnm"][:, 0:1], -1.0, None, op0=ALU.mult), lora)
    p.wait("v", lora)
    p.op("v", lambda e: e.tensor_scalar(ltln[:, :], lt_sb[:, :], bufs["ltrs"][:, 0:1], bufs["ltnm"][:, 0:1], op0=ALU.mult, op1=ALU.add), lora)
    pT = pb[4].bitcast(BF16)
    p.wait("t", lora)
    for c in range(NCH):
        p.op("t", lambda e, c_=c: e.transpose(
            pT[:, c_ * 16:(c_ + 1) * 16], ltln[0:16, c_ * 128:(c_ + 1) * 128],
            C.ident_bf[0:16, 0:16]), lora)
    p.wait("v", lora)
    p.op("v", lambda e: e.tensor_copy(bufs["loraT"][:, :, :], pT[:, 0:128].rearrange("p (a b) -> p a b", a=8)), lora)
    lora_done = p.cnt[lora]
    p.wait("g", lora, lora_done)
    for i in range(4):
        p.dma("g", bufs["e1s"][0:1, i * 1024:(i + 1) * 1024],
              ltln[2 * i:2 * i + 1, :], lraw)
    p.wait("v", lraw)
    p.op("v", lambda e: e.tensor_scalar(bufs["e1s"][0:1, :], bufs["e1s"][0:1, :], bufs["sc_sb"][0:1, 0:1], None, op0=ALU.mult), lora)
    for ch in range(NCH):
        p.op("g", lambda e, ch_=ch: e.memset(bufs["e2b"][:, ch_, :], 0.0), lora)
    p.wait("g", lora)
    for ch in range(NCH):
        istar, half = ch // 2, ch % 2
        for jj in range(4):
            src = ltln[8 + 2 * istar:9 + 2 * istar,
                       512 * half + jj:512 * half + jj + 509:4]
            p.dma("g", bufs["e2b"][4 * jj + istar:4 * jj + istar + 1, ch, 0:128],
                  src, lraw, allow_slow_non_contiguous=True)
    lraw_done = p.cnt[lraw]

    w1l, w1m, hgel = f"{tag}_w1l", f"{tag}_w1m", f"{tag}_hg"
    td, tdr = f"{tag}_td", f"{tag}_tdr"
    uacc, uev, udr = f"{tag}_ua", f"{tag}_ue", f"{tag}_ud"
    w2l, w2m, res = f"{tag}_w2l", f"{tag}_w2m", f"{tag}_res"
    wbufs1 = [bufs["w1a"], bufs["w1b"]]
    wbufs2 = [bufs["w2a"], bufs["w2b"]]
    w1use = [0, 0]
    w2use = [0, 0]
    for ti in range(4):
        ts = slice(ti * 512, (ti + 1) * 512)
        nm = emit_ln(p, C, "Gln", x_sb, xln, ti * 512, 512, pb[0:4],
                     bufs["lns"])
        p.wait("t", lora, lora_done)
        p.wait("t", nm)
        p.wait("t", uev)
        for c in range(NCH):
            p.op("t", lambda e, c_=c, ts_=ts: e.matmul(
                pb[5][0:4, 0:512], bufs["loraT"][:, c_, 1:9:2],
                xln[:, c_, ts_], start=(c_ == 0), stop=(c_ == 7)), td)
        p.wait("v", td)
        p.op("v", lambda e: e.tensor_copy(bufs["t_f"][0:4, :], pb[5][0:4, 0:512]), td)
        p.wait("g", td)
        for i in range(4):
            p.dma("g", bufs["t_row"][0:1, i * 512:(i + 1) * 512],
                  bufs["t_f"][i:i + 1, 0:512], tdr)
        p.wait("g", lraw, lraw_done)
        for ch in range(32):
            wb = wbufs1[ch % 2]
            p.wait("g", w1m, w1use[ch % 2])
            w1l_c = p.dma("g", wb[...],
                          w1t_d[:, ch * 128:(ch + 1) * 128].rearrange(
                              "(c p) m -> p c m", p=128), w1l + str(ch % 2))
            psH = pb[6 if ch % 2 == 0 else 7][:, 0:512]
            p.wait("t", w1l + str(ch % 2), w1l_c)
            p.wait("t", tdr)
            if ch < 2:
                p.wait("t", res)
            else:
                p.wait("t", hgel, p.cnt.get(hgel, 0) - 1)
            for c in range(NCH):
                p.op("t", lambda e, c_=c, w=wb, ps=psH, ts_=ts: e.matmul(
                    ps, w[:, c_, :], xln[:, c_, ts_], start=(c_ == 0), stop=False))
            p.op("t", lambda e, ch_=ch, ps=psH: e.matmul(
                ps, bufs["b1row"][0:1, ch_ * 128:(ch_ + 1) * 128],
                C.ones_row_bf[:, :], start=False, stop=False))
            p.op("t", lambda e, ch_=ch, ps=psH: e.matmul(
                ps, bufs["e1s"][0:1, ch_ * 128:(ch_ + 1) * 128],
                bufs["t_row"][0:1, (ch_ // 8) * 512:(ch_ // 8 + 1) * 512],
                start=False, stop=True), w1m)
            w1use[ch % 2] = p.cnt[w1m]
            p.wait("s", w1m)
            p.op("s", lambda e, ch_=ch, ps=psH: e.activation(
                h_sb[:, ch_, :], ps, AF.Gelu), hgel)
        p.wait("t", hgel)
        p.wait("t", nm)
        for c in range(32):
            p.op("t", lambda e, c_=c: e.matmul(
                pb[4 + c_ // 8][0:4, 0:512], bufs["loraT"][:, c_ % 8, 9:16:2],
                h_sb[:, c_, :], start=(c_ % 8 == 0), stop=(c_ % 8 == 7)), uacc)
        p.wait("v", uacc)
        for jj in range(4):
            p.op("v", lambda e, j=jj: e.tensor_scalar(
                bufs["u_f"][0:4, j, :], pb[4 + j][0:4, 0:512],
                bufs["sc4_sb"][0:4, 0:1], None, op0=ALU.mult), uev)
        p.wait("g", uev)
        for jj in range(4):
            p.dma("g", bufs["u_all"][4 * jj:4 * jj + 4, 0:512],
                  bufs["u_f"][0:4, jj, :], udr)
        for ch in range(NCH):
            wb = wbufs2[ch % 2]
            p.wait("g", w2m, w2use[ch % 2])
            w2l_c = p.dma("g", wb[...],
                          w2t_d[:, ch * 128:(ch + 1) * 128].rearrange(
                              "(c p) m -> p c m", p=128), w2l + str(ch % 2))
            psO = pb[6 if ch % 2 == 0 else 7][:, 0:512]
            p.wait("t", w2l + str(ch % 2), w2l_c)
            p.wait("t", udr)
            if ch < 2:
                p.wait("t", hgel)
                p.wait("t", uev)
            else:
                p.wait("t", res, p.cnt.get(res, 0) - 1)
            for c in range(32):
                p.op("t", lambda e, c_=c, w=wb, ps=psO: e.matmul(
                    ps, w[:, c_, :], h_sb[:, c_, :], start=(c_ == 0), stop=False))
            p.op("t", lambda e, ch_=ch, ps=psO: e.matmul(
                ps, bufs["b2row"][0:1, ch_ * 128:(ch_ + 1) * 128],
                C.ones_row_bf[:, :], start=False, stop=False))
            p.op("t", lambda e, ch_=ch, ps=psO: e.matmul(
                ps, bufs["e2b"][0:16, ch_, :], bufs["u_all"][0:16, 0:512],
                start=False, stop=True), w2m)
            w2use[ch % 2] = p.cnt[w2m]
            p.wait("v", w2m)
            p.op("v", lambda e, ch_=ch, ps=psO, ts_=ts: e.scalar_tensor_tensor(
                x_sb[:, ch_, ts_], ps, 0.5, x_sb[:, ch_, ts_],
                op0=ALU.mult, op1=ALU.add), res)
    return res


def ffn1_bufs(p, tag):
    return {
        "h_sb": p.sb(f"{tag}_h", [128, 32, 512], BF16),
        "lt_sb": p.sb(f"{tag}_lt", [16, D], F32),
        "ltln": p.sb(f"{tag}_ltln", [16, D], BF16),
        "ltst": p.sb(f"{tag}_ltst", [16, 12], F32),
        "ltmv": p.sb(f"{tag}_ltmv", [16, 2], F32),
        "ltrs": p.sb(f"{tag}_ltrs", [16, 1], F32),
        "ltnm": p.sb(f"{tag}_ltnm", [16, 1], F32),
        "loraT": p.sb(f"{tag}_loraT", [128, NCH, 16], BF16),
        "e1s": p.sb(f"{tag}_e1s", [1, 4096], BF16),
        "e2b": p.sb(f"{tag}_e2b", [16, NCH, 128], BF16),
        "t_row": p.sb(f"{tag}_trow", [1, 4 * 512], BF16),
        "t_f": p.sb(f"{tag}_tf", [4, 512], BF16),
        "u_all": p.sb(f"{tag}_uall", [16, 512], BF16),
        "u_f": p.sb(f"{tag}_uf", [4, 4, 512], BF16),
        "b1row": p.sb(f"{tag}_b1r", [1, 4096], BF16),
        "b2row": p.sb(f"{tag}_b2r", [1, D], BF16),
        "sc_sb": p.sb(f"{tag}_sc", [1, 1], F32),
        "sc4_sb": p.sb(f"{tag}_sc4", [4, 1], F32),
        "w1a": p.sb(f"{tag}_w1a", [128, NCH, 128], BF16),
        "w1b": p.sb(f"{tag}_w1b", [128, NCH, 128], BF16),
        "w2a": p.sb(f"{tag}_w2a", [128, 32, 128], BF16),
        "w2b": p.sb(f"{tag}_w2b", [128, 32, 128], BF16),
        "lns": ln_scratch(p, f"{tag}_lns"),
    }


def build_ffn1():
    p = Prog("ffn1")
    nc = p.nc
    x_d = nc.dram_tensor("x_fm", [D, TPC], F32, kind="ExternalInput").ap()
    w1t_d = nc.dram_tensor("w1t", [D, 4096], F32, kind="ExternalInput").ap()
    w2t_d = nc.dram_tensor("w2t", [4096, D], F32, kind="ExternalInput").ap()
    b1r_d = nc.dram_tensor("b1row", [1, 4096], F32, kind="ExternalInput").ap()
    b2r_d = nc.dram_tensor("b2row", [1, D], F32, kind="ExternalInput").ap()
    sc_d = nc.dram_tensor("scal", [1, 1], F32, kind="ExternalInput").ap()
    sc4_d = nc.dram_tensor("scal4", [4, 1], F32, kind="ExternalInput").ap()
    lt_d = nc.dram_tensor("ltoks", [16, D], F32, kind="ExternalInput").ap()
    y_d = nc.dram_tensor("y_fm", [D, TPC], F32, kind="ExternalOutput").ap()

    C = Consts(p)
    x_sb = p.sb("x_sb", [128, NCH, TPC], F32)
    xln = p.sb("xln", [128, NCH, 512], BF16)
    pb = [p.psum(f"pb{i}", [128, 512], F32) for i in range(8)]
    bufs = ffn1_bufs(p, "f1")

    for c in range(NCH):
        p.dma("sy", x_sb[:, c, :], x_d[c * 128:(c + 1) * 128, :], "xld")
    p.dma("sy", bufs["lt_sb"][:, :], lt_d[:, :], "xld")
    p.dma("g", bufs["b1row"][:, :], b1r_d[:, :], "wld")
    p.dma("g", bufs["b2row"][:, :], b2r_d[:, :], "wld")
    p.dma("sy", bufs["sc_sb"][:, :], sc_d[:, :], "xld")
    p.dma("sy", bufs["sc4_sb"][:, :], sc4_d[:, :], "xld")
    for e in ("v", "s", "t", "g"):
        p.wait(e, "xld")
        p.wait(e, "wld")
        p.wait(e, "c_ld")

    class XlnView:
        def __getitem__(self, idx):
            _, c, ts = idx
            off = ts.start % 512
            return xln[:, c, off:off + (ts.stop - ts.start)]
    res = emit_ffn1(p, C, "f1", x_sb, XlnView(), pb, w1t_d, w2t_d, bufs)

    p.wait("sy", res)
    for c in range(NCH):
        p.dma("sy", y_d[c * 128:(c + 1) * 128, :], x_sb[:, c, :], "yst")
    p.wait("sy", "yst")
    return p


def ffn1_inputs(core, x_fm, W1, b1, W2, b2, scale, x_full):
    b = core // 4
    cd = make_consts()
    return dict(
        x_fm=np.ascontiguousarray(x_fm),
        w1t=np.ascontiguousarray(W1.T),
        w2t=np.ascontiguousarray(W2.T),
        b1row=np.ascontiguousarray(b1.reshape(1, 4096)),
        b2row=np.ascontiguousarray(b2.reshape(1, D)),
        scal=np.full((1, 1), scale, np.float32),
        scal4=np.full((4, 1), scale, np.float32),
        ltoks=np.ascontiguousarray(x_full[b, 1:17]),
        **cd,
    )



def attn_params(layer):
    if layer == 0:
        return dict(nheads=8, dh=128, inv_scale=1.0 / 128.0, rope=True)
    return dict(nheads=16, dh=64, inv_scale=1.0 / 8.0, rope=False)


def own_tiles(rr):
    return [2 * rr, 2 * rr + 1, 2 * (7 - rr), 2 * (7 - rr) + 1]


def group_bases(rr):
    ba, bb = 1024 * rr, 1024 * (7 - rr)
    return [ba, ba + 512, bb, bb + 512]


def r3(ap2, a):
    return ap2.rearrange("p (a b) -> p a b", a=a)


def attn_bufs(p, tag):
    return {
        "arena_a": p.sb(f"{tag}_aa", [128, 24576], BF16),
        "arena_b": p.sb(f"{tag}_ab", [128, 16384], BF16),
        "woT": p.sb(f"{tag}_woT", [128, NCH, 1024], BF16),
        "o_all": p.sb(f"{tag}_oall", [128, NCH, 2048], BF16),
        "lns": ln_scratch(p, f"{tag}_lns"),
        "xfl0": p.sb(f"{tag}_xfl0", [128, NCH, 512], BF16),
        "xfl1": p.sb(f"{tag}_xfl1", [128, NCH, 512], BF16),
        "csb": p.sb(f"{tag}_csb", [128, 1024], BF16),
        "csb2": p.sb(f"{tag}_csb2", [128, 1024], BF16),
        "kout": p.sb(f"{tag}_kout", [128, NCH, 128], BF16),
        "vout": p.sb(f"{tag}_vout", [128, NCH, 128], BF16),
        "qout": p.sb(f"{tag}_qout", [128, NCH, 128], BF16),
        "rt1": p.sb(f"{tag}_rt1", [128, 4, 64], F32),
        "rt2": p.sb(f"{tag}_rt2", [128, 4, 64], F32),
        "rt3": p.sb(f"{tag}_rt3", [128, 4, 64], F32),
        "rt4": p.sb(f"{tag}_rt4", [128, 4, 64], F32),
        "ksq": p.sb(f"{tag}_ksq", [128, NCH, 128], BF16),
        "kn": p.sb(f"{tag}_kn", [128, 8], F32),
        "knacc": p.sb(f"{tag}_knacc", [128, 8], F32),
        "qnacc": p.sb(f"{tag}_qnacc", [128, 8], F32),
        "brow": p.sb(f"{tag}_brow", [1, 16], F32),
        "negb": p.sb(f"{tag}_negb", [128, 1], F32),
        "masks": p.sb(f"{tag}_masks", [128, 4, 512], BF16),
        "sccol": p.sb(f"{tag}_sccol", [128, 1], F32),
        "p_sb": p.sb(f"{tag}_psb", [128, 4, 512], BF16),
        "rrow": p.sb(f"{tag}_rrow", [1, 512], F32),
        "rb_sb": p.sb(f"{tag}_rb", [128, 512], F32),
        "xc0": p.sb(f"{tag}_xc0", [128, 512], F32),
        "xc1": p.sb(f"{tag}_xc1", [128, 512], F32),
        "yc0": p.sb(f"{tag}_yc0", [128, 512], F32),
        "yc1": p.sb(f"{tag}_yc1", [128, 512], F32),
    }


def _rope6(p, bufs, cs3, pvsrc, dst, sem):
    p.wait("v", sem)
    p.op("v", lambda e: e.tensor_tensor(
        bufs["rt1"][:, :, :], pvsrc[:, :, 0:64], cs3[:, :, 0:64], op=ALU.mult), sem)
    p.op("v", lambda e: e.tensor_tensor(
        bufs["rt2"][:, :, :], pvsrc[:, :, 64:128], cs3[:, :, 64:128], op=ALU.mult), sem)
    p.op("v", lambda e: e.tensor_tensor(
        bufs["rt3"][:, :, :], pvsrc[:, :, 0:64], cs3[:, :, 64:128], op=ALU.mult), sem)
    p.op("v", lambda e: e.tensor_tensor(
        bufs["rt4"][:, :, :], pvsrc[:, :, 64:128], cs3[:, :, 0:64], op=ALU.mult), sem)
    p.wait("v", sem)
    p.op("v", lambda e: e.tensor_tensor(
        dst[:, :, 0:64], bufs["rt1"][:, :, :], bufs["rt2"][:, :, :], op=ALU.subtract), sem)
    p.op("v", lambda e: e.tensor_tensor(
        dst[:, :, 64:128], bufs["rt3"][:, :, :], bufs["rt4"][:, :, :], op=ALU.add), sem)


def emit_attn(p, C, tag, layer, rr, xfull_d, x_d, y_d, wq_d, wk_d, wv_d,
              woT_d, cs_d, masks_d, sccol_d, kbuf, vbuf, qbuf, pb, bufs,
              a_tiles=16):
    ap = attn_params(layer)
    aA, aB = bufs["arena_a"], bufs["arena_b"]
    wq = r3(aA[:, 0:8192], NCH)
    wk = r3(aA[:, 8192:16384], NCH)
    wv = r3(aA[:, 16384:24576], NCH)
    aBf = aB.bitcast(F32)
    xf0 = r3(aBf[:, 0:4096], NCH)
    xf1 = r3(aBf[:, 4096:8192], NCH)
    khb = aA[:, 0:8192]
    vhb = r3(aA[:, 8192:16384], 64)
    q_sb = r3(aB[:, 0:16384], NCH)
    woT, o_all = bufs["woT"], bufs["o_all"]
    lns = bufs["lns"]
    kout, vout, qout = bufs["kout"], bufs["vout"], bufs["qout"]
    ot, gbs = own_tiles(rr), group_bases(rr)

    xld, wld, csld = f"{tag}_xld", f"{tag}_wld", f"{tag}_cs"
    kvm, rpd, kvev = f"{tag}_kvm", f"{tag}_rpd", f"{tag}_kvev"
    knsm, kst = f"{tag}_knsm", f"{tag}_kst"
    for c in range(NCH):
        p.dma("g", wq[:, c, :], wq_d[c * 128:(c + 1) * 128, :], wld)
        p.dma("g", wk[:, c, :], wk_d[c * 128:(c + 1) * 128, :], wld)
        p.dma("g", wv[:, c, :], wv_d[c * 128:(c + 1) * 128, :], wld)
        p.dma("g", woT[:, c, :], woT_d[c * 128:(c + 1) * 128, :], wld)
    p.dma("sy", bufs["sccol"][:, :], sccol_d[:, :], f"{tag}_scl")
    wld_done = p.cnt[wld]
    for e in ("t", "v", "s", "g"):
        p.wait(e, wld, wld_done)
        p.wait(e, f"{tag}_scl")
        p.wait(e, "c_ld")
    p.op("v", lambda e: e.memset(bufs["knacc"][:, :], 0.0), knsm)
    p.op("v", lambda e: e.memset(bufs["qnacc"][:, :], 0.0), knsm)

    kvm_hist, sq_hist = {}, {}
    for tt in range(a_tiles):
        xf = xf0 if tt % 2 == 0 else xf1
        xfl = bufs["xfl0"] if tt % 2 == 0 else bufs["xfl1"]
        if tt >= 2:
            p.wait("sy", kvm, kvm_hist[tt - 2])
            p.wait("sy", "Gln_sq", sq_hist[tt - 2])
        xldp = xld + str(tt % 2)
        for c in range(NCH):
            p.dma("sy", xf[:, c, :], xfull_d[c * 128:(c + 1) * 128,
                                             tt * 512:(tt + 1) * 512], xldp)
        xld_c = p.cnt[xldp]
        p.wait("s", xldp, xld_c)
        p.wait("v", xldp, xld_c)
        nm = emit_ln(p, C, "Gln", xf, xfl, 0, 512, pb[0:4], lns)
        own = tt in ot
        for sub in range(4):
            cst = bufs["csb"] if (tt * 4 + sub) % 2 == 0 else bufs["csb2"]
            cs3 = r3(cst, NCH)
            csp = csld + str((tt * 4 + sub) % 2)
            if ap["rope"]:
                if p.cnt.get(rpd, 0) > 0 and (tt * 4 + sub) >= 2:
                    p.wait("sy", rpd)
                p.dma("sy", cst[:, :], cs_d[tt * 512 + sub * 128:
                                            tt * 512 + sub * 128 + 128, :], csp)
            cs_c = p.cnt.get(csp, 0)
            p.wait("t", nm)
            p.wait("t", rpd)
            for g2 in range(2):
                for c in range(NCH):
                    p.op("t", lambda e, c_=c, g_=g2, s_=sub, xl=xfl: e.matmul(
                        pb[4 + g_][:, :], xl[:, c_, s_ * 128:(s_ + 1) * 128],
                        wk[:, c_, g_ * 512:(g_ + 1) * 512],
                        start=(c_ == 0), stop=(c_ == 7)), kvm)
            kvm_k = p.cnt[kvm]
            p.wait("t", kvev)
            for g2 in range(2):
                for c in range(NCH):
                    p.op("t", lambda e, c_=c, g_=g2, s_=sub, xl=xfl: e.matmul(
                        pb[6 + g_][:, :], xl[:, c_, s_ * 128:(s_ + 1) * 128],
                        wv[:, c_, g_ * 512:(g_ + 1) * 512],
                        start=(c_ == 0), stop=(c_ == 7)), kvm)
            kvm_v = p.cnt[kvm]
            p.wait("v", kvm, kvm_k)
            p.wait("v", kst)
            if ap["rope"]:
                p.wait("v", csp, cs_c)
                _rope6(p, bufs, cs3[:, 0:4, :], r3(pb[4][:, :], 4), kout[:, 0:4, :], rpd)
                _rope6(p, bufs, cs3[:, 4:8, :], r3(pb[5][:, :], 4), kout[:, 4:8, :], rpd)
            else:
                p.op("v", lambda e: e.tensor_copy(kout[:, 0:4, :], r3(pb[4][:, :], 4)), rpd)
                p.op("v", lambda e: e.tensor_copy(kout[:, 4:8, :], r3(pb[5][:, :], 4)), rpd)
            rpd_k = p.cnt[rpd]
            p.wait("s", kvm, kvm_v)
            p.wait("s", kst)
            p.op("s", lambda e: e.copy(vout[:, 0:4, :], r3(pb[6][:, :], 4)), kvev)
            p.op("s", lambda e: e.copy(vout[:, 4:8, :], r3(pb[7][:, :], 4)), kvev)
            kvev_c = p.cnt[kvev]
            p.wait("s", rpd, rpd_k)
            p.op("s", lambda e: e.activation(
                bufs["ksq"].rearrange("p a b -> p (a b)"),
                kout.rearrange("p a b -> p (a b)"), AF.Square), knsm)
            knsq_c = p.cnt[knsm]
            p.wait("v", knsm, knsq_c)
            for h in range(NCH):
                p.op("v", lambda e, h_=h: e.reduce_sum(
                    bufs["kn"][:, h_:h_ + 1], bufs["ksq"][:, h_, :], axis=AX), knsm)
            p.wait("v", knsm)
            p.op("v", lambda e: e.tensor_tensor(
                bufs["knacc"][:, :], bufs["knacc"][:, :], bufs["kn"][:, :], op=ALU.max), knsm)
            trow = tt * 512 + sub * 128
            p.wait("g", rpd, rpd_k)
            p.dma("g", kbuf[trow:trow + 128, :], kout.rearrange("p a b -> p (a b)"), kst)
            p.wait("g", kvev, kvev_c)
            p.dma("g", vbuf[trow:trow + 128, :], vout.rearrange("p a b -> p (a b)"), kst)
            if own:
                qsub = ot.index(tt) * 4 + sub
                p.wait("t", rpd, p.cnt[rpd])
                for g2 in range(2):
                    for c in range(NCH):
                        p.op("t", lambda e, c_=c, g_=g2, s_=sub, xl=xfl: e.matmul(
                            pb[4 + g_][:, :], xl[:, c_, s_ * 128:(s_ + 1) * 128],
                            wq[:, c_, g_ * 512:(g_ + 1) * 512],
                            start=(c_ == 0), stop=(c_ == 7)), kvm)
                qm_c = p.cnt[kvm]
                p.wait("v", kvm, qm_c)
                p.wait("v", kst)
                if ap["rope"]:
                    _rope6(p, bufs, cs3[:, 0:4, :], r3(pb[4][:, :], 4), qout[:, 0:4, :], rpd)
                    _rope6(p, bufs, cs3[:, 4:8, :], r3(pb[5][:, :], 4), qout[:, 4:8, :], rpd)
                else:
                    p.op("v", lambda e: e.tensor_copy(qout[:, 0:4, :], r3(pb[4][:, :], 4)), rpd)
                    p.op("v", lambda e: e.tensor_copy(qout[:, 4:8, :], r3(pb[5][:, :], 4)), rpd)
                rpd_q = p.cnt[rpd]
                p.wait("s", rpd, rpd_q)
                p.op("s", lambda e: e.activation(
                    bufs["ksq"].rearrange("p a b -> p (a b)"),
                    qout.rearrange("p a b -> p (a b)"), AF.Square), knsm)
                qsq_c = p.cnt[knsm]
                p.wait("v", knsm, qsq_c)
                for h in range(NCH):
                    p.op("v", lambda e, h_=h: e.reduce_sum(
                        bufs["kn"][:, h_:h_ + 1], bufs["ksq"][:, h_, :], axis=AX), knsm)
                p.wait("v", knsm)
                p.op("v", lambda e: e.tensor_tensor(
                    bufs["qnacc"][:, :], bufs["qnacc"][:, :], bufs["kn"][:, :], op=ALU.max), knsm)
                p.wait("g", rpd, rpd_q)
                p.dma("g", qbuf[qsub * 128:qsub * 128 + 128, :],
                      qout.rearrange("p a b -> p (a b)"), kst)
        kvm_hist[tt] = p.cnt[kvm]
        sq_hist[tt] = p.cnt["Gln_sq"]
    phaseA_kst = p.cnt[kst]

    p.wait("t", knsm)
    p.wait("t", kvev)
    p.op("t", lambda e: e.transpose(pb[4][0:8, 0:128], bufs["knacc"][:, 0:8], C.ident_f32[:, :]), knsm)
    p.op("t", lambda e: e.transpose(pb[5][0:8, 0:128], bufs["qnacc"][:, 0:8], C.ident_f32[:, :]), knsm)
    p.wait("v", knsm)
    p.op("v", lambda e: e.reduce_max(bufs["kn"][0:8, 0:1], pb[4][0:8, 0:128], axis=AX), knsm)
    p.op("v", lambda e: e.reduce_max(bufs["kn"][0:8, 1:2], pb[5][0:8, 0:128], axis=AX), knsm)
    p.wait("t", knsm)
    p.op("t", lambda e: e.transpose(pb[6][0:1, 0:8], bufs["kn"][0:8, 0:1], C.ident_f32[0:8, 0:8]), knsm)
    p.op("t", lambda e: e.transpose(pb[6][0:1, 8:16], bufs["kn"][0:8, 1:2], C.ident_f32[0:8, 0:8]), knsm)
    p.wait("v", knsm)
    p.op("v", lambda e: e.tensor_copy(bufs["brow"][0:1, 0:16], pb[6][0:1, 0:16]), knsm)
    p.wait("v", knsm)
    p.op("v", lambda e: e.tensor_tensor(bufs["brow"][0:1, 0:8], bufs["brow"][0:1, 0:8], bufs["brow"][0:1, 8:16], op=ALU.mult), knsm)
    p.wait("v", knsm)
    p.op("v", lambda e: e.tensor_scalar(bufs["brow"][0:1, 0:8], bufs["brow"][0:1, 0:8], ap["inv_scale"] ** 2, None, op0=ALU.mult), knsm)
    p.wait("s", knsm)
    p.op("s", lambda e: e.activation(bufs["brow"][0:1, 0:8], bufs["brow"][0:1, 0:8], AF.Sqrt), knsm)
    p.wait("v", knsm)
    p.op("v", lambda e: e.tensor_scalar(bufs["brow"][0:1, 0:8], bufs["brow"][0:1, 0:8], -1.0, None, op0=ALU.mult), knsm)
    brow_done = p.cnt[knsm]
    p.wait("sy", kst, phaseA_kst)
    qld = f"{tag}_qld"
    for c in range(NCH):
        p.dma("sy", q_sb[:, c, :], qbuf[:, c * 128:(c + 1) * 128], qld,
              transpose=True)
    p.wait("t", qld)

    NH = ap["nheads"]
    sS, sE, sMK, sAV, sOE = (f"{tag}_S", f"{tag}_E", f"{tag}_MK",
                             f"{tag}_AV", f"{tag}_OE")
    khl, vhl, nbs = f"{tag}_khl", f"{tag}_vhl", f"{tag}_nb"
    exp_hist = [0, 0]
    for h in range(NH):
        hc = h if layer == 0 else h // 2
        dsl = slice(0, 128) if layer == 0 else slice(64 * (h % 2), 64 * (h % 2) + 64)
        p.wait("t", knsm, brow_done)
        p.wait("t", nbs)
        p.wait("t", sE)
        p.wait("t", sOE)
        p.op("t", lambda e, hc_=hc: e.matmul(pb[7][:, 0:1], C.ones1_f32[:, :],
             bufs["brow"][0:1, hc_:hc_ + 1], start=True, stop=True), nbs)
        p.wait("v", nbs)
        p.wait("v", sE)
        p.op("v", lambda e: e.tensor_copy(bufs["negb"][:, 0:1], pb[7][:, 0:1]), nbs)
        negb_c = p.cnt[nbs]
        p.wait("s", nbs, negb_c)
        if layer == 0 or h % 2 == 0:
            p.wait("sy", sS)
            p.wait("sy", sAV)
            p.wait("sy", kvm)
            khl_c = p.dma("sy", khb[:, :], kbuf[:, hc * 128:(hc + 1) * 128],
                          khl, transpose=True)
            for part in range(8):
                p.dma("sy", vhb[:, part * 8:(part + 1) * 8, :],
                      vbuf[part * 1024:(part + 1) * 1024,
                           hc * 128:(hc + 1) * 128]
                      .rearrange("(kt p) d -> p kt d", p=128), vhl)
            vhl_c = p.cnt[vhl]
            p.wait("t", khl, khl_c)
            p.wait("t", vhl, vhl_c)
        for g in range(4):
            nreg = gbs[g] // 512 + 1
            for r in range(nreg):
                kg = 512 * r
                diag = (r == nreg - 1)
                p.wait("t", sE, exp_hist[-1])
                for j in range(4):
                    p.op("t", lambda e, j_=j, kg_=kg, d_=dsl, g_=g, hc_=hc: e.matmul(
                        pb[j_][:, :], khb[d_, kg_ + j_ * 128:kg_ + (j_ + 1) * 128],
                        q_sb[d_, hc_, g_ * 512:(g_ + 1) * 512],
                        start=True, stop=True), sS)
                sS_c = p.cnt[sS]
                p.wait("s", sS, sS_c)
                p.wait("s", sAV)
                p.wait("s", sMK)
                for j in range(4):
                    p.op("s", lambda e, j_=j: e.activation(
                        bufs["p_sb"][:, j_, :], pb[j_][:, :], AF.Exp,
                        bias=bufs["negb"][:, 0:1], scale=bufs["sccol"][:, 0:1]), sE)
                exp_hist.append(p.cnt[sE])
                if diag:
                    p.wait("v", sE)
                    for j in range(4):
                        p.op("v", lambda e, j_=j: e.tensor_tensor(
                            bufs["p_sb"][:, j_, :], bufs["p_sb"][:, j_, :],
                            bufs["masks"][:, j_, :], op=ALU.mult), sMK)
                p.wait("t", sE)
                if diag:
                    p.wait("t", sMK)
                if r == 0:
                    p.wait("t", sOE)
                for j in range(4):
                    p.op("t", lambda e, j_=j, r_=r, nr=nreg: e.matmul(
                        pb[5][0:1, :], C.ones_col_bf[:, :], bufs["p_sb"][:, j_, :],
                        start=(r_ == 0 and j_ == 0),
                        stop=(r_ == nr - 1 and j_ == 3)), sAV)
                vsl = slice(0, 128) if layer == 0 else slice(64 * (h % 2), 64 * (h % 2) + 64)
                osl = slice(0, 128) if layer == 0 else slice(0, 64)
                for j in range(4):
                    p.op("t", lambda e, j_=j, kg_=kg, r_=r, nr=nreg, v_=vsl, o_=osl: e.matmul(
                        pb[6][o_, :], vhb[:, kg_ // 128 + j_, v_],
                        bufs["p_sb"][:, j_, :],
                        start=(r_ == 0 and j_ == 0), stop=(r_ == nr - 1 and j_ == 3)), sAV)
            sAV_c = p.cnt[sAV]
            p.wait("v", sAV, sAV_c)
            p.op("v", lambda e: e.reciprocal(bufs["rrow"][0:1, :], pb[5][0:1, :]), sOE)
            rr_c = p.cnt[sOE]
            p.wait("t", sOE, rr_c)
            p.op("t", lambda e: e.matmul(pb[7][:, :], C.ones1_f32[:, :],
                 bufs["rrow"][0:1, :], start=True, stop=True), sOE)
            p.wait("v", sOE)
            p.op("v", lambda e: e.tensor_copy(bufs["rb_sb"][:, :], pb[7][:, :]), sOE)
            p.wait("v", sOE)
            osl = slice(0, 128) if layer == 0 else slice(0, 64)
            if layer == 0:
                p.op("v", lambda e, g_=g, hc_=hc: e.tensor_tensor(
                    o_all[:, hc_, g_ * 512:(g_ + 1) * 512],
                    pb[6][:, :], bufs["rb_sb"][:, :], op=ALU.mult), sOE)
            else:
                p.wait("v", f"{tag}_ost")
                p.op("v", lambda e, o_=osl: e.tensor_tensor(
                    bufs["yc0"].bitcast(BF16)[o_, 0:512],
                    pb[6][o_, :], bufs["rb_sb"][o_, :], op=ALU.mult), sOE)
                p.wait("g", sOE)
                p.dma("g", o_all[64 * (h % 2):64 * (h % 2) + 64, hc,
                                 g * 512:(g + 1) * 512],
                      bufs["yc0"].bitcast(BF16)[0:64, 0:512], f"{tag}_ost")
    oall_done = p.cnt[sOE]
    ost_done = p.cnt.get(f"{tag}_ost", 0)

    oproj, resid, xcl = f"{tag}_op", f"{tag}_res", f"{tag}_xc"
    yst_hist = []
    for g in range(4):
        for mc in range(NCH):
            idx = g * NCH + mc
            xc = bufs["xc0"] if idx % 2 == 0 else bufs["xc1"]
            yc = bufs["yc0"] if idx % 2 == 0 else bufs["yc1"]
            if idx >= 2:
                p.wait("sy", resid, p.cnt.get(resid, 0) - 1)
            p.dma("sy", xc[:, :], x_d[mc * 128:(mc + 1) * 128,
                                      g * 512:(g + 1) * 512], xcl)
            xcl_c = p.cnt[xcl]
            p.wait("t", sOE, oall_done)
            if layer == 1:
                p.wait("t", f"{tag}_ost", ost_done)
            if g * NCH + mc >= 2:
                p.wait("t", resid, p.cnt.get(resid, 0) - 1)
            bnk = pb[0] if (g * NCH + mc) % 2 == 0 else pb[1]
            for hd in range(NCH):
                p.op("t", lambda e, hd_=hd, mc_=mc, g_=g, bk=bnk: e.matmul(
                    bk[:, :], woT[:, hd_, mc_ * 128:(mc_ + 1) * 128],
                    o_all[:, hd_, g_ * 512:(g_ + 1) * 512],
                    start=(hd_ == 0), stop=(hd_ == 7)), oproj)
            op_c = p.cnt[oproj]
            p.wait("v", oproj, op_c)
            p.wait("v", xcl, xcl_c)
            ystp = f"{tag}_yst" + str(idx % 2)
            if idx >= 2:
                p.wait("v", ystp, yst_hist[idx - 2])
            if layer == 1 and idx == 0:
                p.wait("v", f"{tag}_ost", ost_done)
            p.op("v", lambda e, bk=bnk, xc_=xc, yc_=yc: e.tensor_tensor(
                yc_[:, :], xc_[:, :], bk[:, :], op=ALU.add), resid)
            p.wait("sy", resid)
            p.dma("sy", y_d[mc * 128:(mc + 1) * 128, g * 512:(g + 1) * 512],
                  yc[:, :], ystp)
            yst_hist.append(p.cnt[ystp])
    p.wait("sy", f"{tag}_yst0")
    p.wait("sy", f"{tag}_yst1")
    return None


NTILE = 47


def moe_bufs(p, tag):
    ar = p.sb(f"{tag}_ar", [128, 20480], BF16)
    b = {
        "ar": ar,
        "xln": p.sb(f"{tag}_xln", [128, NCH, TPC], BF16),
        "xs_all": p.sb(f"{tag}_xs", [128, 16, 1024], BF16),
        "gw": p.sb(f"{tag}_gw", [128, NCH, 32], BF16),
        "oh_sb": p.sb(f"{tag}_oh", [128, 16, 32], F32),
        "rank_sb": p.sb(f"{tag}_rank", [128, 16, 32], F32),
        "cs_sb": p.sb(f"{tag}_cs", [1, 16, 32], F32),
        "score": p.sb(f"{tag}_scr", [128, 16], F32),
        "maxv": p.sb(f"{tag}_maxv", [128, 1], F32),
        "eid": p.sb(f"{tag}_eid", [128, 1], F32),
        "eq": p.sb(f"{tag}_eq", [128, 32], F32),
        "sel": p.sb(f"{tag}_sel", [128, 32], F32),
        "imb": p.sb(f"{tag}_imb", [128, 32], F32),
        "cnt_i": p.sb(f"{tag}_cnti", [32, 1], I32),
        "pad_f": p.sb(f"{tag}_padf", [32, 1], F32),
        "cmp": p.sb(f"{tag}_cmp", [32, NTILE], F32),
        "erow": p.sb(f"{tag}_erow", [1, NTILE], F32),
        "esml": p.sb(f"{tag}_esml", [1, NTILE], F32),
        "eoff": p.sb(f"{tag}_eoff", [1, 2], F32),
        "base_row": p.sb(f"{tag}_base", [1, 32], F32),
        "t2": p.sb(f"{tag}_t2", [128, 32], F32),
        "t2r": p.sb(f"{tag}_t2r", [1, 32], F32),
        "posu": p.sb(f"{tag}_posu", [128, 16], U32),
        "posf": p.sb(f"{tag}_posf", [128, 16], F32),
        "idxg": p.sb(f"{tag}_idxg", [128, 8], U32),
        "idxd": p.sb(f"{tag}_idxd", [128, 16], U32),
        "idgf": p.sb(f"{tag}_idgf", [128, 8], F32),
        "iddf": p.sb(f"{tag}_iddf", [128, 16], F32),
        "kcoff": p.sb(f"{tag}_kcoff", [128, 8], F32),
        "ichoff": p.sb(f"{tag}_ichoff", [128, 16], F32),
        "xsf0": p.sb(f"{tag}_xsf0", [128, NCH, 128], BF16),
        "xsf1": p.sb(f"{tag}_xsf1", [128, NCH, 128], BF16),
        "dsb": p.sb(f"{tag}_dsb", [128, 1024], BF16),
        "rtc": p.sb(f"{tag}_rtc", [128, 128], F32),
        "lns": ln_scratch(p, f"{tag}_lns"),
    }
    b["wg0"] = ar[:, 0:2048];      b["wg1"] = ar[:, 2048:4096]
    b["wu0"] = ar[:, 4096:6144];   b["wu1"] = ar[:, 6144:8192]
    b["wd0"] = ar[:, 8192:9216];   b["wd1"] = ar[:, 9216:10240]
    b["gs"] = r3(ar[:, 10240:12288], 4)
    b["us"] = r3(ar[:, 12288:14336], 4)
    b["hh"] = r3(ar[:, 14336:16384], 4)
    b["hT"] = r3(ar[:, 16384:18432], 16)
    b["xs_t0"] = ar[:, 18432:19456]
    b["xs_t1"] = ar[:, 19456:20480]
    b["sgwb0"] = r3(ar[:, 0:1024], NCH);   b["suwb0"] = r3(ar[:, 1024:2048], NCH)
    b["sgwb1"] = r3(ar[:, 2048:3072], NCH); b["suwb1"] = r3(ar[:, 3072:4096], NCH)
    b["sdwb0"] = r3(ar[:, 4096:6144], 16)
    b["sdwb1"] = r3(ar[:, 6144:8192], 16)
    b["h_sh"] = r3(ar[:, 10240:18432], 16)
    b["gsh"] = ar[:, 18432:18944]
    b["ush"] = ar[:, 18944:19456]
    return b


def emit_moe(p, C, tag, x_sb, gwd, sgw_d, suw_d, sdw_d,
             rgw_d, ruw_d, rdw_d, xs_sorted, dout, pb, bufs):
    xln = bufs["xln"]
    lns = bufs["lns"]
    ysh = r3(bufs["xs_all"][:, :, :].rearrange("p a b -> p (a b)"), NCH)
    rt, sc_ld = f"{tag}_rt", f"{tag}_scld"

    for c in range(NCH):
        p.dma("g", bufs["gw"][:, c, :], gwd[c * 128:(c + 1) * 128, :], sc_ld)
    sc_done = p.cnt[sc_ld]
    for e in ("t", "v", "s", "g"):
        p.wait(e, sc_ld, sc_done)
        p.wait(e, "c_ld")
        p.wait(e, "xld")
    p.op("v", lambda e: e.tensor_scalar(bufs["imb"][:, :], C.iota32_f[:, :], -65536.0, None, op0=ALU.add), rt)
    p.op("v", lambda e: e.memset(bufs["base_row"][:, :], 0.0), rt)
    for k in range(8):
        p.op("v", lambda e, k_=k: e.tensor_scalar(bufs["kcoff"][:, k_:k_ + 1], C.iota128_f[:, :], 128.0 * k_, None, op0=ALU.add), rt)
    for k in range(16):
        p.op("v", lambda e, k_=k: e.tensor_scalar(bufs["ichoff"][:, k_:k_ + 1], C.iota128_f[:, :], 128.0 * k_, None, op0=ALU.add), rt)

    lgm, ohm, sct = f"{tag}_lg", f"{tag}_ohm", f"{tag}_sct"
    cntm, rkm, rke = f"{tag}_cntm", f"{tag}_rkm", f"{tag}_rke"
    for tt in range(4):
        nm = emit_ln(p, C, "Gln", x_sb, xln, tt * 512, 512, pb[0:4], lns)
        for sub4 in range(4):
            s = tt * 4 + sub4
            tsl = slice(tt * 512 + sub4 * 128, tt * 512 + sub4 * 128 + 128)
            p.wait("t", nm)
            p.wait("t", rt)
            p.wait("t", sct)
            for c in range(NCH):
                p.op("t", lambda e, c_=c, ts_=tsl: e.matmul(
                    pb[4][:, 0:32], xln[:, c_, ts_], bufs["gw"][:, c_, :],
                    start=(c_ == 0), stop=(c_ == 7)), lgm)
            lg_c = p.cnt[lgm]
            p.wait("v", lgm, lg_c)
            p.op("v", lambda e: e.reduce_max(bufs["maxv"][:, :], pb[4][:, 0:32], axis=AX), rt)
            p.wait("v", rt)
            p.op("v", lambda e: e.tensor_scalar(bufs["eq"][:, :], pb[4][:, 0:32], bufs["maxv"][:, 0:1], None, op0=ALU.is_equal), rt)
            p.wait("v", rt)
            p.op("v", lambda e: e.tensor_tensor(bufs["sel"][:, :], bufs["eq"][:, :], bufs["imb"][:, :], op=ALU.mult), rt)
            p.wait("v", rt)
            p.op("v", lambda e: e.tensor_scalar(bufs["sel"][:, :], bufs["sel"][:, :], 65536.0, None, op0=ALU.add), rt)
            p.wait("v", rt)
            p.op("v", lambda e: e.tensor_reduce(bufs["eid"][:, :], bufs["sel"][:, :], axis=AX, op=ALU.min), rt)
            p.wait("v", rt)
            p.op("v", lambda e, s_=s: e.tensor_scalar(bufs["oh_sb"][:, s_, :], C.iota32_f[:, :], bufs["eid"][:, 0:1], None, op0=ALU.is_equal), rt)
            rt_oh = p.cnt[rt]
            p.wait("s", rt, rt_oh)
            p.op("s", lambda e, s_=s: e.activation(bufs["score"][:, s_:s_ + 1], bufs["maxv"][:, :], AF.Sigmoid), ohm)
            score_c = p.cnt[ohm]
            p.wait("t", rt, rt_oh)
            p.op("t", lambda e, s_=s: e.matmul(
                pb[5][0:32, 0:1], bufs["oh_sb"][:, s_, :], C.ones_col_f32[:, :],
                start=(s_ == 0), stop=(s_ == 15)), cntm)
            p.wait("t", rke)
            p.op("t", lambda e, s_=s: e.matmul(
                pb[6][:, 0:32], C.tri128[:, :], bufs["oh_sb"][:, s_, :],
                start=True, stop=True), rkm)
            p.op("t", lambda e, s_=s: e.matmul(
                pb[7][0:1, 0:32], C.ones_col_f32[:, :], bufs["oh_sb"][:, s_, :],
                start=True, stop=True), rkm)
            rkm_c = p.cnt[rkm]
            p.wait("v", rkm, rkm_c)
            p.op("v", lambda e, s_=s: e.tensor_copy(bufs["rank_sb"][:, s_, :], pb[6][:, 0:32]), rke)
            p.op("v", lambda e, s_=s: e.tensor_copy(bufs["cs_sb"][0:1, s_, :], pb[7][0:1, 0:32]), rke)
            pTa = pb[4].bitcast(BF16)
            p.wait("t", rt, rt_oh)
            for c in range(NCH):
                p.op("t", lambda e, c_=c, ts_=tsl, pT_=pTa: e.transpose(
                    pT_[:, c_ * 128:(c_ + 1) * 128], xln[:, c_, ts_],
                    C.ident_bf[:, :]), sct)
            sct_c = p.cnt[sct]
            p.wait("v", sct, sct_c)
            p.wait("v", ohm, score_c)
            p.op("v", lambda e, s_=s, pT_=pTa: e.tensor_scalar(
                bufs["xs_all"][:, s_, :], pT_[:, 0:1024], bufs["score"][:, s_:s_ + 1],
                None, op0=ALU.mult), sct)
    cnt_done = p.cnt[cntm]
    xln_done = p.cnt["Gln_nm"]

    a3, posb = f"{tag}_a3", f"{tag}_posb"
    p.wait("v", cntm, cnt_done)
    p.op("v", lambda e: e.tensor_copy(bufs["cnt_i"][:, :], pb[5][0:32, 0:1]), a3)
    p.wait("v", a3)
    p.op("v", lambda e: e.tensor_scalar(bufs["cnt_i"][:, :], bufs["cnt_i"][:, :], 127.0, None, op0=ALU.add), a3)
    p.wait("v", a3)
    p.op("v", lambda e: e.tensor_scalar(bufs["cnt_i"][:, :], bufs["cnt_i"][:, :], 7, None, op0=ALU.arith_shift_right), a3)
    p.wait("v", a3)
    p.op("v", lambda e: e.tensor_scalar(bufs["cnt_i"][:, :], bufs["cnt_i"][:, :], 7, None, op0=ALU.arith_shift_left), a3)
    p.wait("v", a3)
    p.op("v", lambda e: e.tensor_copy(bufs["pad_f"][:, :], bufs["cnt_i"][:, :]), a3)
    a3_pad = p.cnt[a3]
    p.wait("t", a3, a3_pad)
    p.wait("t", sct)
    p.op("t", lambda e: e.matmul(pb[4][0:32, 0:1], C.tri32[:, :], bufs["pad_f"][:, :], start=True, stop=True), a3)
    po_c = p.cnt[a3]
    p.wait("v", a3, po_c)
    p.op("v", lambda e: e.tensor_scalar(bufs["cmp"][:, :], C.tbases[:, 0:NTILE], pb[4][0:32, 0:1], None, op0=ALU.is_ge), a3)
    cmp_c = p.cnt[a3]
    p.op("v", lambda e: e.tensor_copy(bufs["pad_f"][:, :], pb[4][0:32, 0:1]), a3)
    pocp = p.cnt[a3]
    p.wait("t", a3, cmp_c)
    p.wait("t", a3, pocp)
    p.op("t", lambda e: e.matmul(pb[6][0:1, 0:NTILE], C.ones_col_f32[0:32, :], bufs["cmp"][:, :], start=True, stop=True), a3)
    p.op("t", lambda e: e.transpose(pb[7][0:1, 0:32], bufs["pad_f"][:, :], C.ident_f32[0:32, 0:32]), a3)
    por_c = p.cnt[a3]
    p.wait("v", a3, por_c)
    p.op("v", lambda e: e.tensor_scalar(bufs["erow"][0:1, :], pb[6][0:1, 0:NTILE], -1.0, None, op0=ALU.add), a3)
    p.op("v", lambda e: e.tensor_copy(bufs["t2r"][0:1, :], pb[7][0:1, 0:32]), a3)
    a3_er = p.cnt[a3]
    p.wait("t", a3, a3_er)
    p.op("t", lambda e: e.matmul(pb[5][0:1, 0:1], bufs["pad_f"][:, :], C.ones_col_f32[0:32, 0:1], start=True, stop=True), a3)
    tot_c = p.cnt[a3]
    p.wait("v", a3, tot_c)
    p.op("v", lambda e: e.tensor_scalar(bufs["esml"][0:1, :], C.tbrow[0:1, 0:NTILE], pb[5][0:1, 0:1], None, op0=ALU.is_ge), a3)
    p.wait("v", a3)
    p.op("v", lambda e: e.tensor_scalar(bufs["esml"][0:1, :], bufs["esml"][0:1, :], 3e9, None, op0=ALU.mult), a3)
    a3_done = p.cnt[a3]

    pos = f"{tag}_pos"
    for s in range(16):
        p.wait("v", pos)
        p.wait("v", a3, a3_done)
        p.op("v", lambda e: e.tensor_tensor(bufs["t2"][0:1, 0:32], bufs["t2r"][0:1, :], bufs["base_row"][0:1, :], op=ALU.add), pos)
        pc = p.cnt[pos]
        p.wait("t", pos, pc)
        p.wait("t", posb)
        p.op("t", lambda e: e.matmul(pb[6][:, 0:32], C.ones1_f32[:, :], bufs["t2"][0:1, 0:32], start=True, stop=True), posb)
        pb_c = p.cnt[posb]
        p.wait("v", posb, pb_c)
        p.op("v", lambda e, s_=s: e.tensor_tensor(bufs["t2"][:, 0:32], bufs["rank_sb"][:, s_, :], pb[6][:, 0:32], op=ALU.add), pos)
        p.wait("v", pos)
        p.op("v", lambda e, s_=s: e.tensor_tensor(bufs["t2"][:, 0:32], bufs["t2"][:, 0:32], bufs["oh_sb"][:, s_, :], op=ALU.mult), pos)
        p.wait("v", pos)
        p.op("v", lambda e, s_=s: e.reduce_sum(bufs["posf"][:, s_:s_ + 1], bufs["t2"][:, 0:32], axis=AX), pos)
        p.wait("v", pos)
        p.op("v", lambda e, s_=s: e.tensor_copy(bufs["posu"][:, s_:s_ + 1], bufs["posf"][:, s_:s_ + 1]), pos)
        p.op("v", lambda e, s_=s: e.tensor_tensor(bufs["base_row"][0:1, :], bufs["base_row"][0:1, :], bufs["cs_sb"][0:1, s_, :], op=ALU.add), pos)
    pos_done = p.cnt[pos]

    scat = f"{tag}_scat"
    zs = f"{tag}_zs"
    p.op("g", lambda e: e.memset(bufs["dsb"][:, :], 0.0), zs + "m")
    p.wait("g", zs + "m")
    for jz in range(NTILE):
        p.dma("g", xs_sorted[jz * 128:(jz + 1) * 128, :], bufs["dsb"][:, :], zs)
    p.wait("g", zs)
    p.wait("g", pos, pos_done)
    for s in range(16):
        p.op("g", lambda e, s_=s: e.indirect_dma_start(
            out=xs_sorted[:, :],
            out_offset=IndirectOffsetOnAxis(ap=bufs["posu"][:, s_:s_ + 1], axis=0),
            in_=bufs["xs_all"][:, s_, :], in_offset=None,
            oob_is_err=False), scat, 16)
    scat_done = p.cnt[scat]

    gth, xsg, rmm, rev, rhm, rtp, rdm, rde, rst, idxs = (
        f"{tag}_gth", f"{tag}_xsg", f"{tag}_rmm", f"{tag}_rev", f"{tag}_rhm",
        f"{tag}_rtp", f"{tag}_rdm", f"{tag}_rde", f"{tag}_rst", f"{tag}_idx")
    if not hasattr(p, "_moe_regs"):
        p._moe_regs = {}

        def _init_regs(e, R=p._moe_regs):
            R["bc_g"] = e.alloc_register("moe_bc_g")
            e.reg_mov(R["bc_g"], 32767)
            R["bc_d"] = e.alloc_register("moe_bc_d")
            e.reg_mov(R["bc_d"], 65535)
        p.emit("g", _init_regs)
    REGS = p._moe_regs
    rmm_hist = {}
    rtp_hist = {}
    wuse = {}
    duse = {}
    for j in range(NTILE):
        par = j % 2
        wg = bufs["wg0"] if par == 0 else bufs["wg1"]
        wu = bufs["wu0"] if par == 0 else bufs["wu1"]
        wd = bufs["wd0"] if par == 0 else bufs["wd1"]
        xst = bufs["xs_t0"] if par == 0 else bufs["xs_t1"]
        xsf = bufs["xsf0"] if par == 0 else bufs["xsf1"]
        gthp, xsgp = gth + str(par), xsg + str(par)
        p.wait("v", a3, a3_done)
        p.wait("v", posb)
        p.op("v", lambda e, j_=j: e.tensor_scalar(
            bufs["eoff"][0:1, 0:1], bufs["erow"][0:1, j_:j_ + 1], 1024.0,
            bufs["esml"][0:1, j_:j_ + 1], op0=ALU.mult, op1=ALU.add), idxs)
        p.op("v", lambda e, j_=j: e.tensor_scalar(
            bufs["eoff"][0:1, 1:2], bufs["erow"][0:1, j_:j_ + 1], 2048.0,
            bufs["esml"][0:1, j_:j_ + 1], op0=ALU.mult, op1=ALU.add), idxs)
        ei_c = p.cnt[idxs]
        p.wait("t", idxs, ei_c)
        p.wait("t", posb)
        p.op("t", lambda e: e.matmul(pb[6][:, 0:2], C.ones1_f32[:, :], bufs["eoff"][0:1, 0:2], start=True, stop=True), posb)
        eb_c = p.cnt[posb]
        p.wait("v", posb, eb_c)
        for gs_ in range(4):
            p.wait("v", gth + str(gs_))
        p.wait("v", gth + "d0")
        p.wait("v", gth + "d1")
        p.op("v", lambda e: e.tensor_scalar(bufs["idgf"][:, :], bufs["kcoff"][:, :], pb[6][:, 0:1], None, op0=ALU.add), idxs)
        p.op("v", lambda e: e.tensor_scalar(bufs["iddf"][:, :], bufs["ichoff"][:, :], pb[6][:, 1:2], None, op0=ALU.add), idxs)
        p.wait("v", idxs)
        p.op("v", lambda e: e.tensor_copy(bufs["idxg"][:, :], bufs["idgf"][:, :]), idxs)
        p.op("v", lambda e: e.tensor_copy(bufs["idxd"][:, :], bufs["iddf"][:, :]), idxs)
        idx_c = p.cnt[idxs]
        p.wait("g", idxs, idx_c)
        p.wait("g", scat, scat_done)
        p.wait("g", rtp, rtp_hist.get(j - 2, 0))
        p.dma("g", xst[:, :], xs_sorted[j * 128:(j + 1) * 128, :], xsgp)
        xsg_c = p.cnt[xsgp]
        p.wait("t", xsgp, xsg_c)
        p.wait("t", rde)
        pTr = pb[0].bitcast(BF16)
        for c in range(NCH):
            p.op("t", lambda e, c_=c, x_=xst, pT_=pTr: e.transpose(
                pT_[:, c_ * 128:(c_ + 1) * 128],
                r3(x_[:, :], NCH)[:, c_, :], C.ident_bf[:, :]), rtp)
        rtp_c = p.cnt[rtp]
        p.wait("v", rtp, rtp_c)
        p.op("v", lambda e, x_=xsf, pT_=pTr: e.tensor_copy(
            x_[:, :, :], r3(pT_[:, 0:1024], NCH)), rtp)
        xsf_c = p.cnt[rtp]
        rtp_hist[j] = xsf_c
        p.wait("t", rtp, xsf_c)
        p.wait("t", rev)
        for half, (wsrc, w0, w1, pbase) in enumerate(
                [(rgw_d, bufs["wg0"], bufs["wg1"], 0),
                 (ruw_d, bufs["wu0"], bufs["wu1"], 4)]):
            for c in range(NCH):
                wpar = w0 if c % 2 == 0 else w1
                gsem = gth + str(half * 2 + c % 2)
                p.wait("g", rmm, wuse.get((j, half, c - 2), wuse.get((j - 1, half, 6 + c), 0)))
                p.op("g", lambda e, c_=c, w=wpar, src=wsrc: e.indirect_dma_start(
                    out=w[:, :], out_offset=None, in_=src[:, :],
                    in_offset=IndirectOffsetOnAxis(ap=bufs["idxg"][:, c_:c_ + 1], axis=0),
                    bounds_check=REGS["bc_g"], oob_is_err=False), gsem, 16)
                g_c = p.cnt[gsem]
                p.wait("t", gsem, g_c)
                for ig in range(4):
                    p.op("t", lambda e, ig_=ig, c_=c, x_=xsf, w=wpar, pb0=pbase: e.matmul(
                        pb[pb0 + ig_][:, :], x_[:, c_, :],
                        w[:, ig_ * 512:(ig_ + 1) * 512],
                        start=(c_ == 0), stop=(c_ == 7)), rmm)
                wuse[(j, half, c)] = p.cnt[rmm]
        rmm_c = p.cnt[rmm]
        rmm_hist[j] = rmm_c
        p.wait("s", rmm, rmm_c)
        p.wait("s", rhm)
        for ig in range(4):
            p.op("s", lambda e, ig_=ig: e.activation(bufs["gs"][:, ig_, :], pb[ig_][:, :], AF.Silu), rev)
            p.op("s", lambda e, ig_=ig: e.copy(bufs["us"][:, ig_, :], pb[4 + ig_][:, :]), rev)
        rev_c = p.cnt[rev]
        p.wait("v", rev, rev_c)
        p.wait("v", rtp)
        for ig in range(4):
            p.op("v", lambda e, ig_=ig: e.tensor_tensor(bufs["hh"][:, ig_, :], bufs["gs"][:, ig_, :], bufs["us"][:, ig_, :], op=ALU.mult), rhm)
        rhm_c = p.cnt[rhm]
        p.wait("t", rhm, rhm_c)
        p.wait("t", rdm)
        pT1r = pb[1].bitcast(BF16)
        for q in range(4):
            for w4 in range(4):
                icc = q * 4 + w4
                p.op("t", lambda e, icc_=icc, w4_=w4, pT_=pT1r: e.transpose(
                    pT_[:, w4_ * 128:(w4_ + 1) * 128],
                    bufs["hh"][:, icc_ // 4, (icc_ % 4) * 128:(icc_ % 4 + 1) * 128],
                    C.ident_bf[:, :]), rtp)
            qc = p.cnt[rtp]
            p.wait("v", rtp, qc)
            p.op("v", lambda e, q_=q, pT_=pT1r: e.tensor_copy(
                bufs["hT"][:, 4 * q_:4 * q_ + 4, :], r3(pT_[:, 0:512], 4)), rtp)
            hTq = p.cnt[rtp]
            p.wait("t", rtp, hTq)
        hT_c = p.cnt[rtp]
        p.wait("t", rtp, hT_c)
        p.wait("t", rde)
        for ic in range(16):
            wdp = bufs["wd0"] if ic % 2 == 0 else bufs["wd1"]
            dsem = gth + "d" + str(ic % 2)
            p.wait("g", rdm, duse.get((j, ic - 2), duse.get((j - 1, 14 + ic), 0)))
            p.op("g", lambda e, ic_=ic, w=wdp: e.indirect_dma_start(
                out=w[:, :], out_offset=None, in_=rdw_d[:, :],
                in_offset=IndirectOffsetOnAxis(ap=bufs["idxd"][:, ic_:ic_ + 1], axis=0),
                bounds_check=REGS["bc_d"], oob_is_err=False), dsem, 16)
            d_c = p.cnt[dsem]
            p.wait("t", dsem, d_c)
            for dg in range(2):
                p.op("t", lambda e, ic_=ic, dg_=dg, w=wdp: e.matmul(
                    pb[2 + dg_][:, :], bufs["hT"][:, ic_, :],
                    w[:, dg_ * 512:(dg_ + 1) * 512],
                    start=(ic_ == 0), stop=(ic_ == 15)), rdm)
            duse[(j, ic)] = p.cnt[rdm]
        rdm_c = p.cnt[rdm]
        p.wait("s", rdm, rdm_c)
        p.wait("s", rst)
        p.wait("s", f"{tag}_zs")
        p.op("s", lambda e: e.copy(bufs["dsb"][:, 0:512], pb[2][:, :]), rde)
        p.op("s", lambda e: e.copy(bufs["dsb"][:, 512:1024], pb[3][:, :]), rde)
        rde_c = p.cnt[rde]
        p.wait("g", rde, rde_c)
        p.dma("g", dout[j * 128:(j + 1) * 128, :], bufs["dsb"][:, :], rst)
    routed_done = p.cnt[rst]
    routed_mm_done = p.cnt[rdm]

    shw, shm, she, shh, shy = (f"{tag}_shw", f"{tag}_shm", f"{tag}_she",
                               f"{tag}_shh", f"{tag}_shy")
    shw_hist = {}
    for tt in range(4):
        ts_ = slice(tt * 512, (tt + 1) * 512)
        for ic in range(16):
            par = ic % 2
            wgs = bufs["sgwb0"] if par == 0 else bufs["sgwb1"]
            wus = bufs["suwb0"] if par == 0 else bufs["suwb1"]
            shwp = shw + str(par)
            p.wait("g", shm, shw_hist.get((tt, ic - 2), 0))
            if tt == 0 and ic < 2:
                p.wait("g", rmm, p.cnt[rmm])
                p.wait("g", rdm, routed_mm_done)
                p.wait("g", rtp, p.cnt[rtp])
                p.wait("g", scat, scat_done)
            p.dma("g", wgs[:, :, :], sgw_d[:, ic * 128:(ic + 1) * 128].rearrange(
                "(c p) m -> p c m", p=128), shwp)
            p.dma("g", wus[:, :, :], suw_d[:, ic * 128:(ic + 1) * 128].rearrange(
                "(c p) m -> p c m", p=128), shwp)
            shw_c = p.cnt[shwp]
            gb_ = pb[4 + par]
            ub_ = pb[6 + par]
            p.wait("t", shwp, shw_c)
            p.wait("t", "Gln_nm", xln_done)
            p.wait("t", she)
            for c in range(NCH):
                p.op("t", lambda e, c_=c, w=wgs, ts2=ts_, b=gb_: e.matmul(
                    b[:, :], w[:, c_, :], xln[:, c_, ts2],
                    start=(c_ == 0), stop=(c_ == 7)), shm)
            for c in range(NCH):
                p.op("t", lambda e, c_=c, w=wus, ts2=ts_, b=ub_: e.matmul(
                    b[:, :], w[:, c_, :], xln[:, c_, ts2],
                    start=(c_ == 0), stop=(c_ == 7)), shm)
            shm_c = p.cnt[shm]
            shw_hist[(tt, ic)] = shm_c
            p.wait("s", shm, shm_c)
            p.wait("s", shh)
            p.op("s", lambda e, b=gb_: e.activation(bufs["gsh"][:, :], b[:, :], AF.Silu), she)
            p.op("s", lambda e, b=ub_: e.copy(bufs["ush"][:, :], b[:, :]), she)
            she_c = p.cnt[she]
            p.wait("v", she, she_c)
            p.op("v", lambda e, ic_=ic: e.tensor_tensor(bufs["h_sh"][:, ic_, :], bufs["gsh"][:, :], bufs["ush"][:, :], op=ALU.mult), shh)
        h_done = p.cnt[shh]
        for dm in range(NCH):
            par = dm % 2
            wds = bufs["sdwb0"] if par == 0 else bufs["sdwb1"]
            shwp = shw + "d" + str(par)
            p.wait("g", shm, shw_hist.get((tt, "d", dm - 2), 0))
            p.dma("g", wds[:, :, :], sdw_d[:, dm * 128:(dm + 1) * 128].rearrange(
                "(c p) m -> p c m", p=128), shwp)
            shwd_c = p.cnt[shwp]
            yb = pb[2 + par]
            p.wait("t", shwp, shwd_c)
            p.wait("t", shh, h_done)
            p.wait("t", shy)
            for ic in range(16):
                p.op("t", lambda e, ic_=ic, w=wds, b=yb: e.matmul(
                    b[:, :], w[:, ic_, :], bufs["h_sh"][:, ic_, :],
                    start=(ic_ == 0), stop=(ic_ == 15)), shm)
            y_c = p.cnt[shm]
            shw_hist[(tt, "d", dm)] = y_c
            p.wait("s", shm, y_c)
            p.wait("s", shy)
            p.op("s", lambda e, dm_=dm, ts2=ts_, b=yb: e.copy(ysh[:, dm_, ts2], b[:, :]), shy)
    ysh_done = p.cnt[shy]

    cg, ctp, cres = f"{tag}_cg", f"{tag}_ctp", f"{tag}_cres"
    rt0, rt1 = bufs["xs_t0"], bufs["xs_t1"]
    ctp_hist = {}
    for s in range(16):
        par = s % 2
        rtb = rt0 if par == 0 else rt1
        cgp = cg + str(par)
        tsl = slice(s * 128, (s + 1) * 128)
        p.wait("g", rst, routed_done)
        if s < 2:
            p.wait("g", shm, p.cnt[shm])
        p.wait("g", ctp, ctp_hist.get(s - 2, 0))
        p.op("g", lambda e, s_=s, b=rtb: e.indirect_dma_start(
            out=b[:, :], out_offset=None, in_=dout[:, :],
            in_offset=IndirectOffsetOnAxis(ap=bufs["posu"][:, s_:s_ + 1], axis=0),
            oob_is_err=False), cgp, 16)
        cg_c = p.cnt[cgp]
        p.wait("t", cgp, cg_c)
        p.wait("t", cres)
        p.wait("t", shy, ysh_done)
        pTc = pb[0].bitcast(BF16)
        for c in range(NCH):
            p.op("t", lambda e, c_=c, b=rtb, pT_=pTc: e.transpose(
                pT_[:, c_ * 128:(c_ + 1) * 128], r3(b[:, :], NCH)[:, c_, :],
                C.ident_bf[:, :]), ctp)
        ctp_c = p.cnt[ctp]
        ctp_hist[s] = ctp_c
        p.wait("v", ctp, ctp_c)
        for c in range(NCH):
            p.wait("v", cres)
            p.op("v", lambda e, c_=c, ts2=tsl, pT_=pTc: e.tensor_tensor(
                bufs["rtc"][:, :], pT_[:, c_ * 128:(c_ + 1) * 128], ysh[:, c_, ts2], op=ALU.add), cres)
            p.wait("v", cres)
            p.op("v", lambda e, c_=c, ts2=tsl: e.scalar_tensor_tensor(
                x_sb[:, c_, ts2], bufs["rtc"][:, :], 0.5, x_sb[:, c_, ts2],
                op0=ALU.mult, op1=ALU.add), cres)
    return cres


def build_moe():
    p = Prog("moe")
    nc = p.nc
    x_d = nc.dram_tensor("x_fm", [D, TPC], F32, kind="ExternalInput").ap()
    gwd = nc.dram_tensor("gate_w", [D, 32], F32, kind="ExternalInput").ap()
    sgw_d = nc.dram_tensor("sgw", [D, 2048], F32, kind="ExternalInput").ap()
    suw_d = nc.dram_tensor("suw", [D, 2048], F32, kind="ExternalInput").ap()
    sdw_d = nc.dram_tensor("sdw", [2048, D], F32, kind="ExternalInput").ap()
    rgw_d = nc.dram_tensor("rgw", [32 * 1024, 2048], F32, kind="ExternalInput").ap()
    ruw_d = nc.dram_tensor("ruw", [32 * 1024, 2048], F32, kind="ExternalInput").ap()
    rdw_d = nc.dram_tensor("rdw", [32 * 2048, 1024], F32, kind="ExternalInput").ap()
    y_d = nc.dram_tensor("y_fm", [D, TPC], F32, kind="ExternalOutput").ap()
    xs_sorted = nc.dram_tensor("xs_sorted", [NTILE * 128, D], BF16).ap()
    dout = nc.dram_tensor("dout", [NTILE * 128, D], BF16).ap()

    C = Consts(p)
    x_sb = p.sb("x_sb", [128, NCH, TPC], F32)
    pb = [p.psum(f"pb{i}", [128, 512], F32) for i in range(8)]
    bufs = moe_bufs(p, "m")
    for c in range(NCH):
        p.dma("sy", x_sb[:, c, :], x_d[c * 128:(c + 1) * 128, :], "xld")
    cres = emit_moe(p, C, "m", x_sb, gwd, sgw_d, suw_d, sdw_d,
                    rgw_d, ruw_d, rdw_d, xs_sorted, dout, pb, bufs)
    p.wait("sy", cres)
    for c in range(NCH):
        p.dma("sy", y_d[c * 128:(c + 1) * 128, :], x_sb[:, c, :], "yst")
    p.wait("sy", "yst")
    return p


def moe_inputs(core, x_fm, gate_w, sgw, suw, sdw, rgw, ruw, rdw):
    cd = make_consts()
    return dict(
        x_fm=np.ascontiguousarray(x_fm),
        gate_w=np.ascontiguousarray(gate_w),
        sgw=np.ascontiguousarray(sgw), suw=np.ascontiguousarray(suw),
        sdw=np.ascontiguousarray(sdw),
        rgw=np.ascontiguousarray(rgw.reshape(32 * 1024, 2048)),
        ruw=np.ascontiguousarray(ruw.reshape(32 * 1024, 2048)),
        rdw=np.ascontiguousarray(rdw.reshape(32 * 2048, 1024)),
        **cd,
    )


def build_attn(layer, rr):
    p = Prog(f"attn{layer}_{rr}")
    nc = p.nc
    x_d = nc.dram_tensor("x_fm", [D, TPC], F32, kind="ExternalInput").ap()
    xfull_d = nc.dram_tensor("xfull_fm", [D, S], F32, kind="ExternalInput").ap()
    wq_d = nc.dram_tensor("wq", [D, D], F32, kind="ExternalInput").ap()
    wk_d = nc.dram_tensor("wk", [D, D], F32, kind="ExternalInput").ap()
    wv_d = nc.dram_tensor("wv", [D, D], F32, kind="ExternalInput").ap()
    woT_d = nc.dram_tensor("woT", [D, D], F32, kind="ExternalInput").ap()
    cs_d = nc.dram_tensor("csP", [S, D], BF16, kind="ExternalInput").ap()
    masks_d = nc.dram_tensor("masksT", [4, 128, 512], BF16, kind="ExternalInput").ap()
    sccol_d = nc.dram_tensor("sccol", [128, 1], F32, kind="ExternalInput").ap()
    y_d = nc.dram_tensor("y_fm", [D, TPC], F32, kind="ExternalOutput").ap()
    kbuf = nc.dram_tensor("kbuf", [S, D], BF16).ap()
    vbuf = nc.dram_tensor("vbuf", [S, D], BF16).ap()
    qbuf = nc.dram_tensor("qbuf", [TPC, D], BF16).ap()

    C = Consts(p)
    pb = [p.psum(f"pb{i}", [128, 512], F32) for i in range(8)]
    bufs = attn_bufs(p, "at")
    emit_attn(p, C, "at", layer, rr, xfull_d, x_d, y_d, wq_d, wk_d, wv_d,
              woT_d, cs_d, masks_d, sccol_d, kbuf, vbuf, qbuf, pb, bufs)
    return p


def rope_table():
    half = 64
    dims = np.arange(half, dtype=np.float32)
    ang = np.power(np.float32(10000.0), -dims / np.float32(half))
    radians = np.arange(S, dtype=np.float32)[:, None] * ang[None, :]
    cos = np.cos(radians).astype(np.float32)
    sin = np.sin(radians).astype(np.float32)
    cs = np.concatenate([cos, sin], axis=1)
    csP = np.broadcast_to(cs[:, None, :], (S, 8, 128)).reshape(S, 1024)
    return np.ascontiguousarray(csP.astype(BF16NP))


def attn_masks():
    m = np.zeros((4, 128, 512), np.float32)
    kk = np.arange(128)[:, None]
    qq = np.arange(512)[None, :]
    for j in range(4):
        m[j] = (qq >= kk + 128 * j)
    return m.astype(BF16NP)


def attn_inputs(core, layer, x_fm, x_full, wq, wk, wv, wo):
    b = core // 4
    cd = make_consts()
    ap = attn_params(layer)
    if layer == 0:
        wqr = wq.transpose(1, 0, 2).reshape(D, D)
        wkr = wk.transpose(1, 0, 2).reshape(D, D)
        wvr = wv.transpose(1, 0, 2).reshape(D, D)
        woTr = wo.reshape(D, D)
    else:
        wqr = wq.transpose(2, 0, 1).reshape(D, D)
        wkr = wk.transpose(2, 0, 1).reshape(D, D)
        wvr = wv.transpose(2, 0, 1).reshape(D, D)
        woTr = wo.T
    xfull_fm = np.ascontiguousarray(x_full[b].T)
    return dict(
        x_fm=np.ascontiguousarray(x_fm),
        xfull_fm=xfull_fm,
        wq=np.ascontiguousarray(wqr), wk=np.ascontiguousarray(wkr),
        wv=np.ascontiguousarray(wvr), woT=np.ascontiguousarray(woTr),
        csP=rope_table(), masksT=attn_masks(),
        sccol=np.full((128, 1), ap["inv_scale"], np.float32),
        **cd,
    )



KVROWS = 2052


def attn_fused_bufs(p, tag, A1, A2, A3):
    b = {
        "xfl0": r3(A1[:, 0:4096], NCH),
        "xfl1": r3(A1[:, 4096:8192], NCH),
        "csb": A1[:, 8192:9216],
        "csb2": A1[:, 9216:10240],
        "q_sb": r3(A1[:, 0:16384], NCH),
        "khb": A3[:, 0:8192],
        "vhb": r3(A3[:, 8192:16384], 64),
        "kout": r3(A3[:, 16384:17408], NCH),
        "vout": r3(A3[:, 17408:18432], NCH),
        "qout": r3(A3[:, 18432:19456], NCH),
        "o_all": r3(A2[:, 0:16384], NCH),
        "wod0": r3(A3[:, 0:1024], NCH),
        "wod1": r3(A3[:, 1024:2048], NCH),
    }
    for name, shape, dt in [
        ("rt1", [128, 4, 64], F32), ("rt2", [128, 4, 64], F32),
        ("rt3", [128, 4, 64], F32), ("rt4", [128, 4, 64], F32),
        ("ksq", [128, NCH, 128], BF16), ("kn", [128, 8], F32),
        ("knacc", [128, 8], F32), ("qnacc", [128, 8], F32),
        ("kng", [128, 8], F32),
        ("brow", [1, 16], F32), ("negb", [128, 1], F32),
        ("sccol", [128, 1], F32),
    ]:
        b[name] = p.sb(f"{tag}_{name}", shape, dt)
    b["masks"] = r3(A3[:, 16384:18432], 4)
    b["p_sb"] = r3(A3[:, 18432:20480], 4)
    return b


def emit_attn_fused(p, C, tag, layer, x_sb, pb, bufs, wq_d, wk_d, wv_d,
                    woT_d, cs_d, masks_d, sccol_d, kvloc, kvg, qbuf,
                    rr_sb, entry, lns):
    ap = attn_params(layer)
    NH = ap["nheads"]
    kout, vout, qout = bufs["kout"], bufs["vout"], bufs["qout"]
    q_sb, khb, vhb, o_all = (bufs["q_sb"], bufs["khb"], bufs["vhb"],
                             bufs["o_all"])
    bufs["rrow"] = lns["rtmp"]
    bufs["rb_sb"] = lns["nt"]
    bufs["ost"] = lns["nt2"].bitcast(BF16)[:, 0:512]
    wld, csld = f"{tag}_wld", f"{tag}_cs"
    kvm, rpd, kvev, knsm, kst = (f"{tag}_kvm", f"{tag}_rpd", f"{tag}_kvev",
                                 f"{tag}_knsm", f"{tag}_kst")
    for e in ENGINES:
        p.wait(e, entry[0], entry[1])
    wq = r3(bufs["o_all"][:, :, :].rearrange("p a b -> p (a b)")[:, 0:8192], NCH)
    wk = r3(bufs["o_all"][:, :, :].rearrange("p a b -> p (a b)")[:, 8192:16384], NCH)
    wv = r3(bufs["khb"][:, 0:8192], NCH)
    for c in range(NCH):
        p.dma("g", wq[:, c, :], wq_d[c * 128:(c + 1) * 128, :], wld)
        p.dma("g", wk[:, c, :], wk_d[c * 128:(c + 1) * 128, :], wld)
        p.dma("g", wv[:, c, :], wv_d[c * 128:(c + 1) * 128, :], wld)
    p.dma("sy", bufs["sccol"][:, :], sccol_d[:, :], f"{tag}_scl")
    wld_done = p.cnt[wld]
    for e in ("t", "v", "s"):
        p.wait(e, wld, wld_done)
        p.wait(e, f"{tag}_scl")
    p.op("v", lambda e: e.memset(bufs["knacc"][:, :], 0.0), knsm)
    p.op("v", lambda e: e.memset(bufs["qnacc"][:, :], 0.0), knsm)
    p.op("g", lambda e: e.memset(bufs["ksq"].rearrange("p a b -> p (a b)"), 0.0),
         knsm)
    p.wait("g", knsm)
    p.dma("g", kvloc[2048:2052, :].rearrange("a (b c) -> (a b) c", b=2),
          bufs["ksq"].rearrange("p a b -> p (a b)")[0:8, :], kst)
    ftzd_done = p.cnt[kst]

    for tt in range(4):
        xfl = bufs["xfl0"] if tt % 2 == 0 else bufs["xfl1"]
        nm = emit_ln(p, C, "Gln", x_sb, xfl, tt * 512, 512, pb[0:4], lns,
                     out_t0=0)
        for sub in range(4):
            s16 = tt * 4 + sub
            cst = bufs["csb"] if s16 % 2 == 0 else bufs["csb2"]
            cs3 = r3(cst, NCH)
            csp = csld + str(s16 % 2)
            if ap["rope"]:
                if s16 >= 2:
                    p.wait("sy", rpd)
                p.dma("sy", cst[:, :], cs_d[s16 * 128:(s16 + 1) * 128, :], csp)
            cs_c = p.cnt.get(csp, 0)
            p.wait("t", nm)
            p.wait("t", rpd)
            for g2 in range(2):
                for c in range(NCH):
                    p.op("t", lambda e, c_=c, g_=g2, s_=sub, xl=xfl: e.matmul(
                        pb[4 + g_][:, :], xl[:, c_, s_ * 128:(s_ + 1) * 128],
                        wk[:, c_, g_ * 512:(g_ + 1) * 512],
                        start=(c_ == 0), stop=(c_ == 7)), kvm)
            kvm_k = p.cnt[kvm]
            p.wait("t", kvev)
            for g2 in range(2):
                for c in range(NCH):
                    p.op("t", lambda e, c_=c, g_=g2, s_=sub, xl=xfl: e.matmul(
                        pb[6 + g_][:, :], xl[:, c_, s_ * 128:(s_ + 1) * 128],
                        wv[:, c_, g_ * 512:(g_ + 1) * 512],
                        start=(c_ == 0), stop=(c_ == 7)), kvm)
            kvm_v = p.cnt[kvm]
            p.wait("v", kvm, kvm_k)
            p.wait("v", kst)
            if ap["rope"]:
                p.wait("v", csp, cs_c)
                _rope6(p, bufs, cs3[:, 0:4, :], r3(pb[4][:, :], 4), kout[:, 0:4, :], rpd)
                _rope6(p, bufs, cs3[:, 4:8, :], r3(pb[5][:, :], 4), kout[:, 4:8, :], rpd)
            else:
                p.op("v", lambda e: e.tensor_copy(kout[:, 0:4, :], r3(pb[4][:, :], 4)), rpd)
                p.op("v", lambda e: e.tensor_copy(kout[:, 4:8, :], r3(pb[5][:, :], 4)), rpd)
            rpd_k = p.cnt[rpd]
            p.wait("s", kvm, kvm_v)
            p.wait("s", kst)
            p.op("s", lambda e: e.copy(vout[:, 0:4, :], r3(pb[6][:, :], 4)), kvev)
            p.op("s", lambda e: e.copy(vout[:, 4:8, :], r3(pb[7][:, :], 4)), kvev)
            kvev_c = p.cnt[kvev]
            p.wait("s", rpd, rpd_k)
            p.wait("s", kst, ftzd_done)
            p.op("s", lambda e: e.activation(
                bufs["ksq"].rearrange("p a b -> p (a b)"),
                kout.rearrange("p a b -> p (a b)"), AF.Square), knsm)
            knsq_c = p.cnt[knsm]
            p.wait("v", knsm, knsq_c)
            for h in range(NCH):
                p.op("v", lambda e, h_=h: e.reduce_sum(
                    bufs["kn"][:, h_:h_ + 1], bufs["ksq"][:, h_, :], axis=AX), knsm)
            p.wait("v", knsm)
            p.op("v", lambda e: e.tensor_tensor(
                bufs["knacc"][:, :], bufs["knacc"][:, :], bufs["kn"][:, :], op=ALU.max), knsm)
            trow = s16 * 128
            p.wait("g", rpd, rpd_k)
            p.dma("g", kvloc[trow:trow + 128, 0:1024],
                  kout.rearrange("p a b -> p (a b)"), kst)
            p.wait("g", kvev, kvev_c)
            p.dma("g", kvloc[trow:trow + 128, 1024:2048],
                  vout.rearrange("p a b -> p (a b)"), kst)
            p.wait("t", rpd, p.cnt[rpd])
            for g2 in range(2):
                for c in range(NCH):
                    p.op("t", lambda e, c_=c, g_=g2, s_=sub, xl=xfl: e.matmul(
                        pb[4 + g_][:, :], xl[:, c_, s_ * 128:(s_ + 1) * 128],
                        wq[:, c_, g_ * 512:(g_ + 1) * 512],
                        start=(c_ == 0), stop=(c_ == 7)), kvm)
            qm_c = p.cnt[kvm]
            p.wait("v", kvm, qm_c)
            p.wait("v", kst)
            if ap["rope"]:
                _rope6(p, bufs, cs3[:, 0:4, :], r3(pb[4][:, :], 4), qout[:, 0:4, :], rpd)
                _rope6(p, bufs, cs3[:, 4:8, :], r3(pb[5][:, :], 4), qout[:, 4:8, :], rpd)
            else:
                p.op("v", lambda e: e.tensor_copy(qout[:, 0:4, :], r3(pb[4][:, :], 4)), rpd)
                p.op("v", lambda e: e.tensor_copy(qout[:, 4:8, :], r3(pb[5][:, :], 4)), rpd)
            rpd_q = p.cnt[rpd]
            p.wait("s", rpd, rpd_q)
            p.op("s", lambda e: e.activation(
                bufs["ksq"].rearrange("p a b -> p (a b)"),
                qout.rearrange("p a b -> p (a b)"), AF.Square), knsm)
            qsq_c = p.cnt[knsm]
            p.wait("v", knsm, qsq_c)
            for h in range(NCH):
                p.op("v", lambda e, h_=h: e.reduce_sum(
                    bufs["kn"][:, h_:h_ + 1], bufs["ksq"][:, h_, :], axis=AX), knsm)
            p.wait("v", knsm)
            p.op("v", lambda e: e.tensor_tensor(
                bufs["qnacc"][:, :], bufs["qnacc"][:, :], bufs["kn"][:, :], op=ALU.max), knsm)
            p.wait("g", rpd, rpd_q)
            p.dma("g", qbuf[trow:trow + 128, :],
                  qout.rearrange("p a b -> p (a b)"), kst)
    kn_done = p.cnt[knsm]
    p.wait("g", knsm, kn_done)
    p.dma("g", kvloc[2048:2049, 0:1024].rearrange("q (p a) -> (q p) a", p=128),
          bufs["knacc"][:, :], kst)
    kst_done = p.cnt[kst]
    cc = f"{tag}_cc"
    p.wait("g", kst, kst_done)
    grp = [[0, 1, 2, 3], [4, 5, 6, 7]]
    p.op("g", lambda e: e.collective_compute(
        "AllGather", ALU.bypass, replica_groups=grp,
        ins=[kvloc.opt()], outs=[kvg.opt()]), cc)

    p.wait("g", cc, p.cnt[cc])
    for r in range(4):
        if r > 0:
            p.wait("g", knsm)
        p.dma("g", bufs["kng"][:, :] if r == 0 else bufs["kn"][:, :],
              kvg[r * KVROWS + 2048:r * KVROWS + 2049, 0:1024]
              .rearrange("q (p a) -> (q p) a", p=128), f"{tag}_knl")
        knl_c = p.cnt[f"{tag}_knl"]
        if r > 0:
            p.wait("v", f"{tag}_knl", knl_c)
            p.wait("v", knsm)
            p.op("v", lambda e: e.tensor_tensor(bufs["kng"][:, :], bufs["kng"][:, :], bufs["kn"][:, :], op=ALU.max), knsm)
    p.wait("t", knsm)
    p.wait("t", kvev)
    p.op("t", lambda e: e.transpose(pb[4][0:8, 0:128], bufs["kng"][:, 0:8], C.ident_f32[:, :]), knsm)
    p.op("t", lambda e: e.transpose(pb[5][0:8, 0:128], bufs["qnacc"][:, 0:8], C.ident_f32[:, :]), knsm)
    p.wait("v", knsm)
    p.op("v", lambda e: e.reduce_max(bufs["kn"][0:8, 0:1], pb[4][0:8, 0:128], axis=AX), knsm)
    p.op("v", lambda e: e.reduce_max(bufs["kn"][0:8, 1:2], pb[5][0:8, 0:128], axis=AX), knsm)
    p.wait("t", knsm)
    p.op("t", lambda e: e.transpose(pb[6][0:1, 0:8], bufs["kn"][0:8, 0:1], C.ident_f32[0:8, 0:8]), knsm)
    p.op("t", lambda e: e.transpose(pb[6][0:1, 8:16], bufs["kn"][0:8, 1:2], C.ident_f32[0:8, 0:8]), knsm)
    p.wait("v", knsm)
    p.op("v", lambda e: e.tensor_copy(bufs["brow"][0:1, 0:16], pb[6][0:1, 0:16]), knsm)
    p.wait("v", knsm)
    p.op("v", lambda e: e.tensor_tensor(bufs["brow"][0:1, 0:8], bufs["brow"][0:1, 0:8], bufs["brow"][0:1, 8:16], op=ALU.mult), knsm)
    p.wait("v", knsm)
    p.op("v", lambda e: e.tensor_scalar(bufs["brow"][0:1, 0:8], bufs["brow"][0:1, 0:8], ap["inv_scale"] ** 2, None, op0=ALU.mult), knsm)
    p.wait("s", knsm)
    p.op("s", lambda e: e.activation(bufs["brow"][0:1, 0:8], bufs["brow"][0:1, 0:8], AF.Sqrt), knsm)
    p.wait("v", knsm)
    p.op("v", lambda e: e.tensor_scalar(bufs["brow"][0:1, 0:8], bufs["brow"][0:1, 0:8], -1.0, None, op0=ALU.mult), knsm)
    brow_done = p.cnt[knsm]
    p.wait("g", kst, kst_done)
    for m in range(4):
        p.dma("g", bufs["masks"][:, m, :], masks_d[m, :, :], f"{tag}_mkl")
    mkl_done = p.cnt[f"{tag}_mkl"]
    for e in ("t", "v", "s"):
        p.wait(e, f"{tag}_mkl", mkl_done)
    qld = f"{tag}_qld"
    p.wait("sy", kst, kst_done)
    p.wait("sy", "Gln_st")
    p.wait("sy", kvm)
    for c in range(NCH):
        p.dma("sy", q_sb[:, c, :], qbuf[:, c * 128:(c + 1) * 128], qld,
              transpose=True)
    qld_done = p.cnt[qld]

    pbd = f"{tag}_pbd"
    cc_done = p.cnt[cc]
    p.begin_switch(rr_sb, 4)
    for rr in range(4):
        p.begin_body(idx=rr, prefix="aB")
        p.wait("sy", cc, cc_done)
        _attn_body(p, C, "aB", layer, rr, pb, bufs, kvg, qld, qld_done,
                   knsm, brow_done, kvm, p.cnt[kvm], pbd)
        p.end_body()
    p.end_switch(post_cnt={pbd: 5})
    pbd_done = p.cnt[pbd]

    oproj, resid, wol = f"{tag}_op", f"{tag}_res", f"{tag}_wol"
    for e in ENGINES:
        p.wait(e, pbd, pbd_done)
    wouse = [0, 0]
    for mc in range(NCH):
        wod = bufs["wod0"] if mc % 2 == 0 else bufs["wod1"]
        wolp = wol + str(mc % 2)
        p.wait("g", oproj, wouse[mc % 2])
        p.dma("g", wod[:, :, :], woT_d[:, mc * 128:(mc + 1) * 128].rearrange(
            "(c p) m -> p c m", p=128), wolp)
        wol_c = p.cnt[wolp]
        p.wait("t", wolp, wol_c)
        for g in range(4):
            bnk = pb[(mc * 4 + g) % 2]
            if mc * 4 + g >= 2:
                p.wait("t", resid, p.cnt.get(resid, 0) - 1)
            for hd in range(NCH):
                p.op("t", lambda e, hd_=hd, g_=g, bk=bnk, w=wod: e.matmul(
                    bk[:, :], w[:, hd_, :],
                    o_all[:, hd_, g_ * 512:(g_ + 1) * 512],
                    start=(hd_ == 0), stop=(hd_ == 7)), oproj)
            op_c = p.cnt[oproj]
            wouse[mc % 2] = op_c
            p.wait("v", oproj, op_c)
            p.op("v", lambda e, mc_=mc, g_=g, bk=bnk: e.tensor_tensor(
                x_sb[:, mc_, g_ * 512:(g_ + 1) * 512],
                x_sb[:, mc_, g_ * 512:(g_ + 1) * 512], bk[:, :], op=ALU.add), resid)
    return resid, p.cnt[resid]


def _attn_body(p, C, tag, layer, rr, pb, bufs, kvg, qld, qld_done,
               knsm, brow_done, kvm, kvm_done, pbd):
    ap = attn_params(layer)
    NH = ap["nheads"]
    khb, vhb, q_sb, o_all = (bufs["khb"], bufs["vhb"], bufs["q_sb"],
                             bufs["o_all"])
    gbs = group_bases(rr)
    nbs = f"{tag}_nb"
    sS, sE, sMK, sAV, sOE = (f"{tag}_S", f"{tag}_E", f"{tag}_MK",
                             f"{tag}_AV", f"{tag}_OE")
    khl, vhl, ost = f"{tag}_khl", f"{tag}_vhl", f"{tag}_ost"
    p.wait("t", qld, qld_done)
    p.wait("t", kvm, kvm_done)
    p.wait("sy", kvm, kvm_done)
    p.wait("v", kvm, kvm_done)
    nblk = 8 - rr
    exp_hist = [0]

    def gblock(b):
        r_ = b if b < 4 else 7 - b
        slot = 0 if b < 4 else 1
        return r_ * KVROWS + slot * 1024

    for h in range(NH):
        hc = h if layer == 0 else h // 2
        dsl = slice(0, 128) if layer == 0 else slice(64 * (h % 2), 64 * (h % 2) + 64)
        p.wait("t", knsm, brow_done)
        p.wait("t", nbs)
        p.wait("t", sE)
        p.wait("t", sOE)
        p.op("t", lambda e, hc_=hc: e.matmul(pb[7][:, 0:1], C.ones1_f32[:, :],
             bufs["brow"][0:1, hc_:hc_ + 1], start=True, stop=True), nbs)
        p.wait("v", nbs)
        p.wait("v", sE)
        p.op("v", lambda e: e.tensor_copy(bufs["negb"][:, 0:1], pb[7][:, 0:1]), nbs)
        negb_c = p.cnt[nbs]
        p.wait("s", nbs, negb_c)
        if layer == 0 or h % 2 == 0:
            p.wait("sy", sS)
            p.wait("sy", sAV)
            if h == 0:
                p.wait("sy", qld, qld_done)
            for b in range(nblk):
                base = gblock(b)
                p.dma("sy", khb[:, b * 1024:(b + 1) * 1024],
                      kvg[base:base + 1024, hc * 128:(hc + 1) * 128], khl,
                      transpose=True)
                p.dma("sy", vhb[:, b * 8:(b + 1) * 8, :],
                      kvg[base:base + 1024, 1024 + hc * 128:1024 + (hc + 1) * 128]
                      .rearrange("(kt p) d -> p kt d", p=128), vhl)
            khl_c = p.cnt[khl]
            vhl_c = p.cnt[vhl]
            p.wait("t", khl, khl_c)
            p.wait("t", vhl, vhl_c)
        for g in range(4):
            nreg = gbs[g] // 512 + 1
            for r in range(nreg):
                kg = 512 * r
                diag = (r == nreg - 1)
                p.wait("t", sE, exp_hist[-1])
                for j in range(4):
                    p.op("t", lambda e, j_=j, kg_=kg, d_=dsl, g_=g, hc_=hc: e.matmul(
                        pb[3 + j_][:, :],
                        khb[d_, kg_ + j_ * 128:kg_ + (j_ + 1) * 128],
                        q_sb[d_, hc_, g_ * 512:(g_ + 1) * 512],
                        start=True, stop=True), sS)
                sS_c = p.cnt[sS]
                p.wait("s", sS, sS_c)
                p.wait("s", sAV)
                p.wait("s", sMK)
                for j in range(4):
                    p.op("s", lambda e, j_=j: e.activation(
                        bufs["p_sb"][:, j_, :], pb[3 + j_][:, :], AF.Exp,
                        bias=bufs["negb"][:, 0:1], scale=bufs["sccol"][:, 0:1]), sE)
                exp_hist.append(p.cnt[sE])
                if diag:
                    p.wait("v", sE)
                    for j in range(4):
                        p.op("v", lambda e, j_=j: e.tensor_tensor(
                            bufs["p_sb"][:, j_, :], bufs["p_sb"][:, j_, :],
                            bufs["masks"][:, j_, :], op=ALU.mult), sMK)
                p.wait("t", sE)
                if diag:
                    p.wait("t", sMK)
                if r == 0:
                    p.wait("t", sOE)
                for j in range(4):
                    p.op("t", lambda e, j_=j, r_=r, nr=nreg: e.matmul(
                        pb[1][0:1, :], C.ones_col_bf[:, :], bufs["p_sb"][:, j_, :],
                        start=(r_ == 0 and j_ == 0),
                        stop=(r_ == nr - 1 and j_ == 3)), sAV)
                vsl = slice(0, 128) if layer == 0 else slice(64 * (h % 2), 64 * (h % 2) + 64)
                osl = slice(0, 128) if layer == 0 else slice(0, 64)
                for j in range(4):
                    p.op("t", lambda e, j_=j, kg_=kg, r_=r, nr=nreg, v_=vsl, o_=osl: e.matmul(
                        pb[0][o_, :], vhb[:, kg_ // 128 + j_, v_],
                        bufs["p_sb"][:, j_, :],
                        start=(r_ == 0 and j_ == 0), stop=(r_ == nr - 1 and j_ == 3)), sAV)
            sAV_c = p.cnt[sAV]
            p.wait("v", sAV, sAV_c)
            p.op("v", lambda e: e.reciprocal(bufs["rrow"][0:1, :], pb[1][0:1, :]), sOE)
            rr_c = p.cnt[sOE]
            p.wait("t", sOE, rr_c)
            p.op("t", lambda e: e.matmul(pb[2][:, :], C.ones1_f32[:, :],
                 bufs["rrow"][0:1, :], start=True, stop=True), sOE)
            p.wait("v", sOE)
            p.op("v", lambda e: e.tensor_copy(bufs["rb_sb"][:, :], pb[2][:, :]), sOE)
            p.wait("v", sOE)
            osl = slice(0, 128) if layer == 0 else slice(0, 64)
            if layer == 0:
                p.op("v", lambda e, g_=g, hc_=hc: e.tensor_tensor(
                    o_all[:, hc_, g_ * 512:(g_ + 1) * 512],
                    pb[0][:, :], bufs["rb_sb"][:, :], op=ALU.mult), sOE)
            else:
                p.wait("v", ost)
                p.op("v", lambda e, o_=osl: e.tensor_tensor(
                    bufs["ost"][o_, :], pb[0][o_, :], bufs["rb_sb"][o_, :],
                    op=ALU.mult), sOE)
                p.wait("g", sOE)
                p.dma("g", o_all[64 * (h % 2):64 * (h % 2) + 64, hc,
                                 g * 512:(g + 1) * 512],
                      bufs["ost"][0:64, :], ost)
    p.wait("v", sOE)
    if layer == 1:
        p.wait("g", ost)
        p.wait("v", ost)
    p.op("v", lambda e: e.engine_nop(), pbd)
    p.wait("t", sOE)
    p.op("t", lambda e: e.nop(), pbd)
    p.wait("s", sE)
    p.op("s", lambda e: e.nop(), pbd)
    p.wait("g", ost if layer == 1 else sOE, None if layer == 1 else 0)
    p.op("g", lambda e: e.engine_nop(), pbd)
    p.op("sy", lambda e: e.nop(), pbd)


def build_fused():
    p = Prog("fused")
    nc = p.nc
    x_d = nc.dram_tensor("x_fm", [D, TPC], F32, kind="ExternalInput").ap()
    rr_d = nc.dram_tensor("rrsel", [1, 1], U32, kind="ExternalInput").ap()
    lflag_d = nc.dram_tensor("lflag", [16, 1], F32, kind="ExternalInput").ap()
    cs0_d = nc.dram_tensor("cs0", [TPC, D], BF16, kind="ExternalInput").ap()
    gwd = nc.dram_tensor("gate_w", [D, 32], F32, kind="ExternalInput").ap()
    sgw_d = nc.dram_tensor("sgw", [D, 2048], F32, kind="ExternalInput").ap()
    suw_d = nc.dram_tensor("suw", [D, 2048], F32, kind="ExternalInput").ap()
    sdw_d = nc.dram_tensor("sdw", [2048, D], F32, kind="ExternalInput").ap()
    rgw_d = nc.dram_tensor("rgw", [32 * 1024, 2048], F32, kind="ExternalInput").ap()
    ruw_d = nc.dram_tensor("ruw", [32 * 1024, 2048], F32, kind="ExternalInput").ap()
    rdw_d = nc.dram_tensor("rdw", [32 * 2048, 1024], F32, kind="ExternalInput").ap()
    wq0_d = nc.dram_tensor("wq0", [D, D], F32, kind="ExternalInput").ap()
    wk0_d = nc.dram_tensor("wk0", [D, D], F32, kind="ExternalInput").ap()
    wv0_d = nc.dram_tensor("wv0", [D, D], F32, kind="ExternalInput").ap()
    wo0_d = nc.dram_tensor("wo0", [D, D], F32, kind="ExternalInput").ap()
    wq1_d = nc.dram_tensor("wq1", [D, D], F32, kind="ExternalInput").ap()
    wk1_d = nc.dram_tensor("wk1", [D, D], F32, kind="ExternalInput").ap()
    wv1_d = nc.dram_tensor("wv1", [D, D], F32, kind="ExternalInput").ap()
    wo1_d = nc.dram_tensor("wo1", [D, D], F32, kind="ExternalInput").ap()
    w1t_d = nc.dram_tensor("w1t", [D, 4096], F32, kind="ExternalInput").ap()
    w2t_d = nc.dram_tensor("w2t", [4096, D], F32, kind="ExternalInput").ap()
    b1r_d = nc.dram_tensor("b1row", [1, 4096], F32, kind="ExternalInput").ap()
    b2r_d = nc.dram_tensor("b2row", [1, D], F32, kind="ExternalInput").ap()
    sc_d = nc.dram_tensor("scal", [1, 1], F32, kind="ExternalInput").ap()
    sc4_d = nc.dram_tensor("scal4", [4, 1], F32, kind="ExternalInput").ap()
    masks_d = nc.dram_tensor("masksT", [4, 128, 512], BF16, kind="ExternalInput").ap()
    sc0_d = nc.dram_tensor("sccol0", [128, 1], F32, kind="ExternalInput").ap()
    sc1_d = nc.dram_tensor("sccol1", [128, 1], F32, kind="ExternalInput").ap()
    y_d = nc.dram_tensor("y_fm", [D, TPC], F32, kind="ExternalOutput").ap()
    xs_sorted = nc.dram_tensor("xs_sorted", [NTILE * 128, D], BF16).ap()
    dout = nc.dram_tensor("dout", [NTILE * 128, D], BF16).ap()
    qbuf = nc.dram_tensor("qbuf", [TPC, D], BF16).ap()
    kvloc0 = nc.dram_tensor("kvloc0", [KVROWS, 2048], BF16).ap()
    kvg0 = nc.dram_tensor("kvg0", [4 * KVROWS, 2048], BF16).ap()
    kvloc1 = nc.dram_tensor("kvloc1", [KVROWS, 2048], BF16).ap()
    kvg1 = nc.dram_tensor("kvg1", [4 * KVROWS, 2048], BF16).ap()
    lr_raw = nc.dram_tensor("lr_raw", [16, D], F32).ap()
    lr_in = nc.dram_tensor("lr_in", [16, D], F32).ap()
    lr_out0 = nc.dram_tensor("lr_out0", [16, D], F32).ap()
    lr_out1 = nc.dram_tensor("lr_out1", [16, D], F32).ap()

    C = Consts(p)
    x_sb = p.sb("x_sb", [128, NCH, TPC], F32)
    pb = [p.psum(f"pb{i}", [128, 512], F32) for i in range(8)]
    A1 = p.sb("A1", [128, 16384], BF16)
    A2 = p.sb("A2", [128, 16384], BF16)
    A3 = p.sb("A3", [128, 20480], BF16)
    rr_sb = p.sb("rr_sb", [1, 1], U32)
    lflag = p.sb("lflag_sb", [16, 1], F32)
    lns = ln_scratch(p, "G_lns")

    for c in range(NCH):
        p.dma("sy", x_sb[:, c, :], x_d[c * 128:(c + 1) * 128, :], "xld")
    p.dma("sy", rr_sb[:, :], rr_d[:, :], "xld")
    p.dma("sy", lflag[:, :], lflag_d[:, :], "xld")
    xld_done = p.cnt["xld"]
    for e in ENGINES:
        p.wait(e, "xld", xld_done)
        p.wait(e, "c_ld")

    def mk_moe_bufs():
        b = {
            "ar": A3,
            "xln": r3(A1[:, 0:16384], NCH),
            "xs_all": r3(A2[:, 0:16384], 16),
            "lns": lns,
        }
        for name, shape, dt in [
            ("gw", [128, NCH, 32], BF16), ("oh_sb", [128, 16, 32], F32),
            ("rank_sb", [128, 16, 32], F32), ("cs_sb", [1, 16, 32], F32),
            ("score", [128, 16], F32), ("maxv", [128, 1], F32),
            ("eid", [128, 1], F32), ("eq", [128, 32], F32),
            ("sel", [128, 32], F32), ("imb", [128, 32], F32),
            ("cnt_i", [32, 1], I32), ("pad_f", [32, 1], F32),
            ("cmp", [32, NTILE], F32), ("erow", [1, NTILE], F32),
            ("esml", [1, NTILE], F32), ("eoff", [1, 2], F32),
            ("base_row", [1, 32], F32), ("t2", [128, 32], F32),
            ("t2r", [1, 32], F32), ("posu", [128, 16], U32),
            ("posf", [128, 16], F32), ("idxg", [128, 8], U32),
            ("idxd", [128, 16], U32), ("idgf", [128, 8], F32),
            ("iddf", [128, 16], F32), ("kcoff", [128, 8], F32),
            ("ichoff", [128, 16], F32), ("xsf0", [128, NCH, 128], BF16),
            ("xsf1", [128, NCH, 128], BF16), ("dsb", [128, 1024], BF16),
            ("rtc", [128, 128], F32),
        ]:
            b[name] = p.sb(f"M_{name}", shape, dt)
        ar = A3
        b["wg0"] = ar[:, 0:2048];      b["wg1"] = ar[:, 2048:4096]
        b["wu0"] = ar[:, 4096:6144];   b["wu1"] = ar[:, 6144:8192]
        b["wd0"] = ar[:, 8192:9216];   b["wd1"] = ar[:, 9216:10240]
        b["gs"] = r3(ar[:, 10240:12288], 4)
        b["us"] = r3(ar[:, 12288:14336], 4)
        b["hh"] = r3(ar[:, 14336:16384], 4)
        b["hT"] = r3(ar[:, 16384:18432], 16)
        b["xs_t0"] = ar[:, 18432:19456]
        b["xs_t1"] = ar[:, 19456:20480]
        b["sgwb0"] = r3(ar[:, 0:1024], NCH);   b["suwb0"] = r3(ar[:, 1024:2048], NCH)
        b["sgwb1"] = r3(ar[:, 2048:3072], NCH); b["suwb1"] = r3(ar[:, 3072:4096], NCH)
        b["sdwb0"] = r3(ar[:, 4096:6144], 16)
        b["sdwb1"] = r3(ar[:, 6144:8192], 16)
        b["h_sh"] = r3(ar[:, 10240:18432], 16)
        b["gsh"] = ar[:, 18432:18944]
        b["ush"] = ar[:, 18944:19456]
        return b

    moe_b = mk_moe_bufs()
    attn_b = attn_fused_bufs(p, "AB", A1, A2, A3)

    def mk_ffn1_bufs():
        b = {
            "h_sb": r3(A2[:, 0:16384], 32),
            "lns": lns,
        }
        for name, shape, dt in [
            ("ltst", [16, 12], F32), ("ltmv", [16, 2], F32),
            ("ltrs", [16, 1], F32), ("ltnm", [16, 1], F32),
            ("loraT", [128, NCH, 16], BF16), ("sc_sb", [1, 1], F32),
            ("sc4_sb", [4, 1], F32),
        ]:
            b[name] = p.sb(f"F_{name}", shape, dt)
        b["t_f"] = A1[0:4, 13312:13824]
        b["u_all"] = A1[0:16, 13824:14336]
        b["b2row"] = A1[0:1, 14336:15360]
        b["w1a"] = r3(A3[:, 0:1024], NCH)
        b["w1b"] = r3(A3[:, 1024:2048], NCH)
        b["w2a"] = r3(A3[:, 2048:6144], 32)
        b["w2b"] = r3(A3[:, 6144:10240], 32)
        b["e1s"] = A3[0:1, 10240:14336]
        b["b1row"] = A3[0:1, 14336:18432]
        b["t_row"] = A3[0:1, 18432:20480]
        b["u_f"] = r3(A1[:, 4096:6144], 4)[0:4, :, :]
        b["e2b"] = r3(A1[:, 8192:10240], NCH)[0:16, :, 0:128]
        b["lt_sb"] = A1.bitcast(F32)[0:16, 5120:6144]
        b["ltln"] = A1[0:16, 12288:13312]
        return b

    ffn1_b = mk_ffn1_bufs()
    p.dma("sy", ffn1_b["sc_sb"][:, :], sc_d[:, :], "xld2")
    p.dma("sy", ffn1_b["sc4_sb"][:, :], sc4_d[:, :], "xld2")
    for e in ("v", "s", "t", "g"):
        p.wait(e, "xld2")

    def ffn1_prolog(tagf="fp"):
        p.dma("g", ffn1_b["b1row"][:, :], b1r_d[:, :], tagf + "wl")
        p.dma("g", ffn1_b["b2row"][:, :], b2r_d[:, :], tagf + "wl")
        for e in ("v", "s", "t", "g"):
            p.wait(e, tagf + "wl")

    def barrier(sem, cnt):
        for e in ENGINES:
            p.wait(e, sem, cnt)

    dbg_outs = []

    def dbg_dump(label, sem, cnt):
        if not DBG_FUSED:
            return
        yd = nc.dram_tensor(f"dbg_{label}", [D, TPC], F32,
                            kind="ExternalOutput").ap()
        dbg_outs.append(label)
        p.wait("sy", sem, cnt)
        for c in range(NCH):
            p.dma("sy", yd[c * 128:(c + 1) * 128, :], x_sb[:, c, :],
                  "dbgst")
        p.wait("sy", "dbgst")

    def lora_exchange(lr_out, entry):
        tagl = "LX"
        barrier(*entry)
        for c in range(NCH):
            p.dma("g", lr_raw[:, c * 128:(c + 1) * 128]
                  .rearrange("t m -> m t"), x_sb[:, c, 1:17], tagl + "w",
                  allow_slow_non_contiguous=True)
        p.wait("sy", tagl + "w")
        p.dma("sy", ffn1_b["lt_sb"][:, :], lr_raw[:, :], tagl + "r")
        p.wait("v", tagl + "r")
        p.op("v", lambda e: e.tensor_scalar(
            ffn1_b["lt_sb"][:, :], ffn1_b["lt_sb"][:, :], lflag[:, 0:1],
            None, op0=ALU.mult), tagl + "m")
        p.wait("g", tagl + "m")
        p.dma("g", lr_in[:, :], ffn1_b["lt_sb"][:, :], tagl + "w2")
        p.wait("g", tagl + "w2")
        grp = [[0, 1, 2, 3], [4, 5, 6, 7]]
        p.op("g", lambda e, lo=lr_out: e.collective_compute(
            "AllReduce", ALU.add, replica_groups=grp,
            ins=[lr_in.opt()], outs=[lo.opt()]), tagl + "cc")
        p.wait("sy", tagl + "cc")
        p.dma("sy", ffn1_b["lt_sb"][:, :], lr_out[:, :], tagl + "r2")
        return tagl + "r2", p.cnt[tagl + "r2"]

    entry = ("xld", xld_done)
    res1 = emit_moe(p, C, "m", x_sb, gwd, sgw_d, suw_d, sdw_d,
                    rgw_d, ruw_d, rdw_d, xs_sorted, dout, pb, moe_b)
    entry = (res1, p.cnt[res1])
    dbg_dump("p0", *entry)
    res2, res2c = emit_attn_fused(p, C, "a", 0, x_sb, pb, attn_b,
                                  wq0_d, wk0_d, wv0_d, wo0_d, cs0_d,
                                  masks_d, sc0_d, kvloc0, kvg0, qbuf,
                                  rr_sb[0:1, 0:1], entry, lns)
    entry = (res2, res2c)
    dbg_dump("p1", *entry)
    barrier(*entry)
    res3 = emit_moe(p, C, "m", x_sb, gwd, sgw_d, suw_d, sdw_d,
                    rgw_d, ruw_d, rdw_d, xs_sorted, dout, pb, moe_b)
    entry = (res3, p.cnt[res3])
    dbg_dump("p2", *entry)
    lt_dep = lora_exchange(lr_out0, entry)
    barrier(*lt_dep)
    ffn1_prolog()
    res4 = emit_ffn1(p, C, "f", x_sb, FXlnView(A1), pb, w1t_d, w2t_d, ffn1_b)
    entry = (res4, p.cnt[res4])
    dbg_dump("p3", *entry)
    res5, res5c = emit_attn_fused(p, C, "a", 1, x_sb, pb, attn_b,
                                  wq1_d, wk1_d, wv1_d, wo1_d, cs0_d,
                                  masks_d, sc1_d, kvloc1, kvg1, qbuf,
                                  rr_sb[0:1, 0:1], entry, lns)
    entry = (res5, res5c)
    dbg_dump("p4", *entry)
    lt_dep = lora_exchange(lr_out1, entry)
    barrier(*lt_dep)
    ffn1_prolog()
    res6 = emit_ffn1(p, C, "f", x_sb, FXlnView(A1), pb, w1t_d, w2t_d, ffn1_b)

    p.wait("sy", res6)
    for c in range(NCH):
        p.dma("sy", y_d[c * 128:(c + 1) * 128, :], x_sb[:, c, :], "yst")
    p.wait("sy", "yst")
    return p


DBG_FUSED = False


def build_fused_dbg():
    global DBG_FUSED
    DBG_FUSED = True
    try:
        return build_fused()
    finally:
        DBG_FUSED = False


class FXlnView:

    def __init__(self, A1):
        self.x = r3(A1[:, 0:4096], NCH)

    def __getitem__(self, idx):
        _, c, ts = idx
        off = ts.start % 512
        return self.x[:, c, off:off + (ts.stop - ts.start)]


def fused_inputs(core, x_fm, inputs):
    cd = make_consts()
    b, b0, b1 = core_blocks(core)
    rr = core % 4
    cs_full = rope_table()
    own = np.concatenate([cs_full[b0 * 1024:(b0 + 1) * 1024],
                          cs_full[b1 * 1024:(b1 + 1) * 1024]], 0)
    lflag = np.full((16, 1), 1.0 if rr == 0 else 0.0, np.float32)
    wq0 = np.asarray(inputs["L0_attn_wq"], np.float32).transpose(1, 0, 2).reshape(D, D)
    wk0 = np.asarray(inputs["L0_attn_wk"], np.float32).transpose(1, 0, 2).reshape(D, D)
    wv0 = np.asarray(inputs["L0_attn_wv"], np.float32).transpose(1, 0, 2).reshape(D, D)
    wo0 = np.asarray(inputs["L0_attn_wo"], np.float32).reshape(D, D)
    wq1 = np.asarray(inputs["L1_attn_wq"], np.float32).transpose(2, 0, 1).reshape(D, D)
    wk1 = np.asarray(inputs["L1_attn_wk"], np.float32).transpose(2, 0, 1).reshape(D, D)
    wv1 = np.asarray(inputs["L1_attn_wv"], np.float32).transpose(2, 0, 1).reshape(D, D)
    wo1 = np.ascontiguousarray(np.asarray(inputs["L1_attn_wo"], np.float32).T)
    scale = np.float32(np.asarray(inputs["L1_ffn_scale"]))
    return dict(
        x_fm=np.ascontiguousarray(x_fm),
        rrsel=np.array([[rr]], np.uint32),
        lflag=lflag,
        cs0=np.ascontiguousarray(own),
        gate_w=np.ascontiguousarray(np.asarray(inputs["L0_ffn_gate_w"], np.float32)),
        sgw=np.ascontiguousarray(np.asarray(inputs["L0_ffn_shared_gate_w"], np.float32)),
        suw=np.ascontiguousarray(np.asarray(inputs["L0_ffn_shared_up_w"], np.float32)),
        sdw=np.ascontiguousarray(np.asarray(inputs["L0_ffn_shared_down_w"], np.float32)),
        rgw=np.ascontiguousarray(np.asarray(inputs["L0_ffn_routed_gate_w"], np.float32).reshape(32 * 1024, 2048)),
        ruw=np.ascontiguousarray(np.asarray(inputs["L0_ffn_routed_up_w"], np.float32).reshape(32 * 1024, 2048)),
        rdw=np.ascontiguousarray(np.asarray(inputs["L0_ffn_routed_down_w"], np.float32).reshape(32 * 2048, 1024)),
        wq0=np.ascontiguousarray(wq0), wk0=np.ascontiguousarray(wk0),
        wv0=np.ascontiguousarray(wv0), wo0=np.ascontiguousarray(wo0),
        wq1=np.ascontiguousarray(wq1), wk1=np.ascontiguousarray(wk1),
        wv1=np.ascontiguousarray(wv1), wo1=np.ascontiguousarray(wo1),
        w1t=np.ascontiguousarray(np.asarray(inputs["L1_ffn_W1"], np.float32).T),
        w2t=np.ascontiguousarray(np.asarray(inputs["L1_ffn_W2"], np.float32).T),
        b1row=np.ascontiguousarray(np.asarray(inputs["L1_ffn_b1"], np.float32).reshape(1, 4096)),
        b2row=np.ascontiguousarray(np.asarray(inputs["L1_ffn_b2"], np.float32).reshape(1, D)),
        scal=np.full((1, 1), scale, np.float32),
        scal4=np.full((4, 1), scale, np.float32),
        masksT=attn_masks(),
        sccol0=np.full((128, 1), 1.0 / 128.0, np.float32),
        sccol1=np.full((128, 1), 1.0 / 8.0, np.float32),
        **cd,
    )


def kernel_fused(**inputs):
    x = np.asarray(inputs["L0_attn_x"], np.float32)
    shards = shard_x(x)
    nc = _get_prog("fused", build_fused)
    im = [fused_inputs(c, shards[c], inputs) for c in range(NCORE)]
    res = run_bass_kernel_spmd(nc, im, core_ids=list(range(NCORE)))
    return unshard_x([res.results[i]["y_fm"] for i in range(NCORE)]).astype(np.float32)


_PROGS = {}


def _get_prog(name, builder):
    if name not in _PROGS:
        _PROGS[name] = builder().build()
    return _PROGS[name]


def kernel(**inputs):
    return kernel_fused(**inputs)


def kernel_v0(**inputs):
    x = np.asarray(inputs["L0_attn_x"], np.float32)
    gw = np.asarray(inputs["L0_ffn_gate_w"], np.float32)
    sgw = np.asarray(inputs["L0_ffn_shared_gate_w"], np.float32)
    suw = np.asarray(inputs["L0_ffn_shared_up_w"], np.float32)
    sdw = np.asarray(inputs["L0_ffn_shared_down_w"], np.float32)
    rgw = np.asarray(inputs["L0_ffn_routed_gate_w"], np.float32)
    ruw = np.asarray(inputs["L0_ffn_routed_up_w"], np.float32)
    rdw = np.asarray(inputs["L0_ffn_routed_down_w"], np.float32)
    wq0 = np.asarray(inputs["L0_attn_wq"], np.float32)
    wk0 = np.asarray(inputs["L0_attn_wk"], np.float32)
    wv0 = np.asarray(inputs["L0_attn_wv"], np.float32)
    wo0 = np.asarray(inputs["L0_attn_wo"], np.float32)
    wq1 = np.asarray(inputs["L1_attn_wq"], np.float32)
    wk1 = np.asarray(inputs["L1_attn_wk"], np.float32)
    wv1 = np.asarray(inputs["L1_attn_wv"], np.float32)
    wo1 = np.asarray(inputs["L1_attn_wo"], np.float32)
    W1 = np.asarray(inputs["L1_ffn_W1"], np.float32)
    b1 = np.asarray(inputs["L1_ffn_b1"], np.float32)
    W2 = np.asarray(inputs["L1_ffn_W2"], np.float32)
    b2 = np.asarray(inputs["L1_ffn_b2"], np.float32)
    scale = np.float32(np.asarray(inputs["L1_ffn_scale"]))

    shards = shard_x(x)

    def apply_moe(sh):
        nc = _get_prog("moe", build_moe)
        im = [moe_inputs(c, sh[c], gw, sgw, suw, sdw, rgw, ruw, rdw)
              for c in range(NCORE)]
        res = run_bass_kernel_spmd(nc, im, core_ids=list(range(NCORE)))
        return [res.results[i]["y_fm"] for i in range(NCORE)]

    def apply_attn(layer, sh):
        xfull = unshard_x(sh)
        wq, wk, wv, wo = ((wq0, wk0, wv0, wo0) if layer == 0
                          else (wq1, wk1, wv1, wo1))
        outs = [None] * NCORE
        for rr in range(4):
            nc = _get_prog(f"attn{layer}_{rr}",
                           lambda layer=layer, rr=rr: build_attn(layer, rr))
            cores = [rr, rr + 4]
            im = [attn_inputs(c, layer, sh[c], xfull, wq, wk, wv, wo)
                  for c in cores]
            res = run_bass_kernel_spmd(nc, im, core_ids=[0, 1])
            outs[cores[0]] = res.results[0]["y_fm"]
            outs[cores[1]] = res.results[1]["y_fm"]
        return outs

    def apply_ffn1(sh):
        xfull = unshard_x(sh)
        nc = _get_prog("ffn1", build_ffn1)
        im = [ffn1_inputs(c, sh[c], W1, b1, W2, b2, scale, xfull)
              for c in range(NCORE)]
        res = run_bass_kernel_spmd(nc, im, core_ids=list(range(NCORE)))
        return [res.results[i]["y_fm"] for i in range(NCORE)]

    shards = apply_moe(shards)
    shards = apply_attn(0, shards)
    shards = apply_moe(shards)
    shards = apply_ffn1(shards)
    shards = apply_attn(1, shards)
    shards = apply_ffn1(shards)
    return unshard_x(shards).astype(np.float32)


# version 03 (final)
# speedup vs baseline: 2.3163x; 1.0641x over previous
import numpy as np
import ml_dtypes
from contextlib import ExitStack

import concourse.bass as bass
import concourse.mybir as mybir
from concourse.bass import IndirectOffsetOnAxis
from concourse.bass_utils import run_bass_kernel_spmd

F32 = mybir.dt.float32
BF16 = mybir.dt.bfloat16
U32 = mybir.dt.uint32
I32 = mybir.dt.int32
AF = mybir.ActivationFunctionType
ALU = mybir.AluOpType
AX = mybir.AxisListType.X

BF16NP = ml_dtypes.bfloat16

B, S, D = 2, 8192, 1024
NCORE = 8
TPC = 2048
NCH = 8


def core_blocks(c):
    r = c % 4
    return c // 4, r, 7 - r


ENGINES = ("sy", "g", "t", "v", "s")


class Prog:
    def __init__(self, name="prog"):
        self.nc = bass.Bass("TRN2", target_bir_lowering=False, debug=False)
        self.ctx = ExitStack()
        self.cmds = {e: [] for e in ENGINES}
        self.sems = {}
        self.cnt = {}

    def sem(self, name):
        if name not in self.sems:
            self.sems[name] = self.ctx.enter_context(self.nc.semaphore(name))
            self.cnt[name] = 0
        return self.sems[name]

    def sb(self, name, shape, dt):
        return self.ctx.enter_context(self.nc.sbuf_tensor(name, shape, dt))

    def psum(self, name, shape, dt):
        return self.ctx.enter_context(self.nc.psum_tensor(name, shape, dt))

    def emit(self, eng, fn):
        self.cmds[eng].append(fn)

    def wait(self, eng, sname, val=None):
        if val is None:
            val = self.cnt.get(sname, 0)
        if val <= 0:
            return
        h = self.sem(sname)
        self.emit(eng, lambda e: e.wait_ge(h, val))

    def op(self, eng, fn, sname=None, k=1):
        if sname is None:
            self.emit(eng, lambda e: fn(e))
            return None
        h = self.sem(sname)
        self.cnt[sname] = self.cnt.get(sname, 0) + k
        self.emit(eng, lambda e: fn(e).then_inc(h, k))
        return self.cnt[sname]

    def dma(self, eng, out, in_, sname, **kw):
        return self.op(eng, lambda e: e.dma_start(out=out, in_=in_, **kw),
                       sname, 16)

    def sop(self, eng, fn):
        ordn = eng + "_ord"
        self.wait(eng, ordn)
        return self.op(eng, fn, ordn)

    def begin_switch(self, sel_ap, n=4):
        if not hasattr(self, "segments"):
            self.segments = []
        self.segments.append(("linear", self.cmds))
        self._switch_bodies = []
        self._switch_sel = sel_ap
        self._switch_n = n

    def begin_body(self, idx=None, prefix=None):
        self.cmds = {e: [] for e in ENGINES}
        self._switch_bodies.append(self.cmds)
        self._body_cnt0 = dict(self.cnt)
        self._body_idx, self._body_prefix = idx, prefix
        if idx is not None and prefix is not None:
            saved = getattr(self, "_body_saved", {}).get(idx, {})
            self.cnt.update({k: v for k, v in saved.items()
                             if k.startswith(prefix)})

    def end_body(self):
        if self._body_idx is not None and self._body_prefix is not None:
            if not hasattr(self, "_body_saved"):
                self._body_saved = {}
            self._body_saved[self._body_idx] = {
                k: v for k, v in self.cnt.items()
                if k.startswith(self._body_prefix)}
        self.cnt = dict(self._body_cnt0)

    def end_switch(self, post_cnt=None):
        self.segments.append(("switch", self._switch_bodies,
                              self._switch_sel, self._switch_n))
        self.cmds = {e: [] for e in ENGINES}
        if post_cnt:
            for k, v in post_cnt.items():
                self.cnt[k] = self.cnt.get(k, 0) + v

    def build(self):
        segs = list(getattr(self, "segments", []))
        segs.append(("linear", self.cmds))
        nc = self.nc
        block = self.ctx.enter_context(nc.Block())
        handles = {"sy": nc.sync, "g": nc.gpsimd, "t": nc.tensor,
                   "v": nc.vector, "s": nc.scalar}

        @block.sync
        def _(_e):
            for seg in segs:
                if seg[0] == "linear":
                    for e in ENGINES:
                        for f in seg[1][e]:
                            f(handles[e])
                else:
                    _, bodies, sel_ap, n = seg
                    idx = [handles[e].value_load(sel_ap) for e in ENGINES]
                    engs = [handles[e] for e in ENGINES]
                    for i in nc.Switch(engines=engs, index=idx, n=n):
                        for e in ENGINES:
                            for f in bodies[i][e]:
                                f(handles[e])

        self.ctx.close()
        return nc


def make_consts():
    c = {}
    c["ident_bf"] = np.eye(128, dtype=BF16NP)
    c["ident_f32"] = np.eye(128, dtype=np.float32)
    c["ones_row_bf"] = np.ones((1, 512), dtype=BF16NP)
    c["ones_col_bf"] = np.ones((128, 1), dtype=BF16NP)
    c["ones_col_f32"] = np.ones((128, 1), dtype=np.float32)
    c["ones1_f32"] = np.ones((1, 128), dtype=np.float32)
    c["tri128"] = np.triu(np.ones((128, 128), np.float32), 1)
    c["tri32"] = np.triu(np.ones((32, 32), np.float32), 1)
    c["iota32_f"] = np.broadcast_to(np.arange(32, dtype=np.float32), (128, 32)).copy()
    c["iota128_f"] = np.arange(128, dtype=np.float32).reshape(128, 1)
    c["tbases"] = np.broadcast_to((np.arange(47, dtype=np.float32) * 128)[None, :],
                                  (32, 47)).copy()
    c["tbrow"] = (np.arange(47, dtype=np.float32) * 128)[None, :].copy()
    return c


CONST_SPECS = [
    ("ident_bf", [128, 128], BF16), ("ident_f32", [128, 128], F32), ("ones_row_bf", [1, 512], BF16),
    ("ones_col_bf", [128, 1], BF16), ("ones_col_f32", [128, 1], F32),
    ("ones1_f32", [1, 128], F32), ("tri128", [128, 128], F32),
    ("tri32", [32, 32], F32), ("iota32_f", [128, 32], F32),
    ("iota128_f", [128, 1], F32), ("tbases", [32, 47], F32),
    ("tbrow", [1, 47], F32),
]


class Consts:
    def __init__(self, p):
        nc = p.nc
        for name, shape, dt in CONST_SPECS:
            d = nc.dram_tensor(name, shape, dt, kind="ExternalInput").ap()
            t = p.sb("c_" + name, shape, dt)
            setattr(self, name, t)
            p.dma("sy", t[...], d[...], "c_ld")


def ln_scratch(p, tag):
    return {
        "xsq0": p.sb(f"{tag}_xsq0", [128, 512], BF16),
        "xsq1": p.sb(f"{tag}_xsq1", [128, 512], BF16),
        "xsq2": p.sb(f"{tag}_xsq2", [128, 512], BF16),
        "xsq3": p.sb(f"{tag}_xsq3", [128, 512], BF16),
        "row": p.sb(f"{tag}_row", [1, 1024], F32),
        "rtmp": p.sb(f"{tag}_rtmp", [1, 512], F32),
        "nt": p.sb(f"{tag}_nt", [128, 512], F32),
        "nt2": p.sb(f"{tag}_nt2", [128, 512], F32),
    }


def emit_ln(p, C, tag, x_sb, xln_sb, t0, TT, pb, sc, out_t0=None):
    sq, st, rw, bc, nm = (f"{tag}_sq", f"{tag}_st", f"{tag}_rw",
                          f"{tag}_bc", f"{tag}_nm")
    ts = slice(t0, t0 + TT)
    ots = slice(out_t0, out_t0 + TT) if out_t0 is not None else ts
    ps_sx = pb[0][0:1, 0:TT]
    ps_sx2 = pb[1][0:1, 0:TT]
    ps_mb = pb[2][:, 0:TT]
    ps_rb = pb[3][:, 0:TT]
    row = sc["row"]

    p.wait("t", rw)
    p.wait("t", nm)
    for c in range(NCH):
        bs = sc["xsq0"] if c % 2 == 0 else sc["xsq1"]
        bq = sc["xsq2"] if c % 2 == 0 else sc["xsq3"]
        if p.cnt.get(st, 0) >= 4:
            p.wait("s", st, p.cnt[st] - 2)
        p.op("s", lambda e, b=bs, c_=c: e.copy(b[:, 0:TT], x_sb[:, c_, ts]), sq)
        p.op("s", lambda e, b=bq, c_=c: e.activation(
            b[:, 0:TT], x_sb[:, c_, ts], AF.Square), sq)
        p.wait("t", sq)
        p.op("t", lambda e, b=bs, c_=c: e.matmul(
            ps_sx, C.ones_col_bf[:, :], b[:, 0:TT],
            start=(c_ == 0), stop=(c_ == 7)), st)
        p.op("t", lambda e, b=bq, c_=c: e.matmul(
            ps_sx2, C.ones_col_bf[:, :], b[:, 0:TT],
            start=(c_ == 0), stop=(c_ == 7)), st)
    p.wait("v", st)
    p.op("v", lambda e: e.tensor_scalar(row[0:1, 0:TT], ps_sx, 1.0 / D, None, op0=ALU.mult), rw)
    p.op("v", lambda e: e.tensor_scalar(row[0:1, 512:512 + TT], ps_sx2, 1.0 / D, None, op0=ALU.mult), rw)
    p.wait("v", rw)
    p.op("v", lambda e: e.tensor_tensor(sc["rtmp"][0:1, 0:TT], row[0:1, 0:TT], row[0:1, 0:TT], op=ALU.mult), rw)
    p.wait("v", rw)
    p.op("v", lambda e: e.tensor_tensor(row[0:1, 512:512 + TT], row[0:1, 512:512 + TT], sc["rtmp"][0:1, 0:TT], op=ALU.subtract), rw)
    p.wait("v", rw)
    p.op("v", lambda e: e.tensor_scalar(row[0:1, 512:512 + TT], row[0:1, 512:512 + TT], 1e-5, None, op0=ALU.add), rw)
    p.wait("s", rw)
    p.op("s", lambda e: e.activation(row[0:1, 512:512 + TT], row[0:1, 512:512 + TT], AF.Sqrt), sq)
    p.wait("v", sq)
    p.op("v", lambda e: e.reciprocal(row[0:1, 512:512 + TT], row[0:1, 512:512 + TT]), rw)
    p.wait("t", rw)
    p.op("t", lambda e: e.matmul(ps_mb, C.ones1_f32[:, :], row[0:1, 0:TT], start=True, stop=True), bc)
    p.op("t", lambda e: e.matmul(ps_rb, C.ones1_f32[:, :], row[0:1, 512:512 + TT], start=True, stop=True), bc)
    p.wait("v", bc)
    for c in range(NCH):
        ntb = sc["nt"] if c % 2 == 0 else sc["nt2"]
        if c >= 2:
            p.wait("v", nm, p.cnt[nm] - 1)
        p.op("v", lambda e, c_=c, b=ntb: e.tensor_tensor(b[:, 0:TT], x_sb[:, c_, ts], ps_mb, op=ALU.subtract), nm)
        p.wait("v", nm)
        p.op("v", lambda e, c_=c, b=ntb: e.tensor_tensor(xln_sb[:, c_, ots], b[:, 0:TT], ps_rb, op=ALU.mult), nm)
    return nm


def shard_x(x):
    out = []
    for c in range(NCORE):
        b, b0, b1 = core_blocks(c)
        xa = x[b, b0 * 1024:(b0 + 1) * 1024]
        xb = x[b, b1 * 1024:(b1 + 1) * 1024]
        out.append(np.ascontiguousarray(np.concatenate([xa, xb], 0).T))
    return out


def unshard_x(shards):
    x = np.zeros((B, S, D), dtype=np.asarray(shards[0]).dtype)
    for c in range(NCORE):
        b, b0, b1 = core_blocks(c)
        sh = np.asarray(shards[c]).T
        x[b, b0 * 1024:(b0 + 1) * 1024] = sh[:1024]
        x[b, b1 * 1024:(b1 + 1) * 1024] = sh[1024:]
    return x


def emit_ffn1(p, C, tag, x_sb, xln, pb, w1t_d, w2t_d, bufs):
    h_sb = bufs["h_sb"]
    lt_sb, ltln = bufs["lt_sb"], bufs["ltln"]
    lora, lraw = f"{tag}_lo", f"{tag}_lodma"

    p.op("v", lambda e: e.bn_stats(bufs["ltst"][:, 0:6], lt_sb[:, 0:512]), lora)
    p.op("v", lambda e: e.bn_stats(bufs["ltst"][:, 6:12], lt_sb[:, 512:1024]), lora)
    p.wait("v", lora)
    p.op("v", lambda e: e.bn_aggr(bufs["ltmv"][:, 0:2], bufs["ltst"][:, 0:12]), lora)
    p.wait("v", lora)
    p.op("v", lambda e: e.tensor_scalar(bufs["ltrs"][:, 0:1], bufs["ltmv"][:, 1:2], 1e-5, None, op0=ALU.add), lora)
    p.wait("s", lora)
    p.op("s", lambda e: e.activation(bufs["ltrs"][:, 0:1], bufs["ltrs"][:, 0:1], AF.Sqrt), lora)
    p.wait("v", lora)
    p.op("v", lambda e: e.reciprocal(bufs["ltrs"][:, 0:1], bufs["ltrs"][:, 0:1]), lora)
    p.wait("v", lora)
    p.op("v", lambda e: e.tensor_tensor(bufs["ltnm"][:, 0:1], bufs["ltmv"][:, 0:1], bufs["ltrs"][:, 0:1], op=ALU.mult), lora)
    p.wait("v", lora)
    p.op("v", lambda e: e.tensor_scalar(bufs["ltnm"][:, 0:1], bufs["ltnm"][:, 0:1], -1.0, None, op0=ALU.mult), lora)
    p.wait("v", lora)
    p.op("v", lambda e: e.tensor_scalar(ltln[:, :], lt_sb[:, :], bufs["ltrs"][:, 0:1], bufs["ltnm"][:, 0:1], op0=ALU.mult, op1=ALU.add), lora)
    pT = pb[4].bitcast(BF16)
    p.wait("t", lora)
    for c in range(NCH):
        p.op("t", lambda e, c_=c: e.transpose(
            pT[:, c_ * 16:(c_ + 1) * 16], ltln[0:16, c_ * 128:(c_ + 1) * 128],
            C.ident_bf[0:16, 0:16]), lora)
    p.wait("v", lora)
    p.op("v", lambda e: e.tensor_copy(bufs["loraT"][:, :, :], pT[:, 0:128].rearrange("p (a b) -> p a b", a=8)), lora)
    lora_done = p.cnt[lora]
    p.wait("g", lora, lora_done)
    for i in range(4):
        p.dma("g", bufs["e1s"][0:1, i * 1024:(i + 1) * 1024],
              ltln[2 * i:2 * i + 1, :], lraw)
    p.wait("v", lraw)
    p.op("v", lambda e: e.tensor_scalar(bufs["e1s"][0:1, :], bufs["e1s"][0:1, :], bufs["sc_sb"][0:1, 0:1], None, op0=ALU.mult), lora)
    for ch in range(NCH):
        p.op("g", lambda e, ch_=ch: e.memset(bufs["e2b"][:, ch_, :], 0.0), lora)
    p.wait("g", lora)
    for ch in range(NCH):
        istar, half = ch // 2, ch % 2
        for jj in range(4):
            src = ltln[8 + 2 * istar:9 + 2 * istar,
                       512 * half + jj:512 * half + jj + 509:4]
            p.dma("g", bufs["e2b"][4 * jj + istar:4 * jj + istar + 1, ch, 0:128],
                  src, lraw, allow_slow_non_contiguous=True)
    lraw_done = p.cnt[lraw]

    w1l, w1m, hgel = f"{tag}_w1l", f"{tag}_w1m", f"{tag}_hg"
    td, tdr = f"{tag}_td", f"{tag}_tdr"
    uacc, uev, udr = f"{tag}_ua", f"{tag}_ue", f"{tag}_ud"
    w2l, w2m, res = f"{tag}_w2l", f"{tag}_w2m", f"{tag}_res"
    wbufs1 = [bufs["w1a"], bufs["w1b"]]
    wbufs2 = [bufs["w2a"], bufs["w2b"]]
    w1use = [0, 0]
    w2use = [0, 0]
    for ti in range(4):
        ts = slice(ti * 512, (ti + 1) * 512)
        nm = emit_ln(p, C, "Gln", x_sb, xln, ti * 512, 512, pb[0:4],
                     bufs["lns"])
        p.wait("t", lora, lora_done)
        p.wait("t", nm)
        p.wait("t", uev)
        for c in range(NCH):
            p.op("t", lambda e, c_=c, ts_=ts: e.matmul(
                pb[5][0:4, 0:512], bufs["loraT"][:, c_, 1:9:2],
                xln[:, c_, ts_], start=(c_ == 0), stop=(c_ == 7)), td)
        p.wait("v", td)
        p.op("v", lambda e: e.tensor_copy(bufs["t_f"][0:4, :], pb[5][0:4, 0:512]), td)
        p.wait("g", td)
        for i in range(4):
            p.dma("g", bufs["t_row"][0:1, i * 512:(i + 1) * 512],
                  bufs["t_f"][i:i + 1, 0:512], tdr)
        p.wait("g", lraw, lraw_done)
        for ch in range(32):
            wb = wbufs1[ch % 2]
            p.wait("g", w1m, w1use[ch % 2])
            w1l_c = p.dma("g", wb[...],
                          w1t_d[:, ch * 128:(ch + 1) * 128].rearrange(
                              "(c p) m -> p c m", p=128), w1l + str(ch % 2))
            psH = pb[6 if ch % 2 == 0 else 7][:, 0:512]
            p.wait("t", w1l + str(ch % 2), w1l_c)
            p.wait("t", tdr)
            if ch < 2:
                p.wait("t", res)
            else:
                p.wait("t", hgel, p.cnt.get(hgel, 0) - 1)
            for c in range(NCH):
                p.op("t", lambda e, c_=c, w=wb, ps=psH, ts_=ts: e.matmul(
                    ps, w[:, c_, :], xln[:, c_, ts_], start=(c_ == 0), stop=False))
            p.op("t", lambda e, ch_=ch, ps=psH: e.matmul(
                ps, bufs["b1row"][0:1, ch_ * 128:(ch_ + 1) * 128],
                C.ones_row_bf[:, :], start=False, stop=False))
            p.op("t", lambda e, ch_=ch, ps=psH: e.matmul(
                ps, bufs["e1s"][0:1, ch_ * 128:(ch_ + 1) * 128],
                bufs["t_row"][0:1, (ch_ // 8) * 512:(ch_ // 8 + 1) * 512],
                start=False, stop=True), w1m)
            w1use[ch % 2] = p.cnt[w1m]
            p.wait("s", w1m)
            p.op("s", lambda e, ch_=ch, ps=psH: e.activation(
                h_sb[:, ch_, :], ps, AF.Gelu), hgel)
        p.wait("t", hgel)
        p.wait("t", nm)
        for c in range(32):
            p.op("t", lambda e, c_=c: e.matmul(
                pb[4 + c_ // 8][0:4, 0:512], bufs["loraT"][:, c_ % 8, 9:16:2],
                h_sb[:, c_, :], start=(c_ % 8 == 0), stop=(c_ % 8 == 7)), uacc)
        p.wait("v", uacc)
        for jj in range(4):
            p.op("v", lambda e, j=jj: e.tensor_scalar(
                bufs["u_f"][0:4, j, :], pb[4 + j][0:4, 0:512],
                bufs["sc4_sb"][0:4, 0:1], None, op0=ALU.mult), uev)
        p.wait("g", uev)
        for jj in range(4):
            p.dma("g", bufs["u_all"][4 * jj:4 * jj + 4, 0:512],
                  bufs["u_f"][0:4, jj, :], udr)
        for ch in range(NCH):
            wb = wbufs2[ch % 2]
            p.wait("g", w2m, w2use[ch % 2])
            w2l_c = p.dma("g", wb[...],
                          w2t_d[:, ch * 128:(ch + 1) * 128].rearrange(
                              "(c p) m -> p c m", p=128), w2l + str(ch % 2))
            psO = pb[6 if ch % 2 == 0 else 7][:, 0:512]
            p.wait("t", w2l + str(ch % 2), w2l_c)
            p.wait("t", udr)
            if ch < 2:
                p.wait("t", hgel)
                p.wait("t", uev)
            else:
                p.wait("t", res, p.cnt.get(res, 0) - 1)
            for c in range(32):
                p.op("t", lambda e, c_=c, w=wb, ps=psO: e.matmul(
                    ps, w[:, c_, :], h_sb[:, c_, :], start=(c_ == 0), stop=False))
            p.op("t", lambda e, ch_=ch, ps=psO: e.matmul(
                ps, bufs["b2row"][0:1, ch_ * 128:(ch_ + 1) * 128],
                C.ones_row_bf[:, :], start=False, stop=False))
            p.op("t", lambda e, ch_=ch, ps=psO: e.matmul(
                ps, bufs["e2b"][0:16, ch_, :], bufs["u_all"][0:16, 0:512],
                start=False, stop=True), w2m)
            w2use[ch % 2] = p.cnt[w2m]
            p.wait("v", w2m)
            p.op("v", lambda e, ch_=ch, ps=psO, ts_=ts: e.scalar_tensor_tensor(
                x_sb[:, ch_, ts_], ps, 0.5, x_sb[:, ch_, ts_],
                op0=ALU.mult, op1=ALU.add), res)
    return res


def ffn1_bufs(p, tag):
    return {
        "h_sb": p.sb(f"{tag}_h", [128, 32, 512], BF16),
        "lt_sb": p.sb(f"{tag}_lt", [16, D], F32),
        "ltln": p.sb(f"{tag}_ltln", [16, D], BF16),
        "ltst": p.sb(f"{tag}_ltst", [16, 12], F32),
        "ltmv": p.sb(f"{tag}_ltmv", [16, 2], F32),
        "ltrs": p.sb(f"{tag}_ltrs", [16, 1], F32),
        "ltnm": p.sb(f"{tag}_ltnm", [16, 1], F32),
        "loraT": p.sb(f"{tag}_loraT", [128, NCH, 16], BF16),
        "e1s": p.sb(f"{tag}_e1s", [1, 4096], BF16),
        "e2b": p.sb(f"{tag}_e2b", [16, NCH, 128], BF16),
        "t_row": p.sb(f"{tag}_trow", [1, 4 * 512], BF16),
        "t_f": p.sb(f"{tag}_tf", [4, 512], BF16),
        "u_all": p.sb(f"{tag}_uall", [16, 512], BF16),
        "u_f": p.sb(f"{tag}_uf", [4, 4, 512], BF16),
        "b1row": p.sb(f"{tag}_b1r", [1, 4096], BF16),
        "b2row": p.sb(f"{tag}_b2r", [1, D], BF16),
        "sc_sb": p.sb(f"{tag}_sc", [1, 1], F32),
        "sc4_sb": p.sb(f"{tag}_sc4", [4, 1], F32),
        "w1a": p.sb(f"{tag}_w1a", [128, NCH, 128], BF16),
        "w1b": p.sb(f"{tag}_w1b", [128, NCH, 128], BF16),
        "w2a": p.sb(f"{tag}_w2a", [128, 32, 128], BF16),
        "w2b": p.sb(f"{tag}_w2b", [128, 32, 128], BF16),
        "lns": ln_scratch(p, f"{tag}_lns"),
    }


def build_ffn1():
    p = Prog("ffn1")
    nc = p.nc
    x_d = nc.dram_tensor("x_fm", [D, TPC], F32, kind="ExternalInput").ap()
    w1t_d = nc.dram_tensor("w1t", [D, 4096], F32, kind="ExternalInput").ap()
    w2t_d = nc.dram_tensor("w2t", [4096, D], F32, kind="ExternalInput").ap()
    b1r_d = nc.dram_tensor("b1row", [1, 4096], F32, kind="ExternalInput").ap()
    b2r_d = nc.dram_tensor("b2row", [1, D], F32, kind="ExternalInput").ap()
    sc_d = nc.dram_tensor("scal", [1, 1], F32, kind="ExternalInput").ap()
    sc4_d = nc.dram_tensor("scal4", [4, 1], F32, kind="ExternalInput").ap()
    lt_d = nc.dram_tensor("ltoks", [16, D], F32, kind="ExternalInput").ap()
    y_d = nc.dram_tensor("y_fm", [D, TPC], F32, kind="ExternalOutput").ap()

    C = Consts(p)
    x_sb = p.sb("x_sb", [128, NCH, TPC], F32)
    xln = p.sb("xln", [128, NCH, 512], BF16)
    pb = [p.psum(f"pb{i}", [128, 512], F32) for i in range(8)]
    bufs = ffn1_bufs(p, "f1")

    for c in range(NCH):
        p.dma("sy", x_sb[:, c, :], x_d[c * 128:(c + 1) * 128, :], "xld")
    p.dma("sy", bufs["lt_sb"][:, :], lt_d[:, :], "xld")
    p.dma("g", bufs["b1row"][:, :], b1r_d[:, :], "wld")
    p.dma("g", bufs["b2row"][:, :], b2r_d[:, :], "wld")
    p.dma("sy", bufs["sc_sb"][:, :], sc_d[:, :], "xld")
    p.dma("sy", bufs["sc4_sb"][:, :], sc4_d[:, :], "xld")
    for e in ("v", "s", "t", "g"):
        p.wait(e, "xld")
        p.wait(e, "wld")
        p.wait(e, "c_ld")

    class XlnView:
        def __getitem__(self, idx):
            _, c, ts = idx
            off = ts.start % 512
            return xln[:, c, off:off + (ts.stop - ts.start)]
    res = emit_ffn1(p, C, "f1", x_sb, XlnView(), pb, w1t_d, w2t_d, bufs)

    p.wait("sy", res)
    for c in range(NCH):
        p.dma("sy", y_d[c * 128:(c + 1) * 128, :], x_sb[:, c, :], "yst")
    p.wait("sy", "yst")
    return p


def ffn1_inputs(core, x_fm, W1, b1, W2, b2, scale, x_full):
    b = core // 4
    cd = make_consts()
    return dict(
        x_fm=np.ascontiguousarray(x_fm),
        w1t=np.ascontiguousarray(W1.T),
        w2t=np.ascontiguousarray(W2.T),
        b1row=np.ascontiguousarray(b1.reshape(1, 4096)),
        b2row=np.ascontiguousarray(b2.reshape(1, D)),
        scal=np.full((1, 1), scale, np.float32),
        scal4=np.full((4, 1), scale, np.float32),
        ltoks=np.ascontiguousarray(x_full[b, 1:17]),
        **cd,
    )



def attn_params(layer):
    if layer == 0:
        return dict(nheads=8, dh=128, inv_scale=1.0 / 128.0, rope=True)
    return dict(nheads=16, dh=64, inv_scale=1.0 / 8.0, rope=False)


def own_tiles(rr):
    return [2 * rr, 2 * rr + 1, 2 * (7 - rr), 2 * (7 - rr) + 1]


def group_bases(rr):
    ba, bb = 1024 * rr, 1024 * (7 - rr)
    return [ba, ba + 512, bb, bb + 512]


def r3(ap2, a):
    return ap2.rearrange("p (a b) -> p a b", a=a)


def attn_bufs(p, tag):
    return {
        "arena_a": p.sb(f"{tag}_aa", [128, 24576], BF16),
        "arena_b": p.sb(f"{tag}_ab", [128, 16384], BF16),
        "woT": p.sb(f"{tag}_woT", [128, NCH, 1024], BF16),
        "o_all": p.sb(f"{tag}_oall", [128, NCH, 2048], BF16),
        "lns": ln_scratch(p, f"{tag}_lns"),
        "xfl0": p.sb(f"{tag}_xfl0", [128, NCH, 512], BF16),
        "xfl1": p.sb(f"{tag}_xfl1", [128, NCH, 512], BF16),
        "csb": p.sb(f"{tag}_csb", [128, 1024], BF16),
        "csb2": p.sb(f"{tag}_csb2", [128, 1024], BF16),
        "kout": p.sb(f"{tag}_kout", [128, NCH, 128], BF16),
        "vout": p.sb(f"{tag}_vout", [128, NCH, 128], BF16),
        "qout": p.sb(f"{tag}_qout", [128, NCH, 128], BF16),
        "rt1": p.sb(f"{tag}_rt1", [128, 4, 64], F32),
        "rt2": p.sb(f"{tag}_rt2", [128, 4, 64], F32),
        "rt3": p.sb(f"{tag}_rt3", [128, 4, 64], F32),
        "rt4": p.sb(f"{tag}_rt4", [128, 4, 64], F32),
        "ksq": p.sb(f"{tag}_ksq", [128, NCH, 128], BF16),
        "kn": p.sb(f"{tag}_kn", [128, 8], F32),
        "knacc": p.sb(f"{tag}_knacc", [128, 8], F32),
        "qnacc": p.sb(f"{tag}_qnacc", [128, 8], F32),
        "brow": p.sb(f"{tag}_brow", [1, 16], F32),
        "negb": p.sb(f"{tag}_negb", [128, 1], F32),
        "masks": p.sb(f"{tag}_masks", [128, 4, 512], BF16),
        "sccol": p.sb(f"{tag}_sccol", [128, 1], F32),
        "p_sb": p.sb(f"{tag}_psb", [128, 4, 512], BF16),
        "rrow": p.sb(f"{tag}_rrow", [1, 512], F32),
        "rb_sb": p.sb(f"{tag}_rb", [128, 512], F32),
        "xc0": p.sb(f"{tag}_xc0", [128, 512], F32),
        "xc1": p.sb(f"{tag}_xc1", [128, 512], F32),
        "yc0": p.sb(f"{tag}_yc0", [128, 512], F32),
        "yc1": p.sb(f"{tag}_yc1", [128, 512], F32),
    }


def _rope6(p, bufs, cs3, pvsrc, dst, sem):
    p.wait("v", sem)
    p.op("v", lambda e: e.tensor_tensor(
        bufs["rt1"][:, :, :], pvsrc[:, :, 0:64], cs3[:, :, 0:64], op=ALU.mult), sem)
    p.op("v", lambda e: e.tensor_tensor(
        bufs["rt2"][:, :, :], pvsrc[:, :, 64:128], cs3[:, :, 64:128], op=ALU.mult), sem)
    p.op("v", lambda e: e.tensor_tensor(
        bufs["rt3"][:, :, :], pvsrc[:, :, 0:64], cs3[:, :, 64:128], op=ALU.mult), sem)
    p.op("v", lambda e: e.tensor_tensor(
        bufs["rt4"][:, :, :], pvsrc[:, :, 64:128], cs3[:, :, 0:64], op=ALU.mult), sem)
    p.wait("v", sem)
    p.op("v", lambda e: e.tensor_tensor(
        dst[:, :, 0:64], bufs["rt1"][:, :, :], bufs["rt2"][:, :, :], op=ALU.subtract), sem)
    p.op("v", lambda e: e.tensor_tensor(
        dst[:, :, 64:128], bufs["rt3"][:, :, :], bufs["rt4"][:, :, :], op=ALU.add), sem)


def emit_attn(p, C, tag, layer, rr, xfull_d, x_d, y_d, wq_d, wk_d, wv_d,
              woT_d, cs_d, masks_d, sccol_d, kbuf, vbuf, qbuf, pb, bufs,
              a_tiles=16):
    ap = attn_params(layer)
    aA, aB = bufs["arena_a"], bufs["arena_b"]
    wq = r3(aA[:, 0:8192], NCH)
    wk = r3(aA[:, 8192:16384], NCH)
    wv = r3(aA[:, 16384:24576], NCH)
    aBf = aB.bitcast(F32)
    xf0 = r3(aBf[:, 0:4096], NCH)
    xf1 = r3(aBf[:, 4096:8192], NCH)
    khb = aA[:, 0:8192]
    vhb = r3(aA[:, 8192:16384], 64)
    q_sb = r3(aB[:, 0:16384], NCH)
    woT, o_all = bufs["woT"], bufs["o_all"]
    lns = bufs["lns"]
    kout, vout, qout = bufs["kout"], bufs["vout"], bufs["qout"]
    ot, gbs = own_tiles(rr), group_bases(rr)

    xld, wld, csld = f"{tag}_xld", f"{tag}_wld", f"{tag}_cs"
    kvm, rpd, kvev = f"{tag}_kvm", f"{tag}_rpd", f"{tag}_kvev"
    knsm, kst = f"{tag}_knsm", f"{tag}_kst"
    for c in range(NCH):
        p.dma("g", wq[:, c, :], wq_d[c * 128:(c + 1) * 128, :], wld)
        p.dma("g", wk[:, c, :], wk_d[c * 128:(c + 1) * 128, :], wld)
        p.dma("g", wv[:, c, :], wv_d[c * 128:(c + 1) * 128, :], wld)
        p.dma("g", woT[:, c, :], woT_d[c * 128:(c + 1) * 128, :], wld)
    p.dma("sy", bufs["sccol"][:, :], sccol_d[:, :], f"{tag}_scl")
    wld_done = p.cnt[wld]
    for e in ("t", "v", "s", "g"):
        p.wait(e, wld, wld_done)
        p.wait(e, f"{tag}_scl")
        p.wait(e, "c_ld")
    p.op("v", lambda e: e.memset(bufs["knacc"][:, :], 0.0), knsm)
    p.op("v", lambda e: e.memset(bufs["qnacc"][:, :], 0.0), knsm)

    kvm_hist, sq_hist = {}, {}
    for tt in range(a_tiles):
        xf = xf0 if tt % 2 == 0 else xf1
        xfl = bufs["xfl0"] if tt % 2 == 0 else bufs["xfl1"]
        if tt >= 2:
            p.wait("sy", kvm, kvm_hist[tt - 2])
            p.wait("sy", "Gln_sq", sq_hist[tt - 2])
        xldp = xld + str(tt % 2)
        for c in range(NCH):
            p.dma("sy", xf[:, c, :], xfull_d[c * 128:(c + 1) * 128,
                                             tt * 512:(tt + 1) * 512], xldp)
        xld_c = p.cnt[xldp]
        p.wait("s", xldp, xld_c)
        p.wait("v", xldp, xld_c)
        nm = emit_ln(p, C, "Gln", xf, xfl, 0, 512, pb[0:4], lns)
        own = tt in ot
        for sub in range(4):
            cst = bufs["csb"] if (tt * 4 + sub) % 2 == 0 else bufs["csb2"]
            cs3 = r3(cst, NCH)
            csp = csld + str((tt * 4 + sub) % 2)
            if ap["rope"]:
                if p.cnt.get(rpd, 0) > 0 and (tt * 4 + sub) >= 2:
                    p.wait("sy", rpd)
                p.dma("sy", cst[:, :], cs_d[tt * 512 + sub * 128:
                                            tt * 512 + sub * 128 + 128, :], csp)
            cs_c = p.cnt.get(csp, 0)
            p.wait("t", nm)
            p.wait("t", rpd)
            for g2 in range(2):
                for c in range(NCH):
                    p.op("t", lambda e, c_=c, g_=g2, s_=sub, xl=xfl: e.matmul(
                        pb[4 + g_][:, :], xl[:, c_, s_ * 128:(s_ + 1) * 128],
                        wk[:, c_, g_ * 512:(g_ + 1) * 512],
                        start=(c_ == 0), stop=(c_ == 7)), kvm)
            kvm_k = p.cnt[kvm]
            p.wait("t", kvev)
            for g2 in range(2):
                for c in range(NCH):
                    p.op("t", lambda e, c_=c, g_=g2, s_=sub, xl=xfl: e.matmul(
                        pb[6 + g_][:, :], xl[:, c_, s_ * 128:(s_ + 1) * 128],
                        wv[:, c_, g_ * 512:(g_ + 1) * 512],
                        start=(c_ == 0), stop=(c_ == 7)), kvm)
            kvm_v = p.cnt[kvm]
            p.wait("v", kvm, kvm_k)
            p.wait("v", kst)
            if ap["rope"]:
                p.wait("v", csp, cs_c)
                _rope6(p, bufs, cs3[:, 0:4, :], r3(pb[4][:, :], 4), kout[:, 0:4, :], rpd)
                _rope6(p, bufs, cs3[:, 4:8, :], r3(pb[5][:, :], 4), kout[:, 4:8, :], rpd)
            else:
                p.op("v", lambda e: e.tensor_copy(kout[:, 0:4, :], r3(pb[4][:, :], 4)), rpd)
                p.op("v", lambda e: e.tensor_copy(kout[:, 4:8, :], r3(pb[5][:, :], 4)), rpd)
            rpd_k = p.cnt[rpd]
            p.wait("s", kvm, kvm_v)
            p.wait("s", kst)
            p.op("s", lambda e: e.copy(vout[:, 0:4, :], r3(pb[6][:, :], 4)), kvev)
            p.op("s", lambda e: e.copy(vout[:, 4:8, :], r3(pb[7][:, :], 4)), kvev)
            kvev_c = p.cnt[kvev]
            p.wait("s", rpd, rpd_k)
            p.op("s", lambda e: e.activation(
                bufs["ksq"].rearrange("p a b -> p (a b)"),
                kout.rearrange("p a b -> p (a b)"), AF.Square), knsm)
            knsq_c = p.cnt[knsm]
            p.wait("v", knsm, knsq_c)
            for h in range(NCH):
                p.op("v", lambda e, h_=h: e.reduce_sum(
                    bufs["kn"][:, h_:h_ + 1], bufs["ksq"][:, h_, :], axis=AX), knsm)
            p.wait("v", knsm)
            p.op("v", lambda e: e.tensor_tensor(
                bufs["knacc"][:, :], bufs["knacc"][:, :], bufs["kn"][:, :], op=ALU.max), knsm)
            trow = tt * 512 + sub * 128
            p.wait("g", rpd, rpd_k)
            p.dma("g", kbuf[trow:trow + 128, :], kout.rearrange("p a b -> p (a b)"), kst)
            p.wait("g", kvev, kvev_c)
            p.dma("g", vbuf[trow:trow + 128, :], vout.rearrange("p a b -> p (a b)"), kst)
            if own:
                qsub = ot.index(tt) * 4 + sub
                p.wait("t", rpd, p.cnt[rpd])
                for g2 in range(2):
                    for c in range(NCH):
                        p.op("t", lambda e, c_=c, g_=g2, s_=sub, xl=xfl: e.matmul(
                            pb[4 + g_][:, :], xl[:, c_, s_ * 128:(s_ + 1) * 128],
                            wq[:, c_, g_ * 512:(g_ + 1) * 512],
                            start=(c_ == 0), stop=(c_ == 7)), kvm)
                qm_c = p.cnt[kvm]
                p.wait("v", kvm, qm_c)
                p.wait("v", kst)
                if ap["rope"]:
                    _rope6(p, bufs, cs3[:, 0:4, :], r3(pb[4][:, :], 4), qout[:, 0:4, :], rpd)
                    _rope6(p, bufs, cs3[:, 4:8, :], r3(pb[5][:, :], 4), qout[:, 4:8, :], rpd)
                else:
                    p.op("v", lambda e: e.tensor_copy(qout[:, 0:4, :], r3(pb[4][:, :], 4)), rpd)
                    p.op("v", lambda e: e.tensor_copy(qout[:, 4:8, :], r3(pb[5][:, :], 4)), rpd)
                rpd_q = p.cnt[rpd]
                p.wait("s", rpd, rpd_q)
                p.op("s", lambda e: e.activation(
                    bufs["ksq"].rearrange("p a b -> p (a b)"),
                    qout.rearrange("p a b -> p (a b)"), AF.Square), knsm)
                qsq_c = p.cnt[knsm]
                p.wait("v", knsm, qsq_c)
                for h in range(NCH):
                    p.op("v", lambda e, h_=h: e.reduce_sum(
                        bufs["kn"][:, h_:h_ + 1], bufs["ksq"][:, h_, :], axis=AX), knsm)
                p.wait("v", knsm)
                p.op("v", lambda e: e.tensor_tensor(
                    bufs["qnacc"][:, :], bufs["qnacc"][:, :], bufs["kn"][:, :], op=ALU.max), knsm)
                p.wait("g", rpd, rpd_q)
                p.dma("g", qbuf[qsub * 128:qsub * 128 + 128, :],
                      qout.rearrange("p a b -> p (a b)"), kst)
        kvm_hist[tt] = p.cnt[kvm]
        sq_hist[tt] = p.cnt["Gln_sq"]
    phaseA_kst = p.cnt[kst]

    p.wait("t", knsm)
    p.wait("t", kvev)
    p.op("t", lambda e: e.transpose(pb[4][0:8, 0:128], bufs["knacc"][:, 0:8], C.ident_f32[:, :]), knsm)
    p.op("t", lambda e: e.transpose(pb[5][0:8, 0:128], bufs["qnacc"][:, 0:8], C.ident_f32[:, :]), knsm)
    p.wait("v", knsm)
    p.op("v", lambda e: e.reduce_max(bufs["kn"][0:8, 0:1], pb[4][0:8, 0:128], axis=AX), knsm)
    p.op("v", lambda e: e.reduce_max(bufs["kn"][0:8, 1:2], pb[5][0:8, 0:128], axis=AX), knsm)
    p.wait("t", knsm)
    p.op("t", lambda e: e.transpose(pb[6][0:1, 0:8], bufs["kn"][0:8, 0:1], C.ident_f32[0:8, 0:8]), knsm)
    p.op("t", lambda e: e.transpose(pb[6][0:1, 8:16], bufs["kn"][0:8, 1:2], C.ident_f32[0:8, 0:8]), knsm)
    p.wait("v", knsm)
    p.op("v", lambda e: e.tensor_copy(bufs["brow"][0:1, 0:16], pb[6][0:1, 0:16]), knsm)
    p.wait("v", knsm)
    p.op("v", lambda e: e.tensor_tensor(bufs["brow"][0:1, 0:8], bufs["brow"][0:1, 0:8], bufs["brow"][0:1, 8:16], op=ALU.mult), knsm)
    p.wait("v", knsm)
    p.op("v", lambda e: e.tensor_scalar(bufs["brow"][0:1, 0:8], bufs["brow"][0:1, 0:8], ap["inv_scale"] ** 2, None, op0=ALU.mult), knsm)
    p.wait("s", knsm)
    p.op("s", lambda e: e.activation(bufs["brow"][0:1, 0:8], bufs["brow"][0:1, 0:8], AF.Sqrt), knsm)
    p.wait("v", knsm)
    p.op("v", lambda e: e.tensor_scalar(bufs["brow"][0:1, 0:8], bufs["brow"][0:1, 0:8], -1.0, None, op0=ALU.mult), knsm)
    brow_done = p.cnt[knsm]
    p.wait("sy", kst, phaseA_kst)
    qld = f"{tag}_qld"
    for c in range(NCH):
        p.dma("sy", q_sb[:, c, :], qbuf[:, c * 128:(c + 1) * 128], qld,
              transpose=True)
    p.wait("t", qld)

    NH = ap["nheads"]
    sS, sE, sMK, sAV, sOE = (f"{tag}_S", f"{tag}_E", f"{tag}_MK",
                             f"{tag}_AV", f"{tag}_OE")
    khl, vhl, nbs = f"{tag}_khl", f"{tag}_vhl", f"{tag}_nb"
    exp_hist = [0, 0]
    for h in range(NH):
        hc = h if layer == 0 else h // 2
        dsl = slice(0, 128) if layer == 0 else slice(64 * (h % 2), 64 * (h % 2) + 64)
        p.wait("t", knsm, brow_done)
        p.wait("t", nbs)
        p.wait("t", sE)
        p.wait("t", sOE)
        p.op("t", lambda e, hc_=hc: e.matmul(pb[7][:, 0:1], C.ones1_f32[:, :],
             bufs["brow"][0:1, hc_:hc_ + 1], start=True, stop=True), nbs)
        p.wait("v", nbs)
        p.wait("v", sE)
        p.op("v", lambda e: e.tensor_copy(bufs["negb"][:, 0:1], pb[7][:, 0:1]), nbs)
        negb_c = p.cnt[nbs]
        p.wait("s", nbs, negb_c)
        if layer == 0 or h % 2 == 0:
            p.wait("sy", sS)
            p.wait("sy", sAV)
            p.wait("sy", kvm)
            khl_c = p.dma("sy", khb[:, :], kbuf[:, hc * 128:(hc + 1) * 128],
                          khl, transpose=True)
            for part in range(8):
                p.dma("sy", vhb[:, part * 8:(part + 1) * 8, :],
                      vbuf[part * 1024:(part + 1) * 1024,
                           hc * 128:(hc + 1) * 128]
                      .rearrange("(kt p) d -> p kt d", p=128), vhl)
            vhl_c = p.cnt[vhl]
            p.wait("t", khl, khl_c)
            p.wait("t", vhl, vhl_c)
        for g in range(4):
            nreg = gbs[g] // 512 + 1
            for r in range(nreg):
                kg = 512 * r
                diag = (r == nreg - 1)
                p.wait("t", sE, exp_hist[-1])
                for j in range(4):
                    p.op("t", lambda e, j_=j, kg_=kg, d_=dsl, g_=g, hc_=hc: e.matmul(
                        pb[j_][:, :], khb[d_, kg_ + j_ * 128:kg_ + (j_ + 1) * 128],
                        q_sb[d_, hc_, g_ * 512:(g_ + 1) * 512],
                        start=True, stop=True), sS)
                sS_c = p.cnt[sS]
                p.wait("s", sS, sS_c)
                p.wait("s", sAV)
                p.wait("s", sMK)
                for j in range(4):
                    p.op("s", lambda e, j_=j: e.activation(
                        bufs["p_sb"][:, j_, :], pb[j_][:, :], AF.Exp,
                        bias=bufs["negb"][:, 0:1], scale=bufs["sccol"][:, 0:1]), sE)
                exp_hist.append(p.cnt[sE])
                if diag:
                    p.wait("v", sE)
                    for j in range(4):
                        p.op("v", lambda e, j_=j: e.tensor_tensor(
                            bufs["p_sb"][:, j_, :], bufs["p_sb"][:, j_, :],
                            bufs["masks"][:, j_, :], op=ALU.mult), sMK)
                p.wait("t", sE)
                if diag:
                    p.wait("t", sMK)
                if r == 0:
                    p.wait("t", sOE)
                for j in range(4):
                    p.op("t", lambda e, j_=j, r_=r, nr=nreg: e.matmul(
                        pb[5][0:1, :], C.ones_col_bf[:, :], bufs["p_sb"][:, j_, :],
                        start=(r_ == 0 and j_ == 0),
                        stop=(r_ == nr - 1 and j_ == 3)), sAV)
                vsl = slice(0, 128) if layer == 0 else slice(64 * (h % 2), 64 * (h % 2) + 64)
                osl = slice(0, 128) if layer == 0 else slice(0, 64)
                for j in range(4):
                    p.op("t", lambda e, j_=j, kg_=kg, r_=r, nr=nreg, v_=vsl, o_=osl: e.matmul(
                        pb[6][o_, :], vhb[:, kg_ // 128 + j_, v_],
                        bufs["p_sb"][:, j_, :],
                        start=(r_ == 0 and j_ == 0), stop=(r_ == nr - 1 and j_ == 3)), sAV)
            sAV_c = p.cnt[sAV]
            p.wait("v", sAV, sAV_c)
            p.op("v", lambda e: e.reciprocal(bufs["rrow"][0:1, :], pb[5][0:1, :]), sOE)
            rr_c = p.cnt[sOE]
            p.wait("t", sOE, rr_c)
            p.op("t", lambda e: e.matmul(pb[7][:, :], C.ones1_f32[:, :],
                 bufs["rrow"][0:1, :], start=True, stop=True), sOE)
            p.wait("v", sOE)
            p.op("v", lambda e: e.tensor_copy(bufs["rb_sb"][:, :], pb[7][:, :]), sOE)
            p.wait("v", sOE)
            osl = slice(0, 128) if layer == 0 else slice(0, 64)
            if layer == 0:
                p.op("v", lambda e, g_=g, hc_=hc: e.tensor_tensor(
                    o_all[:, hc_, g_ * 512:(g_ + 1) * 512],
                    pb[6][:, :], bufs["rb_sb"][:, :], op=ALU.mult), sOE)
            else:
                p.wait("v", f"{tag}_ost")
                p.op("v", lambda e, o_=osl: e.tensor_tensor(
                    bufs["yc0"].bitcast(BF16)[o_, 0:512],
                    pb[6][o_, :], bufs["rb_sb"][o_, :], op=ALU.mult), sOE)
                p.wait("g", sOE)
                p.dma("g", o_all[64 * (h % 2):64 * (h % 2) + 64, hc,
                                 g * 512:(g + 1) * 512],
                      bufs["yc0"].bitcast(BF16)[0:64, 0:512], f"{tag}_ost")
    oall_done = p.cnt[sOE]
    ost_done = p.cnt.get(f"{tag}_ost", 0)

    oproj, resid, xcl = f"{tag}_op", f"{tag}_res", f"{tag}_xc"
    yst_hist = []
    for g in range(4):
        for mc in range(NCH):
            idx = g * NCH + mc
            xc = bufs["xc0"] if idx % 2 == 0 else bufs["xc1"]
            yc = bufs["yc0"] if idx % 2 == 0 else bufs["yc1"]
            if idx >= 2:
                p.wait("sy", resid, p.cnt.get(resid, 0) - 1)
            p.dma("sy", xc[:, :], x_d[mc * 128:(mc + 1) * 128,
                                      g * 512:(g + 1) * 512], xcl)
            xcl_c = p.cnt[xcl]
            p.wait("t", sOE, oall_done)
            if layer == 1:
                p.wait("t", f"{tag}_ost", ost_done)
            if g * NCH + mc >= 2:
                p.wait("t", resid, p.cnt.get(resid, 0) - 1)
            bnk = pb[0] if (g * NCH + mc) % 2 == 0 else pb[1]
            for hd in range(NCH):
                p.op("t", lambda e, hd_=hd, mc_=mc, g_=g, bk=bnk: e.matmul(
                    bk[:, :], woT[:, hd_, mc_ * 128:(mc_ + 1) * 128],
                    o_all[:, hd_, g_ * 512:(g_ + 1) * 512],
                    start=(hd_ == 0), stop=(hd_ == 7)), oproj)
            op_c = p.cnt[oproj]
            p.wait("v", oproj, op_c)
            p.wait("v", xcl, xcl_c)
            ystp = f"{tag}_yst" + str(idx % 2)
            if idx >= 2:
                p.wait("v", ystp, yst_hist[idx - 2])
            if layer == 1 and idx == 0:
                p.wait("v", f"{tag}_ost", ost_done)
            p.op("v", lambda e, bk=bnk, xc_=xc, yc_=yc: e.tensor_tensor(
                yc_[:, :], xc_[:, :], bk[:, :], op=ALU.add), resid)
            p.wait("sy", resid)
            p.dma("sy", y_d[mc * 128:(mc + 1) * 128, g * 512:(g + 1) * 512],
                  yc[:, :], ystp)
            yst_hist.append(p.cnt[ystp])
    p.wait("sy", f"{tag}_yst0")
    p.wait("sy", f"{tag}_yst1")
    return None


NTILE = 47


def moe_bufs(p, tag):
    ar = p.sb(f"{tag}_ar", [128, 20480], BF16)
    b = {
        "ar": ar,
        "xln": p.sb(f"{tag}_xln", [128, NCH, TPC], BF16),
        "xs_all": p.sb(f"{tag}_xs", [128, 16, 1024], BF16),
        "gw": p.sb(f"{tag}_gw", [128, NCH, 32], BF16),
        "oh_sb": p.sb(f"{tag}_oh", [128, 16, 32], F32),
        "rank_sb": p.sb(f"{tag}_rank", [128, 16, 32], F32),
        "cs_sb": p.sb(f"{tag}_cs", [1, 16, 32], F32),
        "score": p.sb(f"{tag}_scr", [128, 16], F32),
        "maxv": p.sb(f"{tag}_maxv", [128, 1], F32),
        "eid": p.sb(f"{tag}_eid", [128, 1], F32),
        "eq": p.sb(f"{tag}_eq", [128, 32], F32),
        "sel": p.sb(f"{tag}_sel", [128, 32], F32),
        "imb": p.sb(f"{tag}_imb", [128, 32], F32),
        "cnt_i": p.sb(f"{tag}_cnti", [32, 1], I32),
        "pad_f": p.sb(f"{tag}_padf", [32, 1], F32),
        "cmp": p.sb(f"{tag}_cmp", [32, NTILE], F32),
        "erow": p.sb(f"{tag}_erow", [1, NTILE], F32),
        "esml": p.sb(f"{tag}_esml", [1, NTILE], F32),
        "eoff": p.sb(f"{tag}_eoff", [1, 2], F32),
        "base_row": p.sb(f"{tag}_base", [1, 32], F32),
        "t2": p.sb(f"{tag}_t2", [128, 32], F32),
        "t2r": p.sb(f"{tag}_t2r", [1, 32], F32),
        "posu": p.sb(f"{tag}_posu", [128, 16], U32),
        "posf": p.sb(f"{tag}_posf", [128, 16], F32),
        "idxg": p.sb(f"{tag}_idxg", [128, 8], U32),
        "idxd": p.sb(f"{tag}_idxd", [128, 16], U32),
        "idgf": p.sb(f"{tag}_idgf", [128, 8], F32),
        "iddf": p.sb(f"{tag}_iddf", [128, 16], F32),
        "kcoff": p.sb(f"{tag}_kcoff", [128, 8], F32),
        "ichoff": p.sb(f"{tag}_ichoff", [128, 16], F32),
        "xsf0": p.sb(f"{tag}_xsf0", [128, NCH, 128], BF16),
        "xsf1": p.sb(f"{tag}_xsf1", [128, NCH, 128], BF16),
        "dsb": p.sb(f"{tag}_dsb", [128, 1024], BF16),
        "rtc": p.sb(f"{tag}_rtc", [128, 128], F32),
        "lns": ln_scratch(p, f"{tag}_lns"),
    }
    b["wg0"] = ar[:, 0:2048];      b["wg1"] = ar[:, 2048:4096]
    b["wu0"] = ar[:, 4096:6144];   b["wu1"] = ar[:, 6144:8192]
    b["wd0"] = ar[:, 8192:9216];   b["wd1"] = ar[:, 9216:10240]
    b["gs"] = r3(ar[:, 10240:12288], 4)
    b["us"] = r3(ar[:, 12288:14336], 4)
    b["hh"] = r3(ar[:, 14336:16384], 4)
    b["hT"] = r3(ar[:, 16384:18432], 16)
    b["xs_t0"] = ar[:, 18432:19456]
    b["xs_t1"] = ar[:, 19456:20480]
    b["sgwb0"] = r3(ar[:, 0:1024], NCH);   b["suwb0"] = r3(ar[:, 1024:2048], NCH)
    b["sgwb1"] = r3(ar[:, 2048:3072], NCH); b["suwb1"] = r3(ar[:, 3072:4096], NCH)
    b["sdwb0"] = r3(ar[:, 4096:6144], 16)
    b["sdwb1"] = r3(ar[:, 6144:8192], 16)
    b["h_sh"] = r3(ar[:, 10240:18432], 16)
    b["gsh"] = ar[:, 18432:18944]
    b["ush"] = ar[:, 18944:19456]
    return b


def emit_moe(p, C, tag, x_sb, gwd, sgw_d, suw_d, sdw_d,
             rgw_d, ruw_d, rdw_d, xs_sorted, dout, pb, bufs):
    xln = bufs["xln"]
    lns = bufs["lns"]
    ysh = r3(bufs["xs_all"][:, :, :].rearrange("p a b -> p (a b)"), NCH)
    rt, sc_ld = f"{tag}_rt", f"{tag}_scld"

    for c in range(NCH):
        p.dma("g", bufs["gw"][:, c, :], gwd[c * 128:(c + 1) * 128, :], sc_ld)
    sc_done = p.cnt[sc_ld]
    for e in ("t", "v", "s", "g"):
        p.wait(e, sc_ld, sc_done)
        p.wait(e, "c_ld")
        p.wait(e, "xld")
    p.op("v", lambda e: e.tensor_scalar(bufs["imb"][:, :], C.iota32_f[:, :], -65536.0, None, op0=ALU.add), rt)
    p.op("v", lambda e: e.memset(bufs["base_row"][:, :], 0.0), rt)
    for k in range(8):
        p.op("v", lambda e, k_=k: e.tensor_scalar(bufs["kcoff"][:, k_:k_ + 1], C.iota128_f[:, :], 128.0 * k_, None, op0=ALU.add), rt)
    for k in range(16):
        p.op("v", lambda e, k_=k: e.tensor_scalar(bufs["ichoff"][:, k_:k_ + 1], C.iota128_f[:, :], 128.0 * k_, None, op0=ALU.add), rt)

    lgm, ohm, sct = f"{tag}_lg", f"{tag}_ohm", f"{tag}_sct"
    cntm, rkm, rke = f"{tag}_cntm", f"{tag}_rkm", f"{tag}_rke"
    for tt in range(4):
        nm = emit_ln(p, C, "Gln", x_sb, xln, tt * 512, 512, pb[0:4], lns)
        for sub4 in range(4):
            s = tt * 4 + sub4
            tsl = slice(tt * 512 + sub4 * 128, tt * 512 + sub4 * 128 + 128)
            p.wait("t", nm)
            p.wait("t", rt)
            p.wait("t", sct)
            for c in range(NCH):
                p.op("t", lambda e, c_=c, ts_=tsl: e.matmul(
                    pb[4][:, 0:32], xln[:, c_, ts_], bufs["gw"][:, c_, :],
                    start=(c_ == 0), stop=(c_ == 7)), lgm)
            lg_c = p.cnt[lgm]
            p.wait("v", lgm, lg_c)
            p.op("v", lambda e: e.reduce_max(bufs["maxv"][:, :], pb[4][:, 0:32], axis=AX), rt)
            p.wait("v", rt)
            p.op("v", lambda e: e.tensor_scalar(bufs["eq"][:, :], pb[4][:, 0:32], bufs["maxv"][:, 0:1], None, op0=ALU.is_equal), rt)
            p.wait("v", rt)
            p.op("v", lambda e: e.tensor_tensor(bufs["sel"][:, :], bufs["eq"][:, :], bufs["imb"][:, :], op=ALU.mult), rt)
            p.wait("v", rt)
            p.op("v", lambda e: e.tensor_scalar(bufs["sel"][:, :], bufs["sel"][:, :], 65536.0, None, op0=ALU.add), rt)
            p.wait("v", rt)
            p.op("v", lambda e: e.tensor_reduce(bufs["eid"][:, :], bufs["sel"][:, :], axis=AX, op=ALU.min), rt)
            p.wait("v", rt)
            p.op("v", lambda e, s_=s: e.tensor_scalar(bufs["oh_sb"][:, s_, :], C.iota32_f[:, :], bufs["eid"][:, 0:1], None, op0=ALU.is_equal), rt)
            rt_oh = p.cnt[rt]
            p.wait("s", rt, rt_oh)
            p.op("s", lambda e, s_=s: e.activation(bufs["score"][:, s_:s_ + 1], bufs["maxv"][:, :], AF.Sigmoid), ohm)
            score_c = p.cnt[ohm]
            p.wait("t", rt, rt_oh)
            p.op("t", lambda e, s_=s: e.matmul(
                pb[5][0:32, 0:1], bufs["oh_sb"][:, s_, :], C.ones_col_f32[:, :],
                start=(s_ == 0), stop=(s_ == 15)), cntm)
            p.wait("t", rke)
            p.op("t", lambda e, s_=s: e.matmul(
                pb[6][:, 0:32], C.tri128[:, :], bufs["oh_sb"][:, s_, :],
                start=True, stop=True), rkm)
            p.op("t", lambda e, s_=s: e.matmul(
                pb[7][0:1, 0:32], C.ones_col_f32[:, :], bufs["oh_sb"][:, s_, :],
                start=True, stop=True), rkm)
            rkm_c = p.cnt[rkm]
            p.wait("v", rkm, rkm_c)
            p.op("v", lambda e, s_=s: e.tensor_copy(bufs["rank_sb"][:, s_, :], pb[6][:, 0:32]), rke)
            p.op("v", lambda e, s_=s: e.tensor_copy(bufs["cs_sb"][0:1, s_, :], pb[7][0:1, 0:32]), rke)
            pTa = pb[4].bitcast(BF16)
            p.wait("t", rt, rt_oh)
            for c in range(NCH):
                p.op("t", lambda e, c_=c, ts_=tsl, pT_=pTa: e.transpose(
                    pT_[:, c_ * 128:(c_ + 1) * 128], xln[:, c_, ts_],
                    C.ident_bf[:, :]), sct)
            sct_c = p.cnt[sct]
            p.wait("v", sct, sct_c)
            p.wait("v", ohm, score_c)
            p.op("v", lambda e, s_=s, pT_=pTa: e.tensor_scalar(
                bufs["xs_all"][:, s_, :], pT_[:, 0:1024], bufs["score"][:, s_:s_ + 1],
                None, op0=ALU.mult), sct)
    cnt_done = p.cnt[cntm]
    xln_done = p.cnt["Gln_nm"]

    a3, posb = f"{tag}_a3", f"{tag}_posb"
    p.wait("v", cntm, cnt_done)
    p.op("v", lambda e: e.tensor_copy(bufs["cnt_i"][:, :], pb[5][0:32, 0:1]), a3)
    p.wait("v", a3)
    p.op("v", lambda e: e.tensor_scalar(bufs["cnt_i"][:, :], bufs["cnt_i"][:, :], 127.0, None, op0=ALU.add), a3)
    p.wait("v", a3)
    p.op("v", lambda e: e.tensor_scalar(bufs["cnt_i"][:, :], bufs["cnt_i"][:, :], 7, None, op0=ALU.arith_shift_right), a3)
    p.wait("v", a3)
    p.op("v", lambda e: e.tensor_scalar(bufs["cnt_i"][:, :], bufs["cnt_i"][:, :], 7, None, op0=ALU.arith_shift_left), a3)
    p.wait("v", a3)
    p.op("v", lambda e: e.tensor_copy(bufs["pad_f"][:, :], bufs["cnt_i"][:, :]), a3)
    a3_pad = p.cnt[a3]
    p.wait("t", a3, a3_pad)
    p.wait("t", sct)
    p.op("t", lambda e: e.matmul(pb[4][0:32, 0:1], C.tri32[:, :], bufs["pad_f"][:, :], start=True, stop=True), a3)
    po_c = p.cnt[a3]
    p.wait("v", a3, po_c)
    p.op("v", lambda e: e.tensor_scalar(bufs["cmp"][:, :], C.tbases[:, 0:NTILE], pb[4][0:32, 0:1], None, op0=ALU.is_ge), a3)
    cmp_c = p.cnt[a3]
    p.op("v", lambda e: e.tensor_copy(bufs["pad_f"][:, :], pb[4][0:32, 0:1]), a3)
    pocp = p.cnt[a3]
    p.wait("t", a3, cmp_c)
    p.wait("t", a3, pocp)
    p.op("t", lambda e: e.matmul(pb[6][0:1, 0:NTILE], C.ones_col_f32[0:32, :], bufs["cmp"][:, :], start=True, stop=True), a3)
    p.op("t", lambda e: e.transpose(pb[7][0:1, 0:32], bufs["pad_f"][:, :], C.ident_f32[0:32, 0:32]), a3)
    por_c = p.cnt[a3]
    p.wait("v", a3, por_c)
    p.op("v", lambda e: e.tensor_scalar(bufs["erow"][0:1, :], pb[6][0:1, 0:NTILE], -1.0, None, op0=ALU.add), a3)
    p.op("v", lambda e: e.tensor_copy(bufs["t2r"][0:1, :], pb[7][0:1, 0:32]), a3)
    a3_er = p.cnt[a3]
    p.wait("t", a3, a3_er)
    p.op("t", lambda e: e.matmul(pb[5][0:1, 0:1], bufs["pad_f"][:, :], C.ones_col_f32[0:32, 0:1], start=True, stop=True), a3)
    tot_c = p.cnt[a3]
    p.wait("v", a3, tot_c)
    p.op("v", lambda e: e.tensor_scalar(bufs["esml"][0:1, :], C.tbrow[0:1, 0:NTILE], pb[5][0:1, 0:1], None, op0=ALU.is_ge), a3)
    p.wait("v", a3)
    p.op("v", lambda e: e.tensor_scalar(bufs["esml"][0:1, :], bufs["esml"][0:1, :], 3e9, None, op0=ALU.mult), a3)
    a3_done = p.cnt[a3]

    pos = f"{tag}_pos"
    for s in range(16):
        p.wait("v", pos)
        p.wait("v", a3, a3_done)
        p.op("v", lambda e: e.tensor_tensor(bufs["t2"][0:1, 0:32], bufs["t2r"][0:1, :], bufs["base_row"][0:1, :], op=ALU.add), pos)
        pc = p.cnt[pos]
        p.wait("t", pos, pc)
        p.wait("t", posb)
        p.op("t", lambda e: e.matmul(pb[6][:, 0:32], C.ones1_f32[:, :], bufs["t2"][0:1, 0:32], start=True, stop=True), posb)
        pb_c = p.cnt[posb]
        p.wait("v", posb, pb_c)
        p.op("v", lambda e, s_=s: e.tensor_tensor(bufs["t2"][:, 0:32], bufs["rank_sb"][:, s_, :], pb[6][:, 0:32], op=ALU.add), pos)
        p.wait("v", pos)
        p.op("v", lambda e, s_=s: e.tensor_tensor(bufs["t2"][:, 0:32], bufs["t2"][:, 0:32], bufs["oh_sb"][:, s_, :], op=ALU.mult), pos)
        p.wait("v", pos)
        p.op("v", lambda e, s_=s: e.reduce_sum(bufs["posf"][:, s_:s_ + 1], bufs["t2"][:, 0:32], axis=AX), pos)
        p.wait("v", pos)
        p.op("v", lambda e, s_=s: e.tensor_copy(bufs["posu"][:, s_:s_ + 1], bufs["posf"][:, s_:s_ + 1]), pos)
        p.op("v", lambda e, s_=s: e.tensor_tensor(bufs["base_row"][0:1, :], bufs["base_row"][0:1, :], bufs["cs_sb"][0:1, s_, :], op=ALU.add), pos)
    pos_done = p.cnt[pos]

    scat = f"{tag}_scat"
    zs = f"{tag}_zs"
    p.op("g", lambda e: e.memset(bufs["dsb"][:, :], 0.0), zs + "m")
    p.wait("g", zs + "m")
    for jz in range(NTILE):
        p.dma("g", xs_sorted[jz * 128:(jz + 1) * 128, :], bufs["dsb"][:, :], zs)
    p.wait("g", zs)
    p.wait("g", pos, pos_done)
    for s in range(16):
        p.op("g", lambda e, s_=s: e.indirect_dma_start(
            out=xs_sorted[:, :],
            out_offset=IndirectOffsetOnAxis(ap=bufs["posu"][:, s_:s_ + 1], axis=0),
            in_=bufs["xs_all"][:, s_, :], in_offset=None,
            oob_is_err=False), scat, 16)
    scat_done = p.cnt[scat]

    gth, xsg, rmm, rev, rhm, rtp, rdm, rde, rst, idxs = (
        f"{tag}_gth", f"{tag}_xsg", f"{tag}_rmm", f"{tag}_rev", f"{tag}_rhm",
        f"{tag}_rtp", f"{tag}_rdm", f"{tag}_rde", f"{tag}_rst", f"{tag}_idx")
    if not hasattr(p, "_moe_regs"):
        p._moe_regs = {}

        def _init_regs(e, R=p._moe_regs):
            R["bc_g"] = e.alloc_register("moe_bc_g")
            e.reg_mov(R["bc_g"], 32767)
            R["bc_d"] = e.alloc_register("moe_bc_d")
            e.reg_mov(R["bc_d"], 65535)
        p.emit("g", _init_regs)
    REGS = p._moe_regs
    rmm_hist = {}
    rtp_hist = {}
    wuse = {}
    duse = {}
    for j in range(NTILE):
        par = j % 2
        wg = bufs["wg0"] if par == 0 else bufs["wg1"]
        wu = bufs["wu0"] if par == 0 else bufs["wu1"]
        wd = bufs["wd0"] if par == 0 else bufs["wd1"]
        xst = bufs["xs_t0"] if par == 0 else bufs["xs_t1"]
        xsf = bufs["xsf0"] if par == 0 else bufs["xsf1"]
        gthp, xsgp = gth + str(par), xsg + str(par)
        p.wait("v", a3, a3_done)
        p.wait("v", posb)
        p.op("v", lambda e, j_=j: e.tensor_scalar(
            bufs["eoff"][0:1, 0:1], bufs["erow"][0:1, j_:j_ + 1], 1024.0,
            bufs["esml"][0:1, j_:j_ + 1], op0=ALU.mult, op1=ALU.add), idxs)
        p.op("v", lambda e, j_=j: e.tensor_scalar(
            bufs["eoff"][0:1, 1:2], bufs["erow"][0:1, j_:j_ + 1], 2048.0,
            bufs["esml"][0:1, j_:j_ + 1], op0=ALU.mult, op1=ALU.add), idxs)
        ei_c = p.cnt[idxs]
        p.wait("t", idxs, ei_c)
        p.wait("t", posb)
        p.op("t", lambda e: e.matmul(pb[6][:, 0:2], C.ones1_f32[:, :], bufs["eoff"][0:1, 0:2], start=True, stop=True), posb)
        eb_c = p.cnt[posb]
        p.wait("v", posb, eb_c)
        for gs_ in range(4):
            p.wait("v", gth + str(gs_))
        p.wait("v", gth + "d0")
        p.wait("v", gth + "d1")
        p.op("v", lambda e: e.tensor_scalar(bufs["idgf"][:, :], bufs["kcoff"][:, :], pb[6][:, 0:1], None, op0=ALU.add), idxs)
        p.op("v", lambda e: e.tensor_scalar(bufs["iddf"][:, :], bufs["ichoff"][:, :], pb[6][:, 1:2], None, op0=ALU.add), idxs)
        p.wait("v", idxs)
        p.op("v", lambda e: e.tensor_copy(bufs["idxg"][:, :], bufs["idgf"][:, :]), idxs)
        p.op("v", lambda e: e.tensor_copy(bufs["idxd"][:, :], bufs["iddf"][:, :]), idxs)
        idx_c = p.cnt[idxs]
        p.wait("g", idxs, idx_c)
        p.wait("g", scat, scat_done)
        p.wait("g", rtp, rtp_hist.get(j - 2, 0))
        p.dma("g", xst[:, :], xs_sorted[j * 128:(j + 1) * 128, :], xsgp)
        xsg_c = p.cnt[xsgp]
        p.wait("t", xsgp, xsg_c)
        p.wait("t", rde)
        pTr = pb[0].bitcast(BF16)
        for c in range(NCH):
            p.op("t", lambda e, c_=c, x_=xst, pT_=pTr: e.transpose(
                pT_[:, c_ * 128:(c_ + 1) * 128],
                r3(x_[:, :], NCH)[:, c_, :], C.ident_bf[:, :]), rtp)
        rtp_c = p.cnt[rtp]
        p.wait("v", rtp, rtp_c)
        p.op("v", lambda e, x_=xsf, pT_=pTr: e.tensor_copy(
            x_[:, :, :], r3(pT_[:, 0:1024], NCH)), rtp)
        xsf_c = p.cnt[rtp]
        rtp_hist[j] = xsf_c
        p.wait("t", rtp, xsf_c)
        p.wait("t", rev)
        for half, (wsrc, w0, w1, pbase) in enumerate(
                [(rgw_d, bufs["wg0"], bufs["wg1"], 0),
                 (ruw_d, bufs["wu0"], bufs["wu1"], 4)]):
            for c in range(NCH):
                wpar = w0 if c % 2 == 0 else w1
                gsem = gth + str(half * 2 + c % 2)
                p.wait("g", rmm, wuse.get((j, half, c - 2), wuse.get((j - 1, half, 6 + c), 0)))
                p.op("g", lambda e, c_=c, w=wpar, src=wsrc: e.indirect_dma_start(
                    out=w[:, :], out_offset=None, in_=src[:, :],
                    in_offset=IndirectOffsetOnAxis(ap=bufs["idxg"][:, c_:c_ + 1], axis=0),
                    bounds_check=REGS["bc_g"], oob_is_err=False), gsem, 16)
                g_c = p.cnt[gsem]
                p.wait("t", gsem, g_c)
                for ig in range(4):
                    p.op("t", lambda e, ig_=ig, c_=c, x_=xsf, w=wpar, pb0=pbase: e.matmul(
                        pb[pb0 + ig_][:, :], x_[:, c_, :],
                        w[:, ig_ * 512:(ig_ + 1) * 512],
                        start=(c_ == 0), stop=(c_ == 7)), rmm)
                wuse[(j, half, c)] = p.cnt[rmm]
        rmm_c = p.cnt[rmm]
        rmm_hist[j] = rmm_c
        p.wait("s", rmm, rmm_c)
        p.wait("s", rhm)
        for ig in range(4):
            p.op("s", lambda e, ig_=ig: e.activation(bufs["gs"][:, ig_, :], pb[ig_][:, :], AF.Silu), rev)
            p.op("s", lambda e, ig_=ig: e.copy(bufs["us"][:, ig_, :], pb[4 + ig_][:, :]), rev)
        rev_c = p.cnt[rev]
        p.wait("v", rev, rev_c)
        p.wait("v", rtp)
        for ig in range(4):
            p.op("v", lambda e, ig_=ig: e.tensor_tensor(bufs["hh"][:, ig_, :], bufs["gs"][:, ig_, :], bufs["us"][:, ig_, :], op=ALU.mult), rhm)
        rhm_c = p.cnt[rhm]
        p.wait("t", rhm, rhm_c)
        p.wait("t", rdm)
        pT1r = pb[1].bitcast(BF16)
        for q in range(4):
            for w4 in range(4):
                icc = q * 4 + w4
                p.op("t", lambda e, icc_=icc, w4_=w4, pT_=pT1r: e.transpose(
                    pT_[:, w4_ * 128:(w4_ + 1) * 128],
                    bufs["hh"][:, icc_ // 4, (icc_ % 4) * 128:(icc_ % 4 + 1) * 128],
                    C.ident_bf[:, :]), rtp)
            qc = p.cnt[rtp]
            p.wait("v", rtp, qc)
            p.op("v", lambda e, q_=q, pT_=pT1r: e.tensor_copy(
                bufs["hT"][:, 4 * q_:4 * q_ + 4, :], r3(pT_[:, 0:512], 4)), rtp)
            hTq = p.cnt[rtp]
            p.wait("t", rtp, hTq)
        hT_c = p.cnt[rtp]
        p.wait("t", rtp, hT_c)
        p.wait("t", rde)
        for ic in range(16):
            wdp = bufs["wd0"] if ic % 2 == 0 else bufs["wd1"]
            dsem = gth + "d" + str(ic % 2)
            p.wait("g", rdm, duse.get((j, ic - 2), duse.get((j - 1, 14 + ic), 0)))
            p.op("g", lambda e, ic_=ic, w=wdp: e.indirect_dma_start(
                out=w[:, :], out_offset=None, in_=rdw_d[:, :],
                in_offset=IndirectOffsetOnAxis(ap=bufs["idxd"][:, ic_:ic_ + 1], axis=0),
                bounds_check=REGS["bc_d"], oob_is_err=False), dsem, 16)
            d_c = p.cnt[dsem]
            p.wait("t", dsem, d_c)
            for dg in range(2):
                p.op("t", lambda e, ic_=ic, dg_=dg, w=wdp: e.matmul(
                    pb[2 + dg_][:, :], bufs["hT"][:, ic_, :],
                    w[:, dg_ * 512:(dg_ + 1) * 512],
                    start=(ic_ == 0), stop=(ic_ == 15)), rdm)
            duse[(j, ic)] = p.cnt[rdm]
        rdm_c = p.cnt[rdm]
        p.wait("s", rdm, rdm_c)
        p.wait("s", rst)
        p.wait("s", f"{tag}_zs")
        p.op("s", lambda e: e.copy(bufs["dsb"][:, 0:512], pb[2][:, :]), rde)
        p.op("s", lambda e: e.copy(bufs["dsb"][:, 512:1024], pb[3][:, :]), rde)
        rde_c = p.cnt[rde]
        p.wait("g", rde, rde_c)
        p.dma("g", dout[j * 128:(j + 1) * 128, :], bufs["dsb"][:, :], rst)
    routed_done = p.cnt[rst]
    routed_mm_done = p.cnt[rdm]

    shw, shm, she, shh, shy = (f"{tag}_shw", f"{tag}_shm", f"{tag}_she",
                               f"{tag}_shh", f"{tag}_shy")
    shw_hist = {}
    for tt in range(4):
        ts_ = slice(tt * 512, (tt + 1) * 512)
        for ic in range(16):
            par = ic % 2
            wgs = bufs["sgwb0"] if par == 0 else bufs["sgwb1"]
            wus = bufs["suwb0"] if par == 0 else bufs["suwb1"]
            shwp = shw + str(par)
            p.wait("g", shm, shw_hist.get((tt, ic - 2), 0))
            if tt == 0 and ic < 2:
                p.wait("g", rmm, p.cnt[rmm])
                p.wait("g", rdm, routed_mm_done)
                p.wait("g", rtp, p.cnt[rtp])
                p.wait("g", scat, scat_done)
            p.dma("g", wgs[:, :, :], sgw_d[:, ic * 128:(ic + 1) * 128].rearrange(
                "(c p) m -> p c m", p=128), shwp)
            p.dma("g", wus[:, :, :], suw_d[:, ic * 128:(ic + 1) * 128].rearrange(
                "(c p) m -> p c m", p=128), shwp)
            shw_c = p.cnt[shwp]
            gb_ = pb[4 + par]
            ub_ = pb[6 + par]
            p.wait("t", shwp, shw_c)
            p.wait("t", "Gln_nm", xln_done)
            p.wait("t", she)
            for c in range(NCH):
                p.op("t", lambda e, c_=c, w=wgs, ts2=ts_, b=gb_: e.matmul(
                    b[:, :], w[:, c_, :], xln[:, c_, ts2],
                    start=(c_ == 0), stop=(c_ == 7)), shm)
            for c in range(NCH):
                p.op("t", lambda e, c_=c, w=wus, ts2=ts_, b=ub_: e.matmul(
                    b[:, :], w[:, c_, :], xln[:, c_, ts2],
                    start=(c_ == 0), stop=(c_ == 7)), shm)
            shm_c = p.cnt[shm]
            shw_hist[(tt, ic)] = shm_c
            p.wait("s", shm, shm_c)
            p.wait("s", shh)
            p.op("s", lambda e, b=gb_: e.activation(bufs["gsh"][:, :], b[:, :], AF.Silu), she)
            p.op("s", lambda e, b=ub_: e.copy(bufs["ush"][:, :], b[:, :]), she)
            she_c = p.cnt[she]
            p.wait("v", she, she_c)
            p.op("v", lambda e, ic_=ic: e.tensor_tensor(bufs["h_sh"][:, ic_, :], bufs["gsh"][:, :], bufs["ush"][:, :], op=ALU.mult), shh)
        h_done = p.cnt[shh]
        for dm in range(NCH):
            par = dm % 2
            wds = bufs["sdwb0"] if par == 0 else bufs["sdwb1"]
            shwp = shw + "d" + str(par)
            p.wait("g", shm, shw_hist.get((tt, "d", dm - 2), 0))
            p.dma("g", wds[:, :, :], sdw_d[:, dm * 128:(dm + 1) * 128].rearrange(
                "(c p) m -> p c m", p=128), shwp)
            shwd_c = p.cnt[shwp]
            yb = pb[2 + par]
            p.wait("t", shwp, shwd_c)
            p.wait("t", shh, h_done)
            p.wait("t", shy)
            for ic in range(16):
                p.op("t", lambda e, ic_=ic, w=wds, b=yb: e.matmul(
                    b[:, :], w[:, ic_, :], bufs["h_sh"][:, ic_, :],
                    start=(ic_ == 0), stop=(ic_ == 15)), shm)
            y_c = p.cnt[shm]
            shw_hist[(tt, "d", dm)] = y_c
            p.wait("s", shm, y_c)
            p.wait("s", shy)
            p.op("s", lambda e, dm_=dm, ts2=ts_, b=yb: e.copy(ysh[:, dm_, ts2], b[:, :]), shy)
    ysh_done = p.cnt[shy]

    cg, ctp, cres = f"{tag}_cg", f"{tag}_ctp", f"{tag}_cres"
    rt0, rt1 = bufs["xs_t0"], bufs["xs_t1"]
    ctp_hist = {}
    for s in range(16):
        par = s % 2
        rtb = rt0 if par == 0 else rt1
        cgp = cg + str(par)
        tsl = slice(s * 128, (s + 1) * 128)
        p.wait("g", rst, routed_done)
        if s < 2:
            p.wait("g", shm, p.cnt[shm])
        p.wait("g", ctp, ctp_hist.get(s - 2, 0))
        p.op("g", lambda e, s_=s, b=rtb: e.indirect_dma_start(
            out=b[:, :], out_offset=None, in_=dout[:, :],
            in_offset=IndirectOffsetOnAxis(ap=bufs["posu"][:, s_:s_ + 1], axis=0),
            oob_is_err=False), cgp, 16)
        cg_c = p.cnt[cgp]
        p.wait("t", cgp, cg_c)
        p.wait("t", cres)
        p.wait("t", shy, ysh_done)
        pTc = pb[0].bitcast(BF16)
        for c in range(NCH):
            p.op("t", lambda e, c_=c, b=rtb, pT_=pTc: e.transpose(
                pT_[:, c_ * 128:(c_ + 1) * 128], r3(b[:, :], NCH)[:, c_, :],
                C.ident_bf[:, :]), ctp)
        ctp_c = p.cnt[ctp]
        ctp_hist[s] = ctp_c
        p.wait("v", ctp, ctp_c)
        for c in range(NCH):
            p.wait("v", cres)
            p.op("v", lambda e, c_=c, ts2=tsl, pT_=pTc: e.tensor_tensor(
                bufs["rtc"][:, :], pT_[:, c_ * 128:(c_ + 1) * 128], ysh[:, c_, ts2], op=ALU.add), cres)
            p.wait("v", cres)
            p.op("v", lambda e, c_=c, ts2=tsl: e.scalar_tensor_tensor(
                x_sb[:, c_, ts2], bufs["rtc"][:, :], 0.5, x_sb[:, c_, ts2],
                op0=ALU.mult, op1=ALU.add), cres)
    return cres


def build_moe():
    p = Prog("moe")
    nc = p.nc
    x_d = nc.dram_tensor("x_fm", [D, TPC], F32, kind="ExternalInput").ap()
    gwd = nc.dram_tensor("gate_w", [D, 32], F32, kind="ExternalInput").ap()
    sgw_d = nc.dram_tensor("sgw", [D, 2048], F32, kind="ExternalInput").ap()
    suw_d = nc.dram_tensor("suw", [D, 2048], F32, kind="ExternalInput").ap()
    sdw_d = nc.dram_tensor("sdw", [2048, D], F32, kind="ExternalInput").ap()
    rgw_d = nc.dram_tensor("rgw", [32 * 1024, 2048], F32, kind="ExternalInput").ap()
    ruw_d = nc.dram_tensor("ruw", [32 * 1024, 2048], F32, kind="ExternalInput").ap()
    rdw_d = nc.dram_tensor("rdw", [32 * 2048, 1024], F32, kind="ExternalInput").ap()
    y_d = nc.dram_tensor("y_fm", [D, TPC], F32, kind="ExternalOutput").ap()
    xs_sorted = nc.dram_tensor("xs_sorted", [NTILE * 128, D], BF16).ap()
    dout = nc.dram_tensor("dout", [NTILE * 128, D], BF16).ap()

    C = Consts(p)
    x_sb = p.sb("x_sb", [128, NCH, TPC], F32)
    pb = [p.psum(f"pb{i}", [128, 512], F32) for i in range(8)]
    bufs = moe_bufs(p, "m")
    for c in range(NCH):
        p.dma("sy", x_sb[:, c, :], x_d[c * 128:(c + 1) * 128, :], "xld")
    cres = emit_moe(p, C, "m", x_sb, gwd, sgw_d, suw_d, sdw_d,
                    rgw_d, ruw_d, rdw_d, xs_sorted, dout, pb, bufs)
    p.wait("sy", cres)
    for c in range(NCH):
        p.dma("sy", y_d[c * 128:(c + 1) * 128, :], x_sb[:, c, :], "yst")
    p.wait("sy", "yst")
    return p


def moe_inputs(core, x_fm, gate_w, sgw, suw, sdw, rgw, ruw, rdw):
    cd = make_consts()
    return dict(
        x_fm=np.ascontiguousarray(x_fm),
        gate_w=np.ascontiguousarray(gate_w),
        sgw=np.ascontiguousarray(sgw), suw=np.ascontiguousarray(suw),
        sdw=np.ascontiguousarray(sdw),
        rgw=np.ascontiguousarray(rgw.reshape(32 * 1024, 2048)),
        ruw=np.ascontiguousarray(ruw.reshape(32 * 1024, 2048)),
        rdw=np.ascontiguousarray(rdw.reshape(32 * 2048, 1024)),
        **cd,
    )


def build_attn(layer, rr):
    p = Prog(f"attn{layer}_{rr}")
    nc = p.nc
    x_d = nc.dram_tensor("x_fm", [D, TPC], F32, kind="ExternalInput").ap()
    xfull_d = nc.dram_tensor("xfull_fm", [D, S], F32, kind="ExternalInput").ap()
    wq_d = nc.dram_tensor("wq", [D, D], F32, kind="ExternalInput").ap()
    wk_d = nc.dram_tensor("wk", [D, D], F32, kind="ExternalInput").ap()
    wv_d = nc.dram_tensor("wv", [D, D], F32, kind="ExternalInput").ap()
    woT_d = nc.dram_tensor("woT", [D, D], F32, kind="ExternalInput").ap()
    cs_d = nc.dram_tensor("csP", [S, D], BF16, kind="ExternalInput").ap()
    masks_d = nc.dram_tensor("masksT", [4, 128, 512], BF16, kind="ExternalInput").ap()
    sccol_d = nc.dram_tensor("sccol", [128, 1], F32, kind="ExternalInput").ap()
    y_d = nc.dram_tensor("y_fm", [D, TPC], F32, kind="ExternalOutput").ap()
    kbuf = nc.dram_tensor("kbuf", [S, D], BF16).ap()
    vbuf = nc.dram_tensor("vbuf", [S, D], BF16).ap()
    qbuf = nc.dram_tensor("qbuf", [TPC, D], BF16).ap()

    C = Consts(p)
    pb = [p.psum(f"pb{i}", [128, 512], F32) for i in range(8)]
    bufs = attn_bufs(p, "at")
    emit_attn(p, C, "at", layer, rr, xfull_d, x_d, y_d, wq_d, wk_d, wv_d,
              woT_d, cs_d, masks_d, sccol_d, kbuf, vbuf, qbuf, pb, bufs)
    return p


def rope_table():
    half = 64
    dims = np.arange(half, dtype=np.float32)
    ang = np.power(np.float32(10000.0), -dims / np.float32(half))
    radians = np.arange(S, dtype=np.float32)[:, None] * ang[None, :]
    cos = np.cos(radians).astype(np.float32)
    sin = np.sin(radians).astype(np.float32)
    cs = np.concatenate([cos, sin], axis=1)
    csP = np.broadcast_to(cs[:, None, :], (S, 8, 128)).reshape(S, 1024)
    return np.ascontiguousarray(csP.astype(BF16NP))


def attn_masks():
    m = np.zeros((4, 128, 512), np.float32)
    kk = np.arange(128)[:, None]
    qq = np.arange(512)[None, :]
    for j in range(4):
        m[j] = (qq >= kk + 128 * j)
    return m.astype(BF16NP)


def attn_inputs(core, layer, x_fm, x_full, wq, wk, wv, wo):
    b = core // 4
    cd = make_consts()
    ap = attn_params(layer)
    if layer == 0:
        wqr = wq.transpose(1, 0, 2).reshape(D, D)
        wkr = wk.transpose(1, 0, 2).reshape(D, D)
        wvr = wv.transpose(1, 0, 2).reshape(D, D)
        woTr = wo.reshape(D, D)
    else:
        wqr = wq.transpose(2, 0, 1).reshape(D, D)
        wkr = wk.transpose(2, 0, 1).reshape(D, D)
        wvr = wv.transpose(2, 0, 1).reshape(D, D)
        woTr = wo.T
    xfull_fm = np.ascontiguousarray(x_full[b].T)
    return dict(
        x_fm=np.ascontiguousarray(x_fm),
        xfull_fm=xfull_fm,
        wq=np.ascontiguousarray(wqr), wk=np.ascontiguousarray(wkr),
        wv=np.ascontiguousarray(wvr), woT=np.ascontiguousarray(woTr),
        csP=rope_table(), masksT=attn_masks(),
        sccol=np.full((128, 1), ap["inv_scale"], np.float32),
        **cd,
    )



KVROWS = 2052


def attn_fused_bufs(p, tag, A1, A2, A3):
    b = {
        "xfl0": r3(A1[:, 0:4096], NCH),
        "xfl1": r3(A1[:, 4096:8192], NCH),
        "csb": A1[:, 8192:9216],
        "csb2": A1[:, 9216:10240],
        "q_sb": r3(A1[:, 0:16384], NCH),
        "khb": A3[:, 0:8192],
        "vhb": r3(A3[:, 8192:16384], 64),
        "kout": r3(A3[:, 16384:17408], NCH),
        "vout": r3(A3[:, 17408:18432], NCH),
        "qout": r3(A3[:, 18432:19456], NCH),
        "o_all": r3(A2[:, 0:16384], NCH),
        "wod0": r3(A3[:, 0:1024], NCH),
        "wod1": r3(A3[:, 1024:2048], NCH),
    }
    for name, shape, dt in [
        ("rt1", [128, 4, 64], F32), ("rt2", [128, 4, 64], F32),
        ("rt3", [128, 4, 64], F32), ("rt4", [128, 4, 64], F32),
        ("ksq", [128, NCH, 128], BF16), ("kn", [128, 8], F32),
        ("knacc", [128, 8], F32), ("qnacc", [128, 8], F32),
        ("kng", [128, 8], F32),
        ("brow", [1, 16], F32), ("negb", [128, 1], F32),
        ("sccol", [128, 1], F32),
    ]:
        b[name] = p.sb(f"{tag}_{name}", shape, dt)
    b["masks"] = r3(A3[:, 16384:18432], 4)
    b["p_sb"] = r3(A3[:, 18432:20480], 4)
    return b


def emit_attn_fused(p, C, tag, layer, x_sb, pb, bufs, wq_d, wk_d, wv_d,
                    woT_d, cs_d, masks_d, sccol_d, kvloc, kvg, qbuf,
                    rr_sb, entry, lns):
    ap = attn_params(layer)
    NH = ap["nheads"]
    kout, vout, qout = bufs["kout"], bufs["vout"], bufs["qout"]
    q_sb, khb, vhb, o_all = (bufs["q_sb"], bufs["khb"], bufs["vhb"],
                             bufs["o_all"])
    bufs["rrow"] = lns["rtmp"]
    bufs["rb_sb"] = lns["nt"]
    bufs["ost"] = lns["nt2"].bitcast(BF16)[:, 0:512]
    wld, csld = f"{tag}_wld", f"{tag}_cs"
    kvm, rpd, kvev, knsm, kst = (f"{tag}_kvm", f"{tag}_rpd", f"{tag}_kvev",
                                 f"{tag}_knsm", f"{tag}_kst")
    for e in ENGINES:
        p.wait(e, entry[0], entry[1])
    wq = r3(bufs["o_all"][:, :, :].rearrange("p a b -> p (a b)")[:, 0:8192], NCH)
    wk = r3(bufs["o_all"][:, :, :].rearrange("p a b -> p (a b)")[:, 8192:16384], NCH)
    wv = r3(bufs["khb"][:, 0:8192], NCH)
    for c in range(NCH):
        p.dma("g", wq[:, c, :], wq_d[c * 128:(c + 1) * 128, :], wld)
        p.dma("g", wk[:, c, :], wk_d[c * 128:(c + 1) * 128, :], wld)
        p.dma("g", wv[:, c, :], wv_d[c * 128:(c + 1) * 128, :], wld)
    p.dma("sy", bufs["sccol"][:, :], sccol_d[:, :], f"{tag}_scl")
    wld_done = p.cnt[wld]
    for e in ("t", "v", "s"):
        p.wait(e, wld, wld_done)
        p.wait(e, f"{tag}_scl")
    p.op("v", lambda e: e.memset(bufs["knacc"][:, :], 0.0), knsm)
    p.op("v", lambda e: e.memset(bufs["qnacc"][:, :], 0.0), knsm)
    p.op("g", lambda e: e.memset(bufs["ksq"].rearrange("p a b -> p (a b)"), 0.0),
         knsm)
    p.wait("g", knsm)
    p.dma("g", kvloc[2048:2052, :].rearrange("a (b c) -> (a b) c", b=2),
          bufs["ksq"].rearrange("p a b -> p (a b)")[0:8, :], kst)
    ftzd_done = p.cnt[kst]

    for tt in range(4):
        xfl = bufs["xfl0"] if tt % 2 == 0 else bufs["xfl1"]
        nm = emit_ln(p, C, "Gln", x_sb, xfl, tt * 512, 512, pb[0:4], lns,
                     out_t0=0)
        for sub in range(4):
            s16 = tt * 4 + sub
            cst = bufs["csb"] if s16 % 2 == 0 else bufs["csb2"]
            cs3 = r3(cst, NCH)
            csp = csld + str(s16 % 2)
            if ap["rope"]:
                if s16 >= 2:
                    p.wait("sy", rpd)
                p.dma("sy", cst[:, :], cs_d[s16 * 128:(s16 + 1) * 128, :], csp)
            cs_c = p.cnt.get(csp, 0)
            p.wait("t", nm)
            p.wait("t", rpd)
            for g2 in range(2):
                for c in range(NCH):
                    p.op("t", lambda e, c_=c, g_=g2, s_=sub, xl=xfl: e.matmul(
                        pb[4 + g_][:, :], xl[:, c_, s_ * 128:(s_ + 1) * 128],
                        wk[:, c_, g_ * 512:(g_ + 1) * 512],
                        start=(c_ == 0), stop=(c_ == 7)), kvm)
            kvm_k = p.cnt[kvm]
            p.wait("t", kvev)
            for g2 in range(2):
                for c in range(NCH):
                    p.op("t", lambda e, c_=c, g_=g2, s_=sub, xl=xfl: e.matmul(
                        pb[6 + g_][:, :], xl[:, c_, s_ * 128:(s_ + 1) * 128],
                        wv[:, c_, g_ * 512:(g_ + 1) * 512],
                        start=(c_ == 0), stop=(c_ == 7)), kvm)
            kvm_v = p.cnt[kvm]
            p.wait("v", kvm, kvm_k)
            p.wait("v", kst)
            if ap["rope"]:
                p.wait("v", csp, cs_c)
                _rope6(p, bufs, cs3[:, 0:4, :], r3(pb[4][:, :], 4), kout[:, 0:4, :], rpd)
                _rope6(p, bufs, cs3[:, 4:8, :], r3(pb[5][:, :], 4), kout[:, 4:8, :], rpd)
            else:
                p.op("v", lambda e: e.tensor_copy(kout[:, 0:4, :], r3(pb[4][:, :], 4)), rpd)
                p.op("v", lambda e: e.tensor_copy(kout[:, 4:8, :], r3(pb[5][:, :], 4)), rpd)
            rpd_k = p.cnt[rpd]
            p.wait("s", kvm, kvm_v)
            p.wait("s", kst)
            p.op("s", lambda e: e.copy(vout[:, 0:4, :], r3(pb[6][:, :], 4)), kvev)
            p.op("s", lambda e: e.copy(vout[:, 4:8, :], r3(pb[7][:, :], 4)), kvev)
            kvev_c = p.cnt[kvev]
            p.wait("s", rpd, rpd_k)
            p.wait("s", kst, ftzd_done)
            p.op("s", lambda e: e.activation(
                bufs["ksq"].rearrange("p a b -> p (a b)"),
                kout.rearrange("p a b -> p (a b)"), AF.Square), knsm)
            knsq_c = p.cnt[knsm]
            p.wait("v", knsm, knsq_c)
            for h in range(NCH):
                p.op("v", lambda e, h_=h: e.reduce_sum(
                    bufs["kn"][:, h_:h_ + 1], bufs["ksq"][:, h_, :], axis=AX), knsm)
            p.wait("v", knsm)
            p.op("v", lambda e: e.tensor_tensor(
                bufs["knacc"][:, :], bufs["knacc"][:, :], bufs["kn"][:, :], op=ALU.max), knsm)
            trow = s16 * 128
            p.wait("g", rpd, rpd_k)
            p.dma("g", kvloc[trow:trow + 128, 0:1024],
                  kout.rearrange("p a b -> p (a b)"), kst)
            p.wait("g", kvev, kvev_c)
            p.dma("g", kvloc[trow:trow + 128, 1024:2048],
                  vout.rearrange("p a b -> p (a b)"), kst)
            p.wait("t", rpd, p.cnt[rpd])
            for g2 in range(2):
                for c in range(NCH):
                    p.op("t", lambda e, c_=c, g_=g2, s_=sub, xl=xfl: e.matmul(
                        pb[4 + g_][:, :], xl[:, c_, s_ * 128:(s_ + 1) * 128],
                        wq[:, c_, g_ * 512:(g_ + 1) * 512],
                        start=(c_ == 0), stop=(c_ == 7)), kvm)
            qm_c = p.cnt[kvm]
            p.wait("v", kvm, qm_c)
            p.wait("v", kst)
            if ap["rope"]:
                _rope6(p, bufs, cs3[:, 0:4, :], r3(pb[4][:, :], 4), qout[:, 0:4, :], rpd)
                _rope6(p, bufs, cs3[:, 4:8, :], r3(pb[5][:, :], 4), qout[:, 4:8, :], rpd)
            else:
                p.op("v", lambda e: e.tensor_copy(qout[:, 0:4, :], r3(pb[4][:, :], 4)), rpd)
                p.op("v", lambda e: e.tensor_copy(qout[:, 4:8, :], r3(pb[5][:, :], 4)), rpd)
            rpd_q = p.cnt[rpd]
            p.wait("s", rpd, rpd_q)
            p.op("s", lambda e: e.activation(
                bufs["ksq"].rearrange("p a b -> p (a b)"),
                qout.rearrange("p a b -> p (a b)"), AF.Square), knsm)
            qsq_c = p.cnt[knsm]
            p.wait("v", knsm, qsq_c)
            for h in range(NCH):
                p.op("v", lambda e, h_=h: e.reduce_sum(
                    bufs["kn"][:, h_:h_ + 1], bufs["ksq"][:, h_, :], axis=AX), knsm)
            p.wait("v", knsm)
            p.op("v", lambda e: e.tensor_tensor(
                bufs["qnacc"][:, :], bufs["qnacc"][:, :], bufs["kn"][:, :], op=ALU.max), knsm)
            p.wait("g", rpd, rpd_q)
            p.dma("g", qbuf[trow:trow + 128, :],
                  qout.rearrange("p a b -> p (a b)"), kst)
    kn_done = p.cnt[knsm]
    p.wait("g", knsm, kn_done)
    p.dma("g", kvloc[2048:2049, 0:1024].rearrange("q (p a) -> (q p) a", p=128),
          bufs["knacc"][:, :], kst)
    kst_done = p.cnt[kst]
    cc = f"{tag}_cc"
    p.wait("g", kst, kst_done)
    grp = [[0, 1, 2, 3], [4, 5, 6, 7]]
    p.op("g", lambda e: e.collective_compute(
        "AllGather", ALU.bypass, replica_groups=grp,
        ins=[kvloc.opt()], outs=[kvg.opt()]), cc)

    p.wait("g", cc, p.cnt[cc])
    for r in range(4):
        if r > 0:
            p.wait("g", knsm)
        p.dma("g", bufs["kng"][:, :] if r == 0 else bufs["kn"][:, :],
              kvg[r * KVROWS + 2048:r * KVROWS + 2049, 0:1024]
              .rearrange("q (p a) -> (q p) a", p=128), f"{tag}_knl")
        knl_c = p.cnt[f"{tag}_knl"]
        if r > 0:
            p.wait("v", f"{tag}_knl", knl_c)
            p.wait("v", knsm)
            p.op("v", lambda e: e.tensor_tensor(bufs["kng"][:, :], bufs["kng"][:, :], bufs["kn"][:, :], op=ALU.max), knsm)
    p.wait("t", knsm)
    p.wait("t", kvev)
    p.op("t", lambda e: e.transpose(pb[4][0:8, 0:128], bufs["kng"][:, 0:8], C.ident_f32[:, :]), knsm)
    p.op("t", lambda e: e.transpose(pb[5][0:8, 0:128], bufs["qnacc"][:, 0:8], C.ident_f32[:, :]), knsm)
    p.wait("v", knsm)
    p.op("v", lambda e: e.reduce_max(bufs["kn"][0:8, 0:1], pb[4][0:8, 0:128], axis=AX), knsm)
    p.op("v", lambda e: e.reduce_max(bufs["kn"][0:8, 1:2], pb[5][0:8, 0:128], axis=AX), knsm)
    p.wait("t", knsm)
    p.op("t", lambda e: e.transpose(pb[6][0:1, 0:8], bufs["kn"][0:8, 0:1], C.ident_f32[0:8, 0:8]), knsm)
    p.op("t", lambda e: e.transpose(pb[6][0:1, 8:16], bufs["kn"][0:8, 1:2], C.ident_f32[0:8, 0:8]), knsm)
    p.wait("v", knsm)
    p.op("v", lambda e: e.tensor_copy(bufs["brow"][0:1, 0:16], pb[6][0:1, 0:16]), knsm)
    p.wait("v", knsm)
    p.op("v", lambda e: e.tensor_tensor(bufs["brow"][0:1, 0:8], bufs["brow"][0:1, 0:8], bufs["brow"][0:1, 8:16], op=ALU.mult), knsm)
    p.wait("v", knsm)
    p.op("v", lambda e: e.tensor_scalar(bufs["brow"][0:1, 0:8], bufs["brow"][0:1, 0:8], ap["inv_scale"] ** 2, None, op0=ALU.mult), knsm)
    p.wait("s", knsm)
    p.op("s", lambda e: e.activation(bufs["brow"][0:1, 0:8], bufs["brow"][0:1, 0:8], AF.Sqrt), knsm)
    p.wait("v", knsm)
    p.op("v", lambda e: e.tensor_scalar(bufs["brow"][0:1, 0:8], bufs["brow"][0:1, 0:8], -1.0, None, op0=ALU.mult), knsm)
    brow_done = p.cnt[knsm]
    p.wait("g", kst, kst_done)
    for m in range(4):
        p.dma("g", bufs["masks"][:, m, :], masks_d[m, :, :], f"{tag}_mkl")
    mkl_done = p.cnt[f"{tag}_mkl"]
    for e in ("t", "v", "s"):
        p.wait(e, f"{tag}_mkl", mkl_done)
    qld = f"{tag}_qld"
    p.wait("sy", kst, kst_done)
    p.wait("sy", "Gln_st")
    p.wait("sy", kvm)
    for c in range(NCH):
        p.dma("sy", q_sb[:, c, :], qbuf[:, c * 128:(c + 1) * 128], qld,
              transpose=True)
    qld_done = p.cnt[qld]

    pbd = f"{tag}_pbd"
    cc_done = p.cnt[cc]
    p.begin_switch(rr_sb, 4)
    for rr in range(4):
        p.begin_body(idx=rr, prefix="aB")
        p.wait("sy", cc, cc_done)
        _attn_body(p, C, "aB", layer, rr, pb, bufs, kvg, qld, qld_done,
                   knsm, brow_done, kvm, p.cnt[kvm], pbd)
        p.end_body()
    p.end_switch(post_cnt={pbd: 5})
    pbd_done = p.cnt[pbd]

    oproj, resid, wol = f"{tag}_op", f"{tag}_res", f"{tag}_wol"
    for e in ENGINES:
        p.wait(e, pbd, pbd_done)
    wouse = [0, 0]
    for mc in range(NCH):
        wod = bufs["wod0"] if mc % 2 == 0 else bufs["wod1"]
        wolp = wol + str(mc % 2)
        p.wait("g", oproj, wouse[mc % 2])
        p.dma("g", wod[:, :, :], woT_d[:, mc * 128:(mc + 1) * 128].rearrange(
            "(c p) m -> p c m", p=128), wolp)
        wol_c = p.cnt[wolp]
        p.wait("t", wolp, wol_c)
        for g in range(4):
            bnk = pb[(mc * 4 + g) % 2]
            if mc * 4 + g >= 2:
                p.wait("t", resid, p.cnt.get(resid, 0) - 1)
            for hd in range(NCH):
                p.op("t", lambda e, hd_=hd, g_=g, bk=bnk, w=wod: e.matmul(
                    bk[:, :], w[:, hd_, :],
                    o_all[:, hd_, g_ * 512:(g_ + 1) * 512],
                    start=(hd_ == 0), stop=(hd_ == 7)), oproj)
            op_c = p.cnt[oproj]
            wouse[mc % 2] = op_c
            p.wait("v", oproj, op_c)
            p.op("v", lambda e, mc_=mc, g_=g, bk=bnk: e.tensor_tensor(
                x_sb[:, mc_, g_ * 512:(g_ + 1) * 512],
                x_sb[:, mc_, g_ * 512:(g_ + 1) * 512], bk[:, :], op=ALU.add), resid)
    return resid, p.cnt[resid]


def _attn_body(p, C, tag, layer, rr, pb, bufs, kvg, qld, qld_done,
               knsm, brow_done, kvm, kvm_done, pbd):
    ap = attn_params(layer)
    NH = ap["nheads"]
    khb, vhb, q_sb, o_all = (bufs["khb"], bufs["vhb"], bufs["q_sb"],
                             bufs["o_all"])
    gbs = group_bases(rr)
    nbs = f"{tag}_nb"
    sS, sE, sMK, sAV, sOE = (f"{tag}_S", f"{tag}_E", f"{tag}_MK",
                             f"{tag}_AV", f"{tag}_OE")
    khl, vhl, ost = f"{tag}_khl", f"{tag}_vhl", f"{tag}_ost"
    p.wait("t", qld, qld_done)
    p.wait("t", kvm, kvm_done)
    p.wait("sy", kvm, kvm_done)
    p.wait("v", kvm, kvm_done)
    nblk = 8 - rr
    sEA, sEB = f"{tag}_EA", f"{tag}_EB"
    expA_hist = [p.cnt.get(sEA, 0)]
    expB_hist = [p.cnt.get(sEB, 0)]

    def gblock(b):
        r_ = b if b < 4 else 7 - b
        slot = 0 if b < 4 else 1
        return r_ * KVROWS + slot * 1024

    for h in range(NH):
        hc = h if layer == 0 else h // 2
        dsl = slice(0, 128) if layer == 0 else slice(64 * (h % 2), 64 * (h % 2) + 64)
        p.wait("t", knsm, brow_done)
        p.wait("t", nbs)
        p.wait("t", sEA)
        p.wait("t", sEB)
        p.wait("t", sOE)
        p.op("t", lambda e, hc_=hc: e.matmul(pb[7][:, 0:1], C.ones1_f32[:, :],
             bufs["brow"][0:1, hc_:hc_ + 1], start=True, stop=True), nbs)
        p.wait("v", nbs)
        p.wait("v", sEA)
        p.wait("v", sEB)
        p.op("v", lambda e: e.tensor_copy(bufs["negb"][:, 0:1], pb[7][:, 0:1]), nbs)
        negb_c = p.cnt[nbs]
        p.wait("s", nbs, negb_c)
        if layer == 0 or h % 2 == 0:
            p.wait("sy", sS)
            p.wait("sy", sAV)
            if h == 0:
                p.wait("sy", qld, qld_done)
            for b in range(nblk):
                base = gblock(b)
                p.dma("sy", khb[:, b * 1024:(b + 1) * 1024],
                      kvg[base:base + 1024, hc * 128:(hc + 1) * 128], khl,
                      transpose=True)
                p.dma("sy", vhb[:, b * 8:(b + 1) * 8, :],
                      kvg[base:base + 1024, 1024 + hc * 128:1024 + (hc + 1) * 128]
                      .rearrange("(kt p) d -> p kt d", p=128), vhl)
            khl_c = p.cnt[khl]
            vhl_c = p.cnt[vhl]
            p.wait("t", khl, khl_c)
            p.wait("t", vhl, vhl_c)
        for g in range(4):
            nreg = gbs[g] // 512 + 1
            for r in range(nreg):
                kg = 512 * r
                diag = (r == nreg - 1)
                p.wait("t", sEA, expA_hist[-1])
                for j in (0, 1):
                    p.op("t", lambda e, j_=j, kg_=kg, d_=dsl, g_=g, hc_=hc: e.matmul(
                        pb[3 + j_][:, :],
                        khb[d_, kg_ + j_ * 128:kg_ + (j_ + 1) * 128],
                        q_sb[d_, hc_, g_ * 512:(g_ + 1) * 512],
                        start=True, stop=True), sS)
                sSA_c = p.cnt[sS]
                p.wait("t", sEB, expB_hist[-1])
                for j in (2, 3):
                    p.op("t", lambda e, j_=j, kg_=kg, d_=dsl, g_=g, hc_=hc: e.matmul(
                        pb[3 + j_][:, :],
                        khb[d_, kg_ + j_ * 128:kg_ + (j_ + 1) * 128],
                        q_sb[d_, hc_, g_ * 512:(g_ + 1) * 512],
                        start=True, stop=True), sS)
                sSB_c = p.cnt[sS]
                p.wait("s", sS, sSA_c)
                p.wait("s", sAV)
                p.wait("s", sMK)
                for j in (0, 1):
                    p.op("s", lambda e, j_=j: e.activation(
                        bufs["p_sb"][:, j_, :], pb[3 + j_][:, :], AF.Exp,
                        bias=bufs["negb"][:, 0:1], scale=bufs["sccol"][:, 0:1]), sEA)
                expA_hist.append(p.cnt[sEA])
                p.wait("s", sS, sSB_c)
                for j in (2, 3):
                    p.op("s", lambda e, j_=j: e.activation(
                        bufs["p_sb"][:, j_, :], pb[3 + j_][:, :], AF.Exp,
                        bias=bufs["negb"][:, 0:1], scale=bufs["sccol"][:, 0:1]), sEB)
                expB_hist.append(p.cnt[sEB])
                sMKA_c = sMKB_c = p.cnt.get(sMK, 0)
                if diag:
                    p.wait("v", sEA)
                    for j in (0, 1):
                        p.op("v", lambda e, j_=j: e.tensor_tensor(
                            bufs["p_sb"][:, j_, :], bufs["p_sb"][:, j_, :],
                            bufs["masks"][:, j_, :], op=ALU.mult), sMK)
                    sMKA_c = p.cnt[sMK]
                    p.wait("v", sEB)
                    for j in (2, 3):
                        p.op("v", lambda e, j_=j: e.tensor_tensor(
                            bufs["p_sb"][:, j_, :], bufs["p_sb"][:, j_, :],
                            bufs["masks"][:, j_, :], op=ALU.mult), sMK)
                    sMKB_c = p.cnt[sMK]
                vsl = slice(0, 128) if layer == 0 else slice(64 * (h % 2), 64 * (h % 2) + 64)
                osl = slice(0, 128) if layer == 0 else slice(0, 64)
                p.wait("t", sEA)
                if diag:
                    p.wait("t", sMK, sMKA_c)
                if r == 0:
                    p.wait("t", sOE)
                for j in (0, 1):
                    p.op("t", lambda e, j_=j, r_=r, nr=nreg: e.matmul(
                        pb[1][0:1, :], C.ones_col_bf[:, :], bufs["p_sb"][:, j_, :],
                        start=(r_ == 0 and j_ == 0),
                        stop=False), sAV)
                for j in (0, 1):
                    p.op("t", lambda e, j_=j, kg_=kg, r_=r, nr=nreg, v_=vsl, o_=osl: e.matmul(
                        pb[0][o_, :], vhb[:, kg_ // 128 + j_, v_],
                        bufs["p_sb"][:, j_, :],
                        start=(r_ == 0 and j_ == 0), stop=False), sAV)
                p.wait("t", sEB)
                if diag:
                    p.wait("t", sMK, sMKB_c)
                for j in (2, 3):
                    p.op("t", lambda e, j_=j, r_=r, nr=nreg: e.matmul(
                        pb[1][0:1, :], C.ones_col_bf[:, :], bufs["p_sb"][:, j_, :],
                        start=False,
                        stop=(r_ == nr - 1 and j_ == 3)), sAV)
                for j in (2, 3):
                    p.op("t", lambda e, j_=j, kg_=kg, r_=r, nr=nreg, v_=vsl, o_=osl: e.matmul(
                        pb[0][o_, :], vhb[:, kg_ // 128 + j_, v_],
                        bufs["p_sb"][:, j_, :],
                        start=False, stop=(r_ == nr - 1 and j_ == 3)), sAV)
            sAV_c = p.cnt[sAV]
            p.wait("v", sAV, sAV_c)
            p.op("v", lambda e: e.reciprocal(bufs["rrow"][0:1, :], pb[1][0:1, :]), sOE)
            rr_c = p.cnt[sOE]
            p.wait("t", sOE, rr_c)
            p.op("t", lambda e: e.matmul(pb[2][:, :], C.ones1_f32[:, :],
                 bufs["rrow"][0:1, :], start=True, stop=True), sOE)
            p.wait("v", sOE)
            p.op("v", lambda e: e.tensor_copy(bufs["rb_sb"][:, :], pb[2][:, :]), sOE)
            p.wait("v", sOE)
            osl = slice(0, 128) if layer == 0 else slice(0, 64)
            if layer == 0:
                p.op("v", lambda e, g_=g, hc_=hc: e.tensor_tensor(
                    o_all[:, hc_, g_ * 512:(g_ + 1) * 512],
                    pb[0][:, :], bufs["rb_sb"][:, :], op=ALU.mult), sOE)
            else:
                p.wait("v", ost)
                p.op("v", lambda e, o_=osl: e.tensor_tensor(
                    bufs["ost"][o_, :], pb[0][o_, :], bufs["rb_sb"][o_, :],
                    op=ALU.mult), sOE)
                p.wait("g", sOE)
                p.dma("g", o_all[64 * (h % 2):64 * (h % 2) + 64, hc,
                                 g * 512:(g + 1) * 512],
                      bufs["ost"][0:64, :], ost)
    p.wait("v", sOE)
    if layer == 1:
        p.wait("g", ost)
        p.wait("v", ost)
    p.op("v", lambda e: e.engine_nop(), pbd)
    p.wait("t", sOE)
    p.op("t", lambda e: e.nop(), pbd)
    p.wait("s", sE)
    p.op("s", lambda e: e.nop(), pbd)
    p.wait("g", ost if layer == 1 else sOE, None if layer == 1 else 0)
    p.op("g", lambda e: e.engine_nop(), pbd)
    p.op("sy", lambda e: e.nop(), pbd)


def build_fused():
    p = Prog("fused")
    nc = p.nc
    x_d = nc.dram_tensor("x_fm", [D, TPC], F32, kind="ExternalInput").ap()
    rr_d = nc.dram_tensor("rrsel", [1, 1], U32, kind="ExternalInput").ap()
    lflag_d = nc.dram_tensor("lflag", [16, 1], F32, kind="ExternalInput").ap()
    cs0_d = nc.dram_tensor("cs0", [TPC, D], BF16, kind="ExternalInput").ap()
    gwd = nc.dram_tensor("gate_w", [D, 32], F32, kind="ExternalInput").ap()
    sgw_d = nc.dram_tensor("sgw", [D, 2048], F32, kind="ExternalInput").ap()
    suw_d = nc.dram_tensor("suw", [D, 2048], F32, kind="ExternalInput").ap()
    sdw_d = nc.dram_tensor("sdw", [2048, D], F32, kind="ExternalInput").ap()
    rgw_d = nc.dram_tensor("rgw", [32 * 1024, 2048], F32, kind="ExternalInput").ap()
    ruw_d = nc.dram_tensor("ruw", [32 * 1024, 2048], F32, kind="ExternalInput").ap()
    rdw_d = nc.dram_tensor("rdw", [32 * 2048, 1024], F32, kind="ExternalInput").ap()
    wq0_d = nc.dram_tensor("wq0", [D, D], F32, kind="ExternalInput").ap()
    wk0_d = nc.dram_tensor("wk0", [D, D], F32, kind="ExternalInput").ap()
    wv0_d = nc.dram_tensor("wv0", [D, D], F32, kind="ExternalInput").ap()
    wo0_d = nc.dram_tensor("wo0", [D, D], F32, kind="ExternalInput").ap()
    wq1_d = nc.dram_tensor("wq1", [D, D], F32, kind="ExternalInput").ap()
    wk1_d = nc.dram_tensor("wk1", [D, D], F32, kind="ExternalInput").ap()
    wv1_d = nc.dram_tensor("wv1", [D, D], F32, kind="ExternalInput").ap()
    wo1_d = nc.dram_tensor("wo1", [D, D], F32, kind="ExternalInput").ap()
    w1t_d = nc.dram_tensor("w1t", [D, 4096], F32, kind="ExternalInput").ap()
    w2t_d = nc.dram_tensor("w2t", [4096, D], F32, kind="ExternalInput").ap()
    b1r_d = nc.dram_tensor("b1row", [1, 4096], F32, kind="ExternalInput").ap()
    b2r_d = nc.dram_tensor("b2row", [1, D], F32, kind="ExternalInput").ap()
    sc_d = nc.dram_tensor("scal", [1, 1], F32, kind="ExternalInput").ap()
    sc4_d = nc.dram_tensor("scal4", [4, 1], F32, kind="ExternalInput").ap()
    masks_d = nc.dram_tensor("masksT", [4, 128, 512], BF16, kind="ExternalInput").ap()
    sc0_d = nc.dram_tensor("sccol0", [128, 1], F32, kind="ExternalInput").ap()
    sc1_d = nc.dram_tensor("sccol1", [128, 1], F32, kind="ExternalInput").ap()
    y_d = nc.dram_tensor("y_fm", [D, TPC], F32, kind="ExternalOutput").ap()
    xs_sorted = nc.dram_tensor("xs_sorted", [NTILE * 128, D], BF16).ap()
    dout = nc.dram_tensor("dout", [NTILE * 128, D], BF16).ap()
    qbuf = nc.dram_tensor("qbuf", [TPC, D], BF16).ap()
    kvloc0 = nc.dram_tensor("kvloc0", [KVROWS, 2048], BF16).ap()
    kvg0 = nc.dram_tensor("kvg0", [4 * KVROWS, 2048], BF16).ap()
    kvloc1 = nc.dram_tensor("kvloc1", [KVROWS, 2048], BF16).ap()
    kvg1 = nc.dram_tensor("kvg1", [4 * KVROWS, 2048], BF16).ap()
    lr_raw = nc.dram_tensor("lr_raw", [16, D], F32).ap()
    lr_in = nc.dram_tensor("lr_in", [16, D], F32).ap()
    lr_out0 = nc.dram_tensor("lr_out0", [16, D], F32).ap()
    lr_out1 = nc.dram_tensor("lr_out1", [16, D], F32).ap()

    C = Consts(p)
    x_sb = p.sb("x_sb", [128, NCH, TPC], F32)
    pb = [p.psum(f"pb{i}", [128, 512], F32) for i in range(8)]
    A1 = p.sb("A1", [128, 16384], BF16)
    A2 = p.sb("A2", [128, 16384], BF16)
    A3 = p.sb("A3", [128, 20480], BF16)
    rr_sb = p.sb("rr_sb", [1, 1], U32)
    lflag = p.sb("lflag_sb", [16, 1], F32)
    lns = ln_scratch(p, "G_lns")

    for c in range(NCH):
        p.dma("sy", x_sb[:, c, :], x_d[c * 128:(c + 1) * 128, :], "xld")
    p.dma("sy", rr_sb[:, :], rr_d[:, :], "xld")
    p.dma("sy", lflag[:, :], lflag_d[:, :], "xld")
    xld_done = p.cnt["xld"]
    for e in ENGINES:
        p.wait(e, "xld", xld_done)
        p.wait(e, "c_ld")

    def mk_moe_bufs():
        b = {
            "ar": A3,
            "xln": r3(A1[:, 0:16384], NCH),
            "xs_all": r3(A2[:, 0:16384], 16),
            "lns": lns,
        }
        for name, shape, dt in [
            ("gw", [128, NCH, 32], BF16), ("oh_sb", [128, 16, 32], F32),
            ("rank_sb", [128, 16, 32], F32), ("cs_sb", [1, 16, 32], F32),
            ("score", [128, 16], F32), ("maxv", [128, 1], F32),
            ("eid", [128, 1], F32), ("eq", [128, 32], F32),
            ("sel", [128, 32], F32), ("imb", [128, 32], F32),
            ("cnt_i", [32, 1], I32), ("pad_f", [32, 1], F32),
            ("cmp", [32, NTILE], F32), ("erow", [1, NTILE], F32),
            ("esml", [1, NTILE], F32), ("eoff", [1, 2], F32),
            ("base_row", [1, 32], F32), ("t2", [128, 32], F32),
            ("t2r", [1, 32], F32), ("posu", [128, 16], U32),
            ("posf", [128, 16], F32), ("idxg", [128, 8], U32),
            ("idxd", [128, 16], U32), ("idgf", [128, 8], F32),
            ("iddf", [128, 16], F32), ("kcoff", [128, 8], F32),
            ("ichoff", [128, 16], F32), ("xsf0", [128, NCH, 128], BF16),
            ("xsf1", [128, NCH, 128], BF16), ("dsb", [128, 1024], BF16),
            ("rtc", [128, 128], F32),
        ]:
            b[name] = p.sb(f"M_{name}", shape, dt)
        ar = A3
        b["wg0"] = ar[:, 0:2048];      b["wg1"] = ar[:, 2048:4096]
        b["wu0"] = ar[:, 4096:6144];   b["wu1"] = ar[:, 6144:8192]
        b["wd0"] = ar[:, 8192:9216];   b["wd1"] = ar[:, 9216:10240]
        b["gs"] = r3(ar[:, 10240:12288], 4)
        b["us"] = r3(ar[:, 12288:14336], 4)
        b["hh"] = r3(ar[:, 14336:16384], 4)
        b["hT"] = r3(ar[:, 16384:18432], 16)
        b["xs_t0"] = ar[:, 18432:19456]
        b["xs_t1"] = ar[:, 19456:20480]
        b["sgwb0"] = r3(ar[:, 0:1024], NCH);   b["suwb0"] = r3(ar[:, 1024:2048], NCH)
        b["sgwb1"] = r3(ar[:, 2048:3072], NCH); b["suwb1"] = r3(ar[:, 3072:4096], NCH)
        b["sdwb0"] = r3(ar[:, 4096:6144], 16)
        b["sdwb1"] = r3(ar[:, 6144:8192], 16)
        b["h_sh"] = r3(ar[:, 10240:18432], 16)
        b["gsh"] = ar[:, 18432:18944]
        b["ush"] = ar[:, 18944:19456]
        return b

    moe_b = mk_moe_bufs()
    attn_b = attn_fused_bufs(p, "AB", A1, A2, A3)

    def mk_ffn1_bufs():
        b = {
            "h_sb": r3(A2[:, 0:16384], 32),
            "lns": lns,
        }
        for name, shape, dt in [
            ("ltst", [16, 12], F32), ("ltmv", [16, 2], F32),
            ("ltrs", [16, 1], F32), ("ltnm", [16, 1], F32),
            ("loraT", [128, NCH, 16], BF16), ("sc_sb", [1, 1], F32),
            ("sc4_sb", [4, 1], F32),
        ]:
            b[name] = p.sb(f"F_{name}", shape, dt)
        b["t_f"] = A1[0:4, 13312:13824]
        b["u_all"] = A1[0:16, 13824:14336]
        b["b2row"] = A1[0:1, 14336:15360]
        b["w1a"] = r3(A3[:, 0:1024], NCH)
        b["w1b"] = r3(A3[:, 1024:2048], NCH)
        b["w2a"] = r3(A3[:, 2048:6144], 32)
        b["w2b"] = r3(A3[:, 6144:10240], 32)
        b["e1s"] = A3[0:1, 10240:14336]
        b["b1row"] = A3[0:1, 14336:18432]
        b["t_row"] = A3[0:1, 18432:20480]
        b["u_f"] = r3(A1[:, 4096:6144], 4)[0:4, :, :]
        b["e2b"] = r3(A1[:, 8192:10240], NCH)[0:16, :, 0:128]
        b["lt_sb"] = A1.bitcast(F32)[0:16, 5120:6144]
        b["ltln"] = A1[0:16, 12288:13312]
        return b

    ffn1_b = mk_ffn1_bufs()
    p.dma("sy", ffn1_b["sc_sb"][:, :], sc_d[:, :], "xld2")
    p.dma("sy", ffn1_b["sc4_sb"][:, :], sc4_d[:, :], "xld2")
    for e in ("v", "s", "t", "g"):
        p.wait(e, "xld2")

    def ffn1_prolog(tagf="fp"):
        p.dma("g", ffn1_b["b1row"][:, :], b1r_d[:, :], tagf + "wl")
        p.dma("g", ffn1_b["b2row"][:, :], b2r_d[:, :], tagf + "wl")
        for e in ("v", "s", "t", "g"):
            p.wait(e, tagf + "wl")

    def barrier(sem, cnt):
        for e in ENGINES:
            p.wait(e, sem, cnt)

    dbg_outs = []

    def dbg_dump(label, sem, cnt):
        if not DBG_FUSED:
            return
        yd = nc.dram_tensor(f"dbg_{label}", [D, TPC], F32,
                            kind="ExternalOutput").ap()
        dbg_outs.append(label)
        p.wait("sy", sem, cnt)
        for c in range(NCH):
            p.dma("sy", yd[c * 128:(c + 1) * 128, :], x_sb[:, c, :],
                  "dbgst")
        p.wait("sy", "dbgst")

    def lora_exchange(lr_out, entry):
        tagl = "LX"
        barrier(*entry)
        for c in range(NCH):
            p.dma("g", lr_raw[:, c * 128:(c + 1) * 128]
                  .rearrange("t m -> m t"), x_sb[:, c, 1:17], tagl + "w",
                  allow_slow_non_contiguous=True)
        p.wait("sy", tagl + "w")
        p.dma("sy", ffn1_b["lt_sb"][:, :], lr_raw[:, :], tagl + "r")
        p.wait("v", tagl + "r")
        p.op("v", lambda e: e.tensor_scalar(
            ffn1_b["lt_sb"][:, :], ffn1_b["lt_sb"][:, :], lflag[:, 0:1],
            None, op0=ALU.mult), tagl + "m")
        p.wait("g", tagl + "m")
        p.dma("g", lr_in[:, :], ffn1_b["lt_sb"][:, :], tagl + "w2")
        p.wait("g", tagl + "w2")
        grp = [[0, 1, 2, 3], [4, 5, 6, 7]]
        p.op("g", lambda e, lo=lr_out: e.collective_compute(
            "AllReduce", ALU.add, replica_groups=grp,
            ins=[lr_in.opt()], outs=[lo.opt()]), tagl + "cc")
        p.wait("sy", tagl + "cc")
        p.dma("sy", ffn1_b["lt_sb"][:, :], lr_out[:, :], tagl + "r2")
        return tagl + "r2", p.cnt[tagl + "r2"]

    entry = ("xld", xld_done)
    res1 = emit_moe(p, C, "m", x_sb, gwd, sgw_d, suw_d, sdw_d,
                    rgw_d, ruw_d, rdw_d, xs_sorted, dout, pb, moe_b)
    entry = (res1, p.cnt[res1])
    dbg_dump("p0", *entry)
    res2, res2c = emit_attn_fused(p, C, "a", 0, x_sb, pb, attn_b,
                                  wq0_d, wk0_d, wv0_d, wo0_d, cs0_d,
                                  masks_d, sc0_d, kvloc0, kvg0, qbuf,
                                  rr_sb[0:1, 0:1], entry, lns)
    entry = (res2, res2c)
    dbg_dump("p1", *entry)
    barrier(*entry)
    res3 = emit_moe(p, C, "m", x_sb, gwd, sgw_d, suw_d, sdw_d,
                    rgw_d, ruw_d, rdw_d, xs_sorted, dout, pb, moe_b)
    entry = (res3, p.cnt[res3])
    dbg_dump("p2", *entry)
    lt_dep = lora_exchange(lr_out0, entry)
    barrier(*lt_dep)
    ffn1_prolog()
    res4 = emit_ffn1(p, C, "f", x_sb, FXlnView(A1), pb, w1t_d, w2t_d, ffn1_b)
    entry = (res4, p.cnt[res4])
    dbg_dump("p3", *entry)
    res5, res5c = emit_attn_fused(p, C, "a", 1, x_sb, pb, attn_b,
                                  wq1_d, wk1_d, wv1_d, wo1_d, cs0_d,
                                  masks_d, sc1_d, kvloc1, kvg1, qbuf,
                                  rr_sb[0:1, 0:1], entry, lns)
    entry = (res5, res5c)
    dbg_dump("p4", *entry)
    lt_dep = lora_exchange(lr_out1, entry)
    barrier(*lt_dep)
    ffn1_prolog()
    res6 = emit_ffn1(p, C, "f", x_sb, FXlnView(A1), pb, w1t_d, w2t_d, ffn1_b)

    p.wait("sy", res6)
    for c in range(NCH):
        p.dma("sy", y_d[c * 128:(c + 1) * 128, :], x_sb[:, c, :], "yst")
    p.wait("sy", "yst")
    return p


DBG_FUSED = False


def build_fused_dbg():
    global DBG_FUSED
    DBG_FUSED = True
    try:
        return build_fused()
    finally:
        DBG_FUSED = False


class FXlnView:

    def __init__(self, A1):
        self.x = r3(A1[:, 0:4096], NCH)

    def __getitem__(self, idx):
        _, c, ts = idx
        off = ts.start % 512
        return self.x[:, c, off:off + (ts.stop - ts.start)]


def fused_inputs(core, x_fm, inputs):
    cd = make_consts()
    b, b0, b1 = core_blocks(core)
    rr = core % 4
    cs_full = rope_table()
    own = np.concatenate([cs_full[b0 * 1024:(b0 + 1) * 1024],
                          cs_full[b1 * 1024:(b1 + 1) * 1024]], 0)
    lflag = np.full((16, 1), 1.0 if rr == 0 else 0.0, np.float32)
    wq0 = np.asarray(inputs["L0_attn_wq"], np.float32).transpose(1, 0, 2).reshape(D, D)
    wk0 = np.asarray(inputs["L0_attn_wk"], np.float32).transpose(1, 0, 2).reshape(D, D)
    wv0 = np.asarray(inputs["L0_attn_wv"], np.float32).transpose(1, 0, 2).reshape(D, D)
    wo0 = np.asarray(inputs["L0_attn_wo"], np.float32).reshape(D, D)
    wq1 = np.asarray(inputs["L1_attn_wq"], np.float32).transpose(2, 0, 1).reshape(D, D)
    wk1 = np.asarray(inputs["L1_attn_wk"], np.float32).transpose(2, 0, 1).reshape(D, D)
    wv1 = np.asarray(inputs["L1_attn_wv"], np.float32).transpose(2, 0, 1).reshape(D, D)
    wo1 = np.ascontiguousarray(np.asarray(inputs["L1_attn_wo"], np.float32).T)
    scale = np.float32(np.asarray(inputs["L1_ffn_scale"]))
    return dict(
        x_fm=np.ascontiguousarray(x_fm),
        rrsel=np.array([[rr]], np.uint32),
        lflag=lflag,
        cs0=np.ascontiguousarray(own),
        gate_w=np.ascontiguousarray(np.asarray(inputs["L0_ffn_gate_w"], np.float32)),
        sgw=np.ascontiguousarray(np.asarray(inputs["L0_ffn_shared_gate_w"], np.float32)),
        suw=np.ascontiguousarray(np.asarray(inputs["L0_ffn_shared_up_w"], np.float32)),
        sdw=np.ascontiguousarray(np.asarray(inputs["L0_ffn_shared_down_w"], np.float32)),
        rgw=np.ascontiguousarray(np.asarray(inputs["L0_ffn_routed_gate_w"], np.float32).reshape(32 * 1024, 2048)),
        ruw=np.ascontiguousarray(np.asarray(inputs["L0_ffn_routed_up_w"], np.float32).reshape(32 * 1024, 2048)),
        rdw=np.ascontiguousarray(np.asarray(inputs["L0_ffn_routed_down_w"], np.float32).reshape(32 * 2048, 1024)),
        wq0=np.ascontiguousarray(wq0), wk0=np.ascontiguousarray(wk0),
        wv0=np.ascontiguousarray(wv0), wo0=np.ascontiguousarray(wo0),
        wq1=np.ascontiguousarray(wq1), wk1=np.ascontiguousarray(wk1),
        wv1=np.ascontiguousarray(wv1), wo1=np.ascontiguousarray(wo1),
        w1t=np.ascontiguousarray(np.asarray(inputs["L1_ffn_W1"], np.float32).T),
        w2t=np.ascontiguousarray(np.asarray(inputs["L1_ffn_W2"], np.float32).T),
        b1row=np.ascontiguousarray(np.asarray(inputs["L1_ffn_b1"], np.float32).reshape(1, 4096)),
        b2row=np.ascontiguousarray(np.asarray(inputs["L1_ffn_b2"], np.float32).reshape(1, D)),
        scal=np.full((1, 1), scale, np.float32),
        scal4=np.full((4, 1), scale, np.float32),
        masksT=attn_masks(),
        sccol0=np.full((128, 1), 1.0 / 128.0, np.float32),
        sccol1=np.full((128, 1), 1.0 / 8.0, np.float32),
        **cd,
    )


def kernel_fused(**inputs):
    x = np.asarray(inputs["L0_attn_x"], np.float32)
    shards = shard_x(x)
    nc = _get_prog("fused", build_fused)
    im = [fused_inputs(c, shards[c], inputs) for c in range(NCORE)]
    res = run_bass_kernel_spmd(nc, im, core_ids=list(range(NCORE)))
    return unshard_x([res.results[i]["y_fm"] for i in range(NCORE)]).astype(np.float32)


_PROGS = {}


def _get_prog(name, builder):
    if name not in _PROGS:
        _PROGS[name] = builder().build()
    return _PROGS[name]


def kernel(**inputs):
    return kernel_fused(**inputs)


def kernel_v0(**inputs):
    x = np.asarray(inputs["L0_attn_x"], np.float32)
    gw = np.asarray(inputs["L0_ffn_gate_w"], np.float32)
    sgw = np.asarray(inputs["L0_ffn_shared_gate_w"], np.float32)
    suw = np.asarray(inputs["L0_ffn_shared_up_w"], np.float32)
    sdw = np.asarray(inputs["L0_ffn_shared_down_w"], np.float32)
    rgw = np.asarray(inputs["L0_ffn_routed_gate_w"], np.float32)
    ruw = np.asarray(inputs["L0_ffn_routed_up_w"], np.float32)
    rdw = np.asarray(inputs["L0_ffn_routed_down_w"], np.float32)
    wq0 = np.asarray(inputs["L0_attn_wq"], np.float32)
    wk0 = np.asarray(inputs["L0_attn_wk"], np.float32)
    wv0 = np.asarray(inputs["L0_attn_wv"], np.float32)
    wo0 = np.asarray(inputs["L0_attn_wo"], np.float32)
    wq1 = np.asarray(inputs["L1_attn_wq"], np.float32)
    wk1 = np.asarray(inputs["L1_attn_wk"], np.float32)
    wv1 = np.asarray(inputs["L1_attn_wv"], np.float32)
    wo1 = np.asarray(inputs["L1_attn_wo"], np.float32)
    W1 = np.asarray(inputs["L1_ffn_W1"], np.float32)
    b1 = np.asarray(inputs["L1_ffn_b1"], np.float32)
    W2 = np.asarray(inputs["L1_ffn_W2"], np.float32)
    b2 = np.asarray(inputs["L1_ffn_b2"], np.float32)
    scale = np.float32(np.asarray(inputs["L1_ffn_scale"]))

    shards = shard_x(x)

    def apply_moe(sh):
        nc = _get_prog("moe", build_moe)
        im = [moe_inputs(c, sh[c], gw, sgw, suw, sdw, rgw, ruw, rdw)
              for c in range(NCORE)]
        res = run_bass_kernel_spmd(nc, im, core_ids=list(range(NCORE)))
        return [res.results[i]["y_fm"] for i in range(NCORE)]

    def apply_attn(layer, sh):
        xfull = unshard_x(sh)
        wq, wk, wv, wo = ((wq0, wk0, wv0, wo0) if layer == 0
                          else (wq1, wk1, wv1, wo1))
        outs = [None] * NCORE
        for rr in range(4):
            nc = _get_prog(f"attn{layer}_{rr}",
                           lambda layer=layer, rr=rr: build_attn(layer, rr))
            cores = [rr, rr + 4]
            im = [attn_inputs(c, layer, sh[c], xfull, wq, wk, wv, wo)
                  for c in cores]
            res = run_bass_kernel_spmd(nc, im, core_ids=[0, 1])
            outs[cores[0]] = res.results[0]["y_fm"]
            outs[cores[1]] = res.results[1]["y_fm"]
        return outs

    def apply_ffn1(sh):
        xfull = unshard_x(sh)
        nc = _get_prog("ffn1", build_ffn1)
        im = [ffn1_inputs(c, sh[c], W1, b1, W2, b2, scale, xfull)
              for c in range(NCORE)]
        res = run_bass_kernel_spmd(nc, im, core_ids=list(range(NCORE)))
        return [res.results[i]["y_fm"] for i in range(NCORE)]

    shards = apply_moe(shards)
    shards = apply_attn(0, shards)
    shards = apply_moe(shards)
    shards = apply_ffn1(shards)
    shards = apply_attn(1, shards)
    shards = apply_ffn1(shards)
    return unshard_x(shards).astype(np.float32)
